# version 04 (final)
# baseline (speedup 1.0000x reference)
import numpy as np
from contextlib import ExitStack
import concourse.bass as bass
import concourse.mybir as mybir
from concourse.bass_utils import run_bass_kernel_spmd

F32 = mybir.dt.float32
BF16 = mybir.dt.bfloat16
AF = mybir.ActivationFunctionType
ALU = mybir.AluOpType
AX = mybir.AxisListType

T = 2048
NJ = 16
D = 1024
NI = 8


class Rec:
    __slots__ = ("eng", "fn", "idx", "dma", "marked", "deps", "label", "waits", "dmaord")

class K:
    ENGINES = ("tensor", "vector", "scalar", "gpsimd", "sync")

    def __init__(self, nc):
        self.nc = nc
        self.recs = {e: [] for e in self.ENGINES}
        self.last_w = {}
        self.readers = {}
        self.ndma = {e: 0 for e in self.ENGINES}
        self.evac_tog = 0

    def ins(self, eng, fn, rd=(), wr=(), dma=False, label="", srd=()):
        r = Rec()
        r.eng = eng; r.fn = fn; r.idx = len(self.recs[eng]); r.dma = dma
        r.marked = False; r.deps = set(); r.label = label; r.waits = None
        r.dmaord = self.ndma[eng] + 1 if dma else 0
        if dma:
            self.ndma[eng] += 1
        rdk = [a if isinstance(a, str) else a.name for a in rd]
        wrk = [a if isinstance(a, str) else a.name for a in wr]
        for a in srd:
            key = a if isinstance(a, str) else a.name
            rdk.append(key)
            w = self.last_w.get(key)
            if w is not None and w is not r:
                r.deps.add(w)
        raw_same = eng in ("vector", "scalar", "gpsimd")
        for key in rdk:
            w = self.last_w.get(key)
            if w is not None and w is not r and (w.eng != eng or w.dma or raw_same):
                r.deps.add(w)
        for key in wrk:
            w = self.last_w.get(key)
            if w is not None and (w.eng != eng or w.dma):
                r.deps.add(w)
            for rr in self.readers.get(key, ()):
                if rr is not r and (rr.eng != eng or rr.dma):
                    r.deps.add(rr)
        for key in rdk:
            self.readers.setdefault(key, []).append(r)
        for key in wrk:
            self.last_w[key] = r
            self.readers[key] = []
        self.recs[eng].append(r)
        return r

    def barrier(self):
        lasts = [self.recs[e][-1] for e in self.ENGINES if self.recs[e]]
        for e in self.ENGINES:
            r = self.ins(e, lambda eng: eng.nop(), label="barrier")
            for l in lasts:
                if l is not r and (l.eng != e or l.dma):
                    r.deps.add(l)
        self.last_w = {}
        self.readers = {}

    def evac_eng(self):
        self.evac_tog ^= 1
        return "vector" if self.evac_tog else "scalar"

    def finalize(self, blk, prog, dsem):
        for e in self.ENGINES:
            for r in self.recs[e]:
                for d in r.deps:
                    if not d.dma:
                        d.marked = True
        cpre = {}
        for e in self.ENGINES:
            c = 0; pc = []
            for r in self.recs[e]:
                if r.marked and not r.dma:
                    c += 1
                pc.append(c)
            cpre[e] = pc
        for e in self.ENGINES:
            hw = {}
            for r in self.recs[e]:
                wl = {}
                for d in r.deps:
                    if d.dma:
                        sem, val = dsem[d.eng], 16 * d.dmaord
                    else:
                        sem, val = prog[d.eng], cpre[d.eng][d.idx]
                    if val > wl.get(sem, 0):
                        wl[sem] = val
                r.waits = [(s, v) for s, v in wl.items() if v > hw.get(s, 0)]
                for s, v in wl.items():
                    if v > hw.get(s, 0):
                        hw[s] = v

        def mkbody(e):
            items = self.recs[e]
            tot = self.ndma[e]

            def body(eng):
                for r in items:
                    for s, v in r.waits:
                        eng.wait_ge(s, v)
                    i = r.fn(eng)
                    if r.dma:
                        i.then_inc(dsem[e], 16)
                    elif r.marked:
                        i.then_inc(prog[e], 1)
                if tot:
                    eng.wait_ge(dsem[e], 16 * tot)
            return body

        for e in self.ENGINES:
            getattr(blk, e)(mkbody(e))



def _host_consts():
    c = {}
    c["c_ident"] = np.eye(128, dtype=np.float32)
    i = np.arange(128)
    c["c_triu01"] = (i[:, None] <= i[None, :]).astype(np.float32)
    c["c_tril01"] = (i[:, None] >= i[None, :]).astype(np.float32)
    c["c_upstrict"] = (i[:, None] < i[None, :]).astype(np.float32)
    c["c_ones_col"] = np.ones((128, 1), np.float32)
    c["c_ones_row"] = np.ones((1, 128), np.float32)
    c["c_pos"] = (np.arange(NJ, dtype=np.float32)[None, :] * 128
                  + np.arange(128, dtype=np.float32)[:, None]).astype(np.float32)
    inv = 1.0 / 10000.0 ** (np.arange(0, 64, 2, dtype=np.float32) / 64)
    tt = np.arange(T, dtype=np.float32)
    f = tt[:, None] * inv[None, :]
    cos1 = np.cos(f); sin1 = np.sin(f)
    C64 = np.concatenate([cos1, cos1], 1)
    S64 = np.concatenate([-sin1, sin1], 1)
    c["c_ropeC"] = np.tile(C64, (1, 16)).astype(np.float32)
    c["c_ropeS"] = np.tile(S64, (1, 16)).astype(np.float32)
    half_idx = np.arange(0, 64, 2, dtype=np.float32)
    invf = 1.0 / 10000.0 ** (half_idx / 64)
    fr = tt[:, None] * invf[None, :]
    fr2 = np.concatenate([fr, fr], axis=-1)
    base = (half_idx + 0.4 * 64) / (1.4 * 64)
    power = (tt - T // 2) / 256.0
    sc = base[None, :] ** power[:, None]
    sc2 = np.concatenate([sc, sc], axis=-1)
    cos2, sin2 = np.cos(fr2), np.sin(fr2)
    sgn = np.where(np.arange(64) < 32, -1.0, 1.0).astype(np.float32)[None, :]
    c["c_xqC"] = np.tile(cos2 * sc2, (1, 8)).astype(np.float32)
    c["c_xqS"] = np.tile(sin2 * sc2 * sgn, (1, 8)).astype(np.float32)
    c["c_xkC"] = np.tile(cos2 / sc2, (1, 8)).astype(np.float32)
    c["c_xkS"] = np.tile(sin2 / sc2 * sgn, (1, 8)).astype(np.float32)
    return c

def prep_inputs(d):
    w = {}
    g = lambda n: np.ascontiguousarray(np.asarray(d[n], np.float32))
    tr = lambda n: np.ascontiguousarray(np.asarray(d[n], np.float32).T)
    w["L0_W_dq"] = g("L0_attn_W_dq")
    w["L0_W_dqT"] = tr("L0_attn_W_dq")
    w["L0_W_uq"] = g("L0_attn_W_uq")
    w["L0_W_uqT"] = tr("L0_attn_W_uq")
    w["L0_W_dkvT"] = tr("L0_attn_W_dkv")
    w["L0_W_uk"] = g("L0_attn_W_uk")
    w["L0_W_uv"] = g("L0_attn_W_uv")
    w["L0_W_oT"] = tr("L0_attn_W_o")
    for n in ["ln1_w", "ln2_w", "bq", "bk", "bv", "qn_w", "kn_w", "bg", "bo", "br", "b1", "b2",
              "Wq", "Wk", "Wv", "Wg", "Wo", "Wr", "W1", "W2"]:
        w["L0_" + n] = g("L0_ffn_" + n)
    w["L1_gamma"] = g("L1_attn_gamma")
    w["L1_w_q"] = g("L1_attn_w_q")
    w["L1_w_kv"] = g("L1_attn_w_kv")
    w["L1_w_out"] = g("L1_attn_w_out")
    for n in ["Wg", "Ws1", "Ws3", "Ws2", "We1", "We3", "We2"]:
        w["L1_" + n] = g("L1_ffn_" + n)
    w["L2_wqkv"] = g("L2_attn_wqkv")
    w["L2_router_wT"] = tr("L2_ffn_router_w")
    w["L2_router_b"] = g("L2_ffn_router_b")
    w["L2_expert_wT"] = np.ascontiguousarray(
        np.asarray(d["L2_ffn_expert_w"], np.float32).transpose(0, 2, 1))
    w["L2_expert_b"] = g("L2_ffn_expert_b")
    w["L3_ln_g"] = g("L3_attn_ln_g"); w["L3_ln_b"] = g("L3_attn_ln_b")
    w["L3_qkv_wT"] = tr("L3_attn_qkv_w"); w["L3_qkv_b"] = g("L3_attn_qkv_b")
    w["L3_gate_wT"] = tr("L3_attn_gate_w"); w["L3_gate_b"] = g("L3_attn_gate_b")
    w["L3_proj_wT"] = tr("L3_attn_proj_w"); w["L3_proj_b"] = g("L3_attn_proj_b")
    for n in ["wqkv", "q_scale", "k_scale", "wo", "ln_g", "ln_b", "w1", "b1", "w2", "b2"]:
        w["L3f_" + n] = g("L3_ffn_" + n)
    w.update(_host_consts())
    return w





BCLS = [16, 32, 64, 128, 256, 512, 1024, 2048, 4096, 8192, 12288, 16384, 24576, 32768, 49152, 65536]
I8 = mybir.dt.int8

class P:

    def __init__(self, nc, k, ctx):
        self.nc, self.k, self.ctx = nc, k, ctx
        self.uid = 0
        self.freel = {}
        self.used = {}

    def name(self, base):
        self.uid += 1
        return f"{base}_{self.uid}"

    def sb(self, base, shape, dt=F32):
        return self.ctx.enter_context(self.nc.sbuf_tensor(self.name(base), shape, dt))

    def _cls(self, n):
        for c in BCLS:
            if n <= c:
                return c
        raise ValueError(f"too big: {n}")

    def _view(self, t, shape, dt):
        nel = int(np.prod(shape[1:])) if len(shape) > 1 else 1
        esz = 4 if dt in (F32, mybir.dt.int32) else 2
        v = t[:, :nel * esz].bitcast(dt)
        if len(shape) == 3:
            return v.rearrange("q (a b) -> q a b", a=shape[1])
        if shape[0] != 128:
            return v[:shape[0], :]
        return v

    def get(self, base, shape, dt=F32):
        esz = 4 if dt in (F32, mybir.dt.int32) else 2
        nby = int(np.prod(shape[1:])) * esz
        c = self._cls(nby)
        key = ("sb", c)
        lst = self.freel.get(key)
        if lst:
            t = lst.pop()
        else:
            try:
                t = self.ctx.enter_context(self.nc.sbuf_tensor(self.name("pl_" + base), [128, c], I8))
                import os as _o
                if _o.environ.get("KPOOL"):
                    print(f"NEWBLK {t.name} cls={c}")
            except AssertionError:
                for c2 in BCLS:
                    if c2 > c and self.freel.get(("sb", c2)):
                        t = self.freel[("sb", c2)].pop()
                        key = ("sb", c2)
                        break
                else:
                    fr = {}
                    us = {}
                    for kk, vv in self.freel.items():
                        if vv:
                            fr[str(kk)] = len(vv)
                    for nmm, (kk2, _) in self.used.items():
                        us.setdefault(str(kk2), []).append(nmm)
                    print("POOL free:", fr)
                    print("POOL used:", {kk: vv for kk, vv in sorted(us.items())})
                    raise
        self.used[t.name] = (key, t)
        return self._view(t, shape, dt)

    def gps(self, base, shape, dt=F32):
        c = 512 if dt == F32 else 1024
        assert int(np.prod(shape[1:])) <= c, shape
        key = ("ps", str(dt), c)
        lst = self.freel.get(key)
        if lst:
            t = lst.pop()
        else:
            t = self.ctx.enter_context(self.nc.psum_tensor(self.name("pp_" + base), [128, c], dt))
        self.used[t.name] = (key, t)
        return self._view_ps(t, shape)

    def _view_ps(self, t, shape):
        nel = int(np.prod(shape[1:])) if len(shape) > 1 else 1
        v = t[:, :nel]
        if len(shape) == 3:
            return v.rearrange("q (a b) -> q a b", a=shape[1])
        if shape[0] != 128:
            return t[:shape[0], :shape[1]]
        return v

    def rel(self, *views):
        for v in views:
            nm = v.name
            key, t = self.used.pop(nm)
            self.freel.setdefault(key, []).append(t)

    _cvals = None

    def cval(self, value):
        if self._cvals is None:
            self._cvals = {}
        if value not in self._cvals:
            t = self.sb(f"cv_{len(self._cvals)}", [128, 1], F32)
            self.k.ins("vector", lambda e, t=t, v=value: e.memset(t[:, :], v), rd=[], wr=[t])
            self._cvals[value] = t
        return self._cvals[value][:, :]


def ecopy(e, out, in_):
    if hasattr(e, "tensor_copy"):
        return e.tensor_copy(out, in_)
    return e.copy(out, in_)


def load_bf(p, src, shape_sb, label, rearr=None):
    t = p.get(label, shape_sb, BF16)
    s = src if rearr is None else src.rearrange(rearr[0], **rearr[1])
    p.k.ins("gpsimd", lambda e, d=t, s=s: e.dma_start(d, s), rd=[s], wr=[t], dma=True, label=label)
    return t

def load_f32(p, src, shape_sb, label, rearr=None):
    t = p.get(label, shape_sb, F32)
    s = src if rearr is None else src.rearrange(rearr[0], **rearr[1])
    p.k.ins("sync", lambda e, d=t, s=s: e.dma_start(d, s), rd=[s], wr=[t], dma=True, label=label)
    return t

def load_w_chunks(p, src, Kdim, N, label, into=None):
    nk = Kdim // 128
    s = src.rearrange("(kc kp) n -> kc kp n", kp=128)
    t = into if into is not None else p.get(label, [128, nk, N], BF16)
    for kc in range(nk):
        p.k.ins("gpsimd", lambda e, d=t, s=s, kc=kc: e.dma_start(d[:, kc, :], s[kc]),
                rd=[s], wr=[t], dma=True, label=f"{label}k{kc}")
    return t


def transpose_pairs(p, pairs, consts, label=""):
    k = p.k
    ptr = consts["ptr"]
    for n, (s_ap, d_ap, s_t, d_t) in enumerate(pairs):
        ps = ptr[n % len(ptr)]
        k.ins("tensor", lambda e, o=ps, i=s_ap: e.transpose(o[:i.shape[1], :i.shape[0]], i, consts["ident"][:, :]),
              rd=[s_t, consts["ident"], ps], wr=[ps], label="tr" + label)
        k.ins(k.evac_eng(), lambda e, o=d_ap, i=ps: ecopy(e, o, i[:o.shape[0], :o.shape[1]]),
              rd=[ps], wr=[d_t], label="te" + label)


def tm_to_fm(p, dst_fm, src_tm, consts, nj=NJ, ni=NI, label="t2f"):
    pairs = []
    for j in range(nj):
        for i in range(ni):
            pairs.append((src_tm[:, j, 128 * i:128 * (i + 1)],
                          dst_fm[:, i, 128 * j:128 * (j + 1)], src_tm, dst_fm))
    transpose_pairs(p, pairs, consts, label)


def tm_to_fm_cast(p, dst_fm, src_tm_f32, consts, nj=NJ, ni=NI, label="t2fc"):
    k = p.k
    cs = [p.get(label + "c", [128, ni * 128], BF16) for _ in range(2)]
    for j in range(nj):
        c = cs[j % 2]
        k.ins("scalar", lambda e, j=j, c=c: e.copy(c[:, :], src_tm_f32[:, j, :]),
              rd=[src_tm_f32], wr=[c])
        pairs = [(c[:, 128 * i:128 * (i + 1)], dst_fm[:, i, 128 * j:128 * (j + 1)], c, dst_fm)
                 for i in range(ni)]
        transpose_pairs(p, pairs, consts, label)
    p.rel(cs[0], cs[1])


def bn_meanvar(p, mv, src_tm, j, width, scratch, label="bn"):
    k = p.k
    if width <= 512:
        k.ins("vector", lambda e: e.bn_stats(scratch[:, 0:6], src_tm[:, j, 0:width]), rd=[src_tm], wr=[scratch])
        k.ins("vector", lambda e: e.bn_aggr(mv[:, j, :], scratch[:, 0:6]), rd=[scratch], wr=[mv])
    else:
        k.ins("vector", lambda e: e.bn_stats(scratch[:, 0:6], src_tm[:, j, 0:512]), rd=[src_tm], wr=[scratch])
        k.ins("vector", lambda e: e.bn_stats(scratch[:, 6:12], src_tm[:, j, 512:1024]), rd=[src_tm], wr=[scratch])
        k.ins("vector", lambda e: e.bn_aggr(mv[:, j, :], scratch[:, 0:12]), rd=[scratch], wr=[mv])


def layernorm_tm(p, xn_tm, x_tm, consts, eps=1e-5, label="ln"):
    k = p.k
    mv = p.get(label + "mv", [128, NJ, 2], F32)
    rstd = p.get(label + "rs", [128, NJ], F32)
    nmean = p.get(label + "nm", [128, NJ], F32)
    scr = p.get(label + "sc", [128, 12], F32)
    for j in range(NJ):
        bn_meanvar(p, mv, x_tm, j, D, scr, label)
    epsap = p.cval(float(eps))
    k.ins("scalar", lambda e: e.activation(rstd[:, :], mv[:, :, 1], AF.Sqrt, bias=epsap),
          rd=[mv, epsap], wr=[rstd])
    k.ins("vector", lambda e: e.reciprocal(rstd[:, :], rstd[:, :]), rd=[rstd], wr=[rstd])
    k.ins("vector", lambda e: e.tensor_scalar_mul(nmean[:, :], mv[:, :, 0], -1.0), rd=[mv], wr=[nmean])
    for j in range(NJ):
        k.ins("vector", lambda e, j=j: e.tensor_scalar(
            xn_tm[:, j, :], x_tm[:, j, :], nmean[:, j:j + 1], rstd[:, j:j + 1], ALU.add, ALU.mult),
            rd=[x_tm], srd=[nmean, rstd], wr=[xn_tm])
    p.rel(mv, rstd, nmean, scr)


def rms_apply(p, dst_bf, src_tm, wrep, eps, label):
    k = p.k
    mv = p.get(label + "m", [128, NJ, 2], F32)
    rs = p.get(label + "r", [128, NJ], F32)
    tmp = p.get(label + "t", [128, D], F32)
    scr12 = p.get(label + "sc", [128, 12], F32)
    for j in range(NJ):
        bn_meanvar(p, mv, src_tm, j, D, scr12, label)
    k.ins("vector", lambda e: e.tensor_mul(rs[:, :], mv[:, :, 0], mv[:, :, 0]), rd=[mv], wr=[rs])
    k.ins("vector", lambda e: e.tensor_add(rs[:, :], rs[:, :], mv[:, :, 1]), rd=[mv, rs], wr=[rs])
    epsap = p.cval(float(eps))
    k.ins("scalar", lambda e: e.activation(rs[:, :], rs[:, :], AF.Sqrt, bias=epsap), rd=[rs, epsap], wr=[rs])
    k.ins("vector", lambda e: e.reciprocal(rs[:, :], rs[:, :]), rd=[rs], wr=[rs])
    for j in range(NJ):
        k.ins("vector", lambda e, j=j: e.tensor_scalar_mul(tmp[:], src_tm[:, j, :], rs[:, j:j + 1]),
              rd=[src_tm], srd=[rs], wr=[tmp])
        k.ins("vector", lambda e, j=j: e.tensor_mul(dst_bf[:, j, :], tmp[:], wrep[:]),
              rd=[tmp, wrep], wr=[dst_bf])
    p.rel(mv, rs, tmp, scr12)


def cast_tm(p, dst, src, nj=NJ):
    for j in range(nj):
        p.k.ins("scalar", lambda e, j=j: e.copy(dst[:, j, :], src[:, j, :]), rd=[src], wr=[dst])


def replicate_row(p, dst_sb, row_sb, ncols, consts, ps, label="rep"):
    k = p.k
    for b in range(0, ncols, 512):
        n = min(512, ncols - b)
        k.ins("tensor", lambda e, b=b, n=n: e.matmul(
            ps[:, :n], consts["ones_row32"][:, :], row_sb[:, b:b + n], start=True, stop=True),
            rd=[consts["ones_row32"], row_sb, ps], wr=[ps], label=label)
        k.ins(k.evac_eng(), lambda e, b=b, n=n: ecopy(e, dst_sb[:, b:b + n], ps[:, :n]),
              rd=[ps], wr=[dst_sb], label=label + "e")

def load_rep(p, src, ncols, consts, ps, label):
    row = load_f32(p, src, [1, ncols], label + "r")
    rep = p.get(label, [128, ncols], F32)
    replicate_row(p, rep, row, ncols, consts, ps, label)
    p.rel(row)
    return rep


def setup_consts(p, aps):
    c = {}
    c["ident"] = load_bf(p, aps["c_ident"], [128, 128], "ident")
    c["triu"] = load_bf(p, aps["c_triu01"], [128, 128], "triu")
    c["tril"] = load_bf(p, aps["c_tril01"], [128, 128], "tril")
    c["upstrict"] = load_bf(p, aps["c_upstrict"], [128, 128], "upstrict")
    c["ones_col_bf"] = load_bf(p, aps["c_ones_col"], [128, 1], "onescolb")
    c["zeros_row_bf"] = p.sb("zrow", [1, 1024], BF16)
    p.k.ins("vector", lambda e: e.memset(c["zeros_row_bf"][:, :], 0.0), rd=[], wr=[c["zeros_row_bf"]])
    c["ones_row_bf"] = p.sb("orow", [1, 128], BF16)
    p.k.ins("vector", lambda e: e.memset(c["ones_row_bf"][:, :], 1.0), rd=[], wr=[c["ones_row_bf"]])
    c["ones_row32"] = load_f32(p, aps["c_ones_row"], [1, 128], "onesrow32")
    c["pos"] = load_f32(p, aps["c_pos"], [128, NJ], "pos")
    c["ptr"] = [p.gps("ptr", [128, 128], BF16) for _ in range(2)]
    return c


def causal_attention(p, consts, nheads, dv, getK, getQ, getV, epilogue, scale, pools, label):
    k = p.k
    psc_pool, pctx, pden, probs_pool, dinv = pools
    denr = p.get(label + "dr", [1, 512], F32)
    for h in range(nheads):
        for tb in range(4):
            nkb = 4 * (tb + 1)

            def qk_exp(kb):
                pr = probs_pool[kb % len(probs_pool)]
                psc = psc_pool[kb % len(psc_pool)]
                ksl = getK(h, kb)
                qsl = getQ(h, tb)
                k.ins("tensor", lambda e, ksl=ksl, qsl=qsl, psc=psc: e.matmul(
                    psc[:, :512], ksl, qsl, start=True, stop=True),
                    rd=[ksl, qsl, psc], wr=[psc], label=label + "qk")
                k.ins("scalar", lambda e, pr=pr, psc=psc: e.activation(pr[:, :], psc[:, :512], AF.Exp, scale=scale),
                      rd=[psc], wr=[pr], label=label + "exp")
                for sub in range(4):
                    tq0 = 512 * tb + 128 * sub
                    if 128 * kb > tq0:
                        k.ins("vector", lambda e, pr=pr, sub=sub: e.memset(pr[:, 128 * sub:128 * (sub + 1)], 0.0),
                              rd=[], wr=[pr], label=label + "z")
                    elif 128 * kb == tq0:
                        k.ins("vector", lambda e, pr=pr, sub=sub: e.tensor_mul(
                            pr[:, 128 * sub:128 * (sub + 1)], pr[:, 128 * sub:128 * (sub + 1)],
                            consts["triu"][:, :]), rd=[consts["triu"]], wr=[pr], label=label + "m")

            def den_pv(kb):
                pr = probs_pool[kb % len(probs_pool)]
                k.ins("tensor", lambda e, pr=pr, kb=kb, nkb=nkb: e.matmul(
                    pden[0:1, 0:512], consts["ones_col_bf"][:, :], pr[:, :],
                    start=(kb == 0), stop=(kb == nkb - 1)),
                    rd=[pr, consts["ones_col_bf"], pden], wr=[pden], label=label + "dn")
                vsl = getV(h, kb)
                k.ins("tensor", lambda e, vsl=vsl, pr=pr, kb=kb, nkb=nkb: e.matmul(
                    pctx[:dv, :512], vsl, pr[:, :], start=(kb == 0), stop=(kb == nkb - 1)),
                    rd=[vsl, pr, pctx], wr=[pctx], label=label + "pv")

            qk_exp(0)
            for kb in range(1, nkb):
                qk_exp(kb)
                den_pv(kb - 1)
            den_pv(nkb - 1)
            k.ins("vector", lambda e: e.reciprocal(denr[0:1, :], pden[0:1, 0:512]), rd=[pden], wr=[denr],
                  label=label + "di")
            for sub in range(4):
                k.ins("tensor", lambda e, sub=sub: e.matmul(
                    pden[:, 1:2], denr[0:1, 128 * sub:128 * (sub + 1)], consts["ones_row32"][0:1, 0:1],
                    start=True, stop=True), rd=[denr, consts["ones_row32"], pden], wr=[pden],
                    label=label + "dt")
                k.ins("vector", lambda e, sub=sub: e.tensor_copy(dinv[:, sub:sub + 1], pden[:, 1:2]),
                      rd=[pden], wr=[dinv], label=label + "dc")
            epilogue(h, tb, pctx, dinv)
    p.rel(denr)


def stream_ln_to_fm(p, x_src_dram, xn_fm, consts, xn_spill=None, eps=1e-5, label="sln",
                    grep=None, brep=None, post_cb=None, load_cb=None):
    k = p.k
    xsv = x_src_dram.rearrange("(j q) c -> q j c", q=128) if x_src_dram is not None else None
    spv = xn_spill.rearrange("(j q) c -> q j c", q=128) if xn_spill is not None else None
    xt = [p.get(label + "x", [128, D], F32) for _ in range(2)]
    nt = [p.get(label + "n", [128, D], F32) for _ in range(2)]
    bt = [p.get(label + "b", [128, D], BF16) for _ in range(2)]
    mv = p.get(label + "mv", [128, 1, 2], F32)
    scr12 = p.get(label + "sc", [128, 12], F32)
    rstd = p.get(label + "rs", [128, 1], F32)
    nmean = p.get(label + "nm", [128, 1], F32)
    epsap = p.cval(float(eps))
    for j in range(NJ):
        xv, nv, bv = xt[j % 2], nt[j % 2], bt[j % 2]
        if load_cb is not None:
            load_cb(j, xv)
        else:
            k.ins("sync", lambda e, j=j, xv=xv: e.dma_start(xv[:, :], xsv[:, j, :]),
                  rd=[x_src_dram], wr=[xv], dma=True)
        k.ins("vector", lambda e, xv=xv: e.bn_stats(scr12[:, 0:6], xv[:, 0:512]), rd=[xv], wr=[scr12])
        k.ins("vector", lambda e, xv=xv: e.bn_stats(scr12[:, 6:12], xv[:, 512:1024]), rd=[xv], wr=[scr12])
        k.ins("vector", lambda e: e.bn_aggr(mv[:, 0, :], scr12[:, 0:12]), rd=[scr12], wr=[mv])
        k.ins("scalar", lambda e: e.activation(rstd[:, :], mv[:, 0, 1:2], AF.Sqrt, bias=epsap),
              rd=[mv, epsap], wr=[rstd])
        k.ins("vector", lambda e: e.reciprocal(rstd[:, :], rstd[:, :]), rd=[rstd], wr=[rstd])
        k.ins("vector", lambda e: e.tensor_scalar_mul(nmean[:, :], mv[:, 0, 0:1], -1.0), rd=[mv], wr=[nmean])
        k.ins("vector", lambda e, xv=xv, nv=nv: e.tensor_scalar(
            nv[:, :], xv[:, :], nmean[:, 0:1], rstd[:, 0:1], ALU.add, ALU.mult),
            rd=[xv], srd=[nmean, rstd], wr=[nv])
        if grep is not None:
            k.ins("vector", lambda e, nv=nv: e.tensor_mul(nv[:, :], nv[:, :], grep[:, :]),
                  rd=[grep, nv], wr=[nv])
        if brep is not None:
            k.ins("vector", lambda e, nv=nv: e.tensor_add(nv[:, :], nv[:, :], brep[:, :]),
                  rd=[brep, nv], wr=[nv])
        if spv is not None:
            k.ins("sync", lambda e, j=j, nv=nv: e.dma_start(spv[:, j, :], nv[:, :]),
                  rd=[nv], wr=[xn_spill], dma=True)
        if post_cb is not None:
            post_cb(j, nv)
        k.ins("scalar", lambda e, nv=nv, bv=bv: e.copy(bv[:, :], nv[:, :]), rd=[nv], wr=[bv])
        pairs = [(bv[:, 128 * i:128 * (i + 1)], xn_fm[:, i, 128 * j:128 * (j + 1)], bv, xn_fm)
                 for i in range(NI)]
        transpose_pairs(p, pairs, consts, label)
    p.rel(xt[0], xt[1], nt[0], nt[1], bt[0], bt[1], mv, scr12, rstd, nmean)


def row_to_cols(p, dst_col, row_sb, n, consts, ps, label="r2c"):
    k = p.k
    one11 = consts["ones_row32"][0:1, 0:1]
    nck = (n + 127) // 128
    for ci in range(nck):
        w = min(128, n - 128 * ci)
        k.ins("tensor", lambda e, ci=ci, w=w: e.matmul(
            ps[:w, 0:1], row_sb[0:1, 128 * ci:128 * ci + w], one11, start=True, stop=True),
            rd=[row_sb, consts["ones_row32"], ps], wr=[ps], label=label)
        k.ins(k.evac_eng(), lambda e, ci=ci, w=w: ecopy(e, dst_col[:w, ci:ci + 1], ps[:w, 0:1]),
              rd=[ps], wr=[dst_col], label=label + "e")


def stream_rms_to_fm(p, src_dram, dst_fm, wrep, eps, consts, label="srms"):
    k = p.k
    xsv = src_dram.rearrange("(j q) c -> q j c", q=128)
    xt = [p.get(label + "x", [128, D], F32) for _ in range(2)]
    bt = [p.get(label + "b", [128, D], BF16) for _ in range(2)]
    mv = p.get(label + "mv", [128, 1, 2], F32)
    scr12 = p.get(label + "sc", [128, 12], F32)
    rs = p.get(label + "rs", [128, 1], F32)
    epsap = p.cval(float(eps))
    for j in range(NJ):
        xv, bv = xt[j % 2], bt[j % 2]
        k.ins("sync", lambda e, j=j, xv=xv: e.dma_start(xv[:, :], xsv[:, j, :]),
              rd=[src_dram], wr=[xv], dma=True)
        k.ins("vector", lambda e, xv=xv: e.bn_stats(scr12[:, 0:6], xv[:, 0:512]), rd=[xv], wr=[scr12])
        k.ins("vector", lambda e, xv=xv: e.bn_stats(scr12[:, 6:12], xv[:, 512:1024]), rd=[xv], wr=[scr12])
        k.ins("vector", lambda e: e.bn_aggr(mv[:, 0, :], scr12[:, 0:12]), rd=[scr12], wr=[mv])
        k.ins("vector", lambda e: e.tensor_mul(rs[:, :], mv[:, 0, 0:1], mv[:, 0, 0:1]), rd=[mv], wr=[rs])
        k.ins("vector", lambda e: e.tensor_add(rs[:, :], rs[:, :], mv[:, 0, 1:2]), rd=[mv, rs], wr=[rs])
        k.ins("scalar", lambda e: e.activation(rs[:, :], rs[:, :], AF.Sqrt, bias=epsap), rd=[rs, epsap], wr=[rs])
        k.ins("vector", lambda e: e.reciprocal(rs[:, :], rs[:, :]), rd=[rs], wr=[rs])
        k.ins("vector", lambda e, xv=xv: e.tensor_scalar_mul(xv[:, :], xv[:, :], rs[:, 0:1]),
              rd=[xv], srd=[rs], wr=[xv])
        k.ins("vector", lambda e, xv=xv, bv=bv: e.tensor_mul(bv[:, :], xv[:, :], wrep[:, :]),
              rd=[xv, wrep], wr=[bv])
        pairs = [(bv[:, 128 * i:128 * (i + 1)], dst_fm[:, i, 128 * j:128 * (j + 1)], bv, dst_fm)
                 for i in range(NI)]
        transpose_pairs(p, pairs, consts, label)
    p.rel(xt[0], xt[1], bt[0], bt[1], mv, scr12, rs)



def emit_L0(p, aps, consts, state, dram):
    k = p.k
    x_src = state["x_src"]

    ps = [p.gps("aps", [128, 512], F32) for _ in range(4)]
    pden = p.gps("apd", [128, 512], F32)
    xn_fm = p.get("l0_xnf", [128, NI, T], BF16)
    stream_ln_to_fm(p, x_src, xn_fm, consts, xn_spill=dram["xn"], eps=1e-5, label="l0ln")

    W_uq = load_w_chunks(p, aps["L0_W_uq"], 1024, 128, "wuq")
    W_uk = load_w_chunks(p, aps["L0_W_uk"], 1024, 128, "wuk")
    M1 = p.get("M1", [128, 128], BF16)
    for ck in range(8):
        k.ins("tensor", lambda e, ck=ck: e.matmul(ps[0][:, :128], W_uq[:, ck, :], W_uk[:, ck, :],
                                                  start=(ck == 0), stop=(ck == 7)),
              rd=[W_uq, W_uk, ps[0]], wr=[ps[0]])
    k.ins("scalar", lambda e: e.copy(M1[:, :], ps[0][:, :128]), rd=[ps[0]], wr=[M1])
    p.rel(W_uq, W_uk)
    W_dq = load_bf(p, aps["L0_W_dq"], [128, 1024], "wdq")
    keff = p.get("keff", [128, 8, 128], BF16)
    for ci in range(8):
        k.ins("tensor", lambda e, ci=ci: e.matmul(ps[1][:, :128], W_dq[:, 128 * ci:128 * (ci + 1)],
                                                  M1[:, :], start=True, stop=True),
              rd=[W_dq, M1, ps[1]], wr=[ps[1]])
        k.ins(k.evac_eng(), lambda e, ci=ci: ecopy(e, keff[:, ci, :], ps[1][:, :128]),
              rd=[ps[1]], wr=[keff])
    p.rel(W_dq, M1)
    W_uv = load_w_chunks(p, aps["L0_W_uv"], 1024, 128, "wuv")
    W_oT = load_w_chunks(p, aps["L0_W_oT"], 1024, 1024, "woT")
    Vf = p.get("Vf", [128, 1024], BF16)
    for nb in range(2):
        for ck in range(8):
            k.ins("tensor", lambda e, nb=nb, ck=ck: e.matmul(
                ps[0][:, :512], W_uv[:, ck, :], W_oT[:, ck, 512 * nb:512 * (nb + 1)],
                start=(ck == 0), stop=(ck == 7)), rd=[W_uv, W_oT, ps[0]], wr=[ps[0]])
        k.ins(k.evac_eng(), lambda e, nb=nb: ecopy(e, Vf[:, 512 * nb:512 * (nb + 1)], ps[0][:, :512]),
              rd=[ps[0]], wr=[Vf])
    p.rel(W_uv, W_oT)
    W_dqT = load_w_chunks(p, aps["L0_W_dqT"], 1024, 128, "wdqT")
    W_uqT = load_bf(p, aps["L0_W_uqT"], [128, 1024], "wuqT")
    W_dkvT = load_w_chunks(p, aps["L0_W_dkvT"], 1024, 128, "wdkvT")
    ckv_fm = p.get("ckvf", [128, T], BF16)
    qlat_fm = p.get("qlatf", [128, T], BF16)
    for tb in range(4):
        for (dst, wch) in ((ckv_fm, W_dkvT), (qlat_fm, W_dqT)):
            for ck in range(8):
                k.ins("tensor", lambda e, tb=tb, wch=wch, ck=ck: e.matmul(
                    ps[0][:, :512], wch[:, ck, :], xn_fm[:, ck, 512 * tb:512 * (tb + 1)],
                    start=(ck == 0), stop=(ck == 7)), rd=[wch, xn_fm, ps[0]], wr=[ps[0]])
            k.ins(k.evac_eng(), lambda e, dst=dst, tb=tb: ecopy(e,
                dst[:, 512 * tb:512 * (tb + 1)], ps[0][:, :512]), rd=[ps[0]], wr=[dst])
    ckv_tm = p.get("ckvt", [128, NJ, 128], BF16)
    transpose_pairs(p, [(ckv_fm[:, 128 * j:128 * (j + 1)], ckv_tm[:, j, :], ckv_fm, ckv_tm)
                        for j in range(NJ)], consts, "ckvT")
    q_fm = p.get("l0qf", [128, 8, T], BF16)
    for ci in range(8):
        for tb in range(4):
            k.ins("tensor", lambda e, ci=ci, tb=tb: e.matmul(
                ps[0][:, :512], W_uqT[:, 128 * ci:128 * (ci + 1)],
                qlat_fm[:, 512 * tb:512 * (tb + 1)], start=True, stop=True),
                rd=[W_uqT, qlat_fm, ps[0]], wr=[ps[0]])
            k.ins(k.evac_eng(), lambda e, ci=ci, tb=tb: ecopy(e,
                q_fm[:, ci, 512 * tb:512 * (tb + 1)], ps[0][:, :512]), rd=[ps[0]], wr=[q_fm])
    p.rel(qlat_fm, W_dqT, W_dkvT, W_uqT)

    ql_h = p.get("l0qlh", [128, T], BF16)
    probs_pool = [p.get("apr", [128, 512], BF16) for _ in range(4)]
    ctx_bf = p.get("l0ctx", [128, 512], BF16)
    dinv = p.get("l0dinv", [128, 4], F32)
    py = p.gps("l0py", [128, 64], F32)
    s_stripe = [p.get("sstA", [128, 4, 64], F32), p.get("sstB", [128, 4, 64], F32)]
    s_dview = dram["s"].rearrange("(j q) c -> q j c", q=128)
    cur_h = [None]

    def mla_getK(h, kb):
        if cur_h[0] != h:
            ci, po = h // 2, (h % 2) * 64
            for tb in range(4):
                k.ins("tensor", lambda e, tb=tb, ci=ci, po=po: e.matmul(
                    ps[2][:, :512], keff[po:po + 64, ci, :], q_fm[po:po + 64, ci, 512 * tb:512 * (tb + 1)],
                    start=True, stop=True), rd=[keff, q_fm, ps[2]], wr=[ps[2]])
                k.ins(k.evac_eng(), lambda e, tb=tb: ecopy(e, ql_h[:, 512 * tb:512 * (tb + 1)], ps[2][:, :512]),
                      rd=[ps[2]], wr=[ql_h])
            cur_h[0] = h
        return ckv_fm[:, 128 * kb:128 * (kb + 1)]

    def mla_getQ(h, tb):
        return ql_h[:, 512 * tb:512 * (tb + 1)]

    def mla_getV(h, kb):
        return ckv_tm[:, kb, :]

    def mla_epi(h, tb, pctx, dinv_):
        sstb = s_stripe[tb % 2]
        k.ins(k.evac_eng(), lambda e: ecopy(e, ctx_bf[:, :], pctx[:, :512]), rd=[pctx], wr=[ctx_bf])
        for sub in range(4):
            k.ins("tensor", lambda e, sub=sub, h=h: e.matmul(
                py[:, :64], ctx_bf[:, 128 * sub:128 * (sub + 1)], Vf[:, 64 * h:64 * (h + 1)],
                start=True, stop=True), rd=[ctx_bf, Vf, py], wr=[py])
            k.ins("vector", lambda e, sub=sub, sstb=sstb: e.tensor_scalar_mul(
                sstb[:, sub, :], py[:, :64], dinv_[:, sub:sub + 1]),
                rd=[py], srd=[dinv_], wr=[sstb])
        k.ins("sync", lambda e, h=h, tb=tb, sstb=sstb: e.dma_start(
            s_dview[:, 4 * tb:4 * (tb + 1), 64 * h:64 * (h + 1)], sstb[:, :, :]),
            rd=[sstb], wr=[dram["s"]], dma=True)

    causal_attention(p, consts, 16, 128, mla_getK, mla_getQ, mla_getV, mla_epi, 0.125,
                     ([ps[0], ps[3]], ps[1], pden, probs_pool, dinv), "mla")
    p.rel(xn_fm, keff, Vf, ckv_fm, ckv_tm, q_fm, ql_h,
          *probs_pool, ctx_bf, dinv, py, s_stripe[0], s_stripe[1],
          ps[0], ps[1], ps[2], ps[3], pden)

    ps = [p.gps("fps", [128, 512], F32) for _ in range(4)]
    pden = p.gps("fpd", [128, 512], F32)
    ln1r = load_rep(p, aps["L0_ln1_w"], D, consts, ps[3], "ln1")
    h_fm = p.get("l0hf", [128, NI, T], BF16)
    stream_rms_to_fm(p, dram["xn"], h_fm, ln1r, 1e-6, consts, "rms1")
    p.rel(ln1r)

    q_tm2 = [p.get("l0qA", [128, NJ, 512], F32), p.get("l0qB", [128, NJ, 512], F32)]
    kv_tm = p.get("l0kv", [128, NJ, 512], F32)
    qnr = load_rep(p, aps["L0_qn_w"], 64, consts, ps[3], "qn")
    knr = load_rep(p, aps["L0_kn_w"], 64, consts, ps[3], "kn")
    Wq = load_w_chunks(p, aps["L0_Wq"], D, D, "Wq")
    Wk = load_w_chunks(p, aps["L0_Wk"], D, 256, "Wk")
    Wv = load_w_chunks(p, aps["L0_Wv"], D, 256, "Wv")
    bq_row = load_f32(p, aps["L0_bq"], [1, D], "bq")
    bk_row = load_f32(p, aps["L0_bk"], [1, 256], "bk")
    bv_row = load_f32(p, aps["L0_bv"], [1, 256], "bv")
    for j in range(NJ):
        for nb in range(2):
            for ck in range(8):
                k.ins("tensor", lambda e, j=j, nb=nb, ck=ck: e.matmul(
                    ps[0][:, :512], h_fm[:, ck, 128 * j:128 * (j + 1)], Wq[:, ck, 512 * nb:512 * (nb + 1)],
                    start=(ck == 0), stop=False), rd=[h_fm, Wq, ps[0]], wr=[ps[0]])
            k.ins("tensor", lambda e, nb=nb: e.matmul(
                ps[0][:, :512], consts["ones_row32"][:, :], bq_row[:, 512 * nb:512 * (nb + 1)],
                start=False, stop=True), rd=[consts["ones_row32"], bq_row, ps[0]], wr=[ps[0]])
            k.ins("vector", lambda e, j=j, nb=nb: e.tensor_copy(
                q_tm2[nb][:, j, :], ps[0][:, :512]), rd=[ps[0]], wr=[q_tm2[nb]])
        for (wch, brow, off, pp) in ((Wk, bk_row, 0, ps[1]), (Wv, bv_row, 256, ps[2])):
            for ck in range(8):
                k.ins("tensor", lambda e, j=j, wch=wch, ck=ck, pp=pp: e.matmul(
                    pp[:, :256], h_fm[:, ck, 128 * j:128 * (j + 1)], wch[:, ck, :],
                    start=(ck == 0), stop=False), rd=[h_fm, wch, pp], wr=[pp])
            k.ins("tensor", lambda e, brow=brow, pp=pp: e.matmul(
                pp[:, :256], consts["ones_row32"][:, :], brow[:, :],
                start=False, stop=True), rd=[consts["ones_row32"], brow, pp], wr=[pp])
            k.ins("vector", lambda e, j=j, off=off, pp=pp: e.tensor_copy(
                kv_tm[:, j, off:off + 256], pp[:, :256]), rd=[pp], wr=[kv_tm])
    p.rel(h_fm, Wq, Wk, Wv, bq_row, bk_row, bv_row)

    ropeC_t = [p.get("rC", [128, D], F32)]
    ropeS_t = [p.get("rS", [128, D], F32)]
    rCview = aps["c_ropeC"].rearrange("(j q) w -> j q w", q=128)
    rSview = aps["c_ropeS"].rearrange("(j q) w -> j q w", q=128)
    scr = p.get("l0scr", [128, 2 * D], F32)
    sqv = p.get("l0sq", [128, NJ, 16], F32)
    skv = p.get("l0sk", [128, NJ, 4], F32)
    qsq = scr[:, D:2 * D]

    def rope2_tab(buf, j, width, Cs, Ss):
        k.ins("vector", lambda e, buf=buf, j=j, width=width, Cs=Cs: e.tensor_mul(
            scr[:, 0:width], buf[:, j, 0:width], Cs), rd=[buf, ropeC_t[0]], wr=[scr])
        nh = width // 64
        bv = buf[:, j, 0:width].rearrange("q (h s d) -> q h s d", h=nh, s=2)
        sv = Ss.rearrange("q (h s d) -> q h s d", h=nh, s=2)
        ov = scr[:, D:D + width].rearrange("q (h s d) -> q h s d", h=nh, s=2)
        k.ins("vector", lambda e, bv=bv, sv=sv, ov=ov: e.tensor_tensor(
            ov[:, :, 0, :], bv[:, :, 1, :], sv[:, :, 0, :], ALU.mult), rd=[buf, ropeS_t[0]], wr=[scr])
        k.ins("vector", lambda e, bv=bv, sv=sv, ov=ov: e.tensor_tensor(
            ov[:, :, 1, :], bv[:, :, 0, :], sv[:, :, 1, :], ALU.mult), rd=[buf, ropeS_t[0]], wr=[scr])
        k.ins("vector", lambda e, buf=buf, j=j, width=width: e.tensor_add(
            buf[:, j, 0:width], scr[:, 0:width], scr[:, D:D + width]), rd=[scr], wr=[buf])

    def rope2(buf, j, width, Ct, St):
        rope2_tab(buf, j, width, Ct[:, 0:width], St[:, 0:width])

    def postnorm(s_):
        k.ins("vector", lambda e, s_=s_: e.tensor_scalar(s_[:, :, :], s_[:, :, :], 1.0 / 64.0, 1e-6, ALU.mult, ALU.add),
              rd=[s_], wr=[s_])
        k.ins("scalar", lambda e, s_=s_: e.activation(s_[:, :, :], s_[:, :, :], AF.Sqrt), rd=[s_], wr=[s_])
        k.ins("vector", lambda e, s_=s_: e.reciprocal(s_[:, :, :], s_[:, :, :]), rd=[s_], wr=[s_])

    Ct, St = ropeC_t[0], ropeS_t[0]
    k_bf = p.get("l0kb", [128, NJ, 256], BF16)
    v_bf = p.get("l0vb", [128, NJ, 256], BF16)
    for j in range(NJ):
        k.ins("sync", lambda e, j=j, Ct=Ct: e.dma_start(Ct[:, 0:256], rCview[j, :, 0:256]),
              rd=[rCview], wr=[Ct], dma=True)
        k.ins("sync", lambda e, j=j, St=St: e.dma_start(St[:, 0:256], rSview[j, :, 0:256]),
              rd=[rSview], wr=[St], dma=True)
        rope2(kv_tm, j, 256, Ct, St)
        k.ins("scalar", lambda e, j=j: e.activation(qsq[:, :256], kv_tm[:, j, 0:256], AF.Square),
              rd=[kv_tm], wr=[scr])
        k.ins("vector", lambda e, j=j: e.reduce_sum(skv[:, j, :], qsq[:, :256].rearrange("q (g d) -> q g d", g=4), AX.X),
              rd=[scr], wr=[skv])
    postnorm(skv)
    for j in range(NJ):
        for hh in range(4):
            k.ins("vector", lambda e, j=j, hh=hh: e.tensor_scalar_mul(
                scr[:, 0:64], kv_tm[:, j, 64 * hh:64 * (hh + 1)], skv[:, j, hh:hh + 1]),
                rd=[kv_tm], srd=[skv], wr=[scr])
            k.ins("vector", lambda e, j=j, hh=hh: e.tensor_mul(
                k_bf[:, j, 64 * hh:64 * (hh + 1)], scr[:, 0:64], knr[:, :]), rd=[scr, knr], wr=[k_bf])
        k.ins("scalar", lambda e, j=j: e.copy(v_bf[:, j, :], kv_tm[:, j, 256:512]), rd=[kv_tm], wr=[v_bf])
    p.rel(kv_tm, knr, skv)

    q_bf = p.get("l0qb", [128, NJ, D], BF16)
    for j in range(NJ):
        k.ins("sync", lambda e, j=j, Ct=Ct: e.dma_start(Ct[:, :], rCview[j]), rd=[rCview], wr=[Ct], dma=True)
        k.ins("sync", lambda e, j=j, St=St: e.dma_start(St[:, :], rSview[j]), rd=[rSview], wr=[St], dma=True)
        for half in range(2):
            qt = q_tm2[half]
            rope2_tab(qt, j, 512, Ct[:, 512 * half:512 * (half + 1)], St[:, 512 * half:512 * (half + 1)])
            k.ins("scalar", lambda e, j=j, qt=qt: e.activation(qsq[:, 0:512], qt[:, j, :], AF.Square),
                  rd=[qt], wr=[scr])
            k.ins("vector", lambda e, j=j, half=half: e.reduce_sum(
                sqv[:, j, 8 * half:8 * (half + 1)], qsq[:, 0:512].rearrange("q (g d) -> q g d", g=8), AX.X),
                rd=[scr], wr=[sqv])
    postnorm(sqv)
    for j in range(NJ):
        for hh in range(16):
            qt = q_tm2[hh // 8]
            co = 64 * (hh % 8)
            k.ins("vector", lambda e, j=j, hh=hh, qt=qt, co=co: e.tensor_scalar_mul(
                scr[:, 0:64], qt[:, j, co:co + 64], sqv[:, j, hh:hh + 1]),
                rd=[qt], srd=[sqv], wr=[scr])
            k.ins("vector", lambda e, j=j, hh=hh: e.tensor_mul(
                q_bf[:, j, 64 * hh:64 * (hh + 1)], scr[:, 0:64], qnr[:, :]), rd=[scr, qnr], wr=[q_bf])
    p.rel(q_tm2[0], q_tm2[1], qnr, ropeC_t[0], ropeS_t[0], scr, sqv)

    Wg = load_bf(p, aps["L0_Wg"], [64, 64], "Wg0")
    bgr = load_rep(p, aps["L0_bg"], 64, consts, ps[3], "bg")
    q_fm2 = p.get("l0qf2", [128, NI, T], BF16)
    tm_to_fm(p, q_fm2, q_bf, consts)
    p.rel(q_bf)
    k_fm2 = p.get("l0kf2", [128, 2, T], BF16)
    transpose_pairs(p, [(k_bf[:, j, 128 * i:128 * (i + 1)], k_fm2[:, i, 128 * j:128 * (j + 1)], k_bf, k_fm2)
                        for j in range(NJ) for i in range(2)], consts, "kT")
    p.rel(k_bf)
    gat_tm = p.get("l0gat", [128, NJ, D], BF16)
    probs_pool = [p.get("fpr", [128, 512], BF16) for _ in range(4)]
    ctx_bf2 = p.get("fctx", [128, 512], BF16)
    dinv2 = p.get("fdi", [128, 4], F32)
    pg = p.gps("fpg", [128, 64], F32)
    gt = p.get("fgt", [128, 64], F32)
    ct = p.get("fct", [128, 64], F32)

    qsc = p.get("l0qsc", [128, 512], BF16)
    qsc_cur = [None]

    def gqa_getK(h, kb):
        kvh = h // 4
        ki, ko = kvh // 2, (kvh % 2) * 64
        return k_fm2[ko:ko + 64, ki, 128 * kb:128 * (kb + 1)]

    def gqa_getQ(h, tb):
        qi, qo = h // 2, (h % 2) * 64
        ko = ((h // 4) % 2) * 64
        if qo == ko:
            return q_fm2[qo:qo + 64, qi, 512 * tb:512 * (tb + 1)]
        if qsc_cur[0] != (h, tb):
            k.ins("vector", lambda e, qi=qi, qo=qo, ko=ko, tb=tb: e.tensor_copy(
                qsc[ko:ko + 64, :], q_fm2[qo:qo + 64, qi, 512 * tb:512 * (tb + 1)]),
                rd=[q_fm2], wr=[qsc])
            qsc_cur[0] = (h, tb)
        return qsc[ko:ko + 64, :]

    def gqa_getV(h, kb):
        kvh = h // 4
        return v_bf[:, kb, 64 * kvh:64 * (kvh + 1)]

    def gqa_epi(h, tb, pctx, dinv_):
        k.ins(k.evac_eng(), lambda e: ecopy(e, ctx_bf2[:64, :], pctx[:64, :512]), rd=[pctx], wr=[ctx_bf2])
        for sub in range(4):
            j = 4 * tb + sub
            k.ins("tensor", lambda e, sub=sub: e.matmul(
                pg[:, :64], ctx_bf2[:64, 128 * sub:128 * (sub + 1)], Wg[:, :], start=True, stop=True),
                rd=[ctx_bf2, Wg, pg], wr=[pg])
            k.ins("vector", lambda e, sub=sub: e.tensor_scalar_mul(gt[:, :], pg[:, :64], dinv_[:, sub:sub + 1]),
                  rd=[pg], srd=[dinv_], wr=[gt])
            k.ins("vector", lambda e: e.tensor_add(gt[:, :], gt[:, :], bgr[:, :]), rd=[bgr, gt], wr=[gt])
            k.ins("scalar", lambda e: e.activation(gt[:, :], gt[:, :], AF.Sigmoid), rd=[gt], wr=[gt])
            k.ins("tensor", lambda e, sub=sub: e.transpose(
                consts["ptr"][0][:, :64], ctx_bf2[:64, 128 * sub:128 * (sub + 1)], consts["ident"][:64, :64]),
                rd=[ctx_bf2, consts["ident"], consts["ptr"][0]], wr=[consts["ptr"][0]])
            k.ins("vector", lambda e, sub=sub: e.tensor_scalar_mul(
                ct[:, :], consts["ptr"][0][:, :64], dinv_[:, sub:sub + 1]),
                rd=[consts["ptr"][0]], srd=[dinv_], wr=[ct])
            k.ins("vector", lambda e, j=j, h=h: e.tensor_mul(
                gat_tm[:, j, 64 * h:64 * (h + 1)], ct[:, :], gt[:, :]), rd=[ct, gt], wr=[gat_tm])

    causal_attention(p, consts, 16, 64, gqa_getK, gqa_getQ, gqa_getV, gqa_epi, 0.125,
                     ([ps[0], ps[3]], ps[1], pden, probs_pool, dinv2), "gqa")
    p.rel(q_fm2, k_fm2, v_bf, *probs_pool, ctx_bf2, dinv2, pg, gt, ct, Wg, bgr, qsc)

    gat_fm = p.get("l0gatf", [128, NI, T], BF16)
    tm_to_fm(p, gat_fm, gat_tm, consts)
    p.rel(gat_tm)
    Wo = load_w_chunks(p, aps["L0_Wo"], D, D, "Wo0")
    bo_row = load_f32(p, aps["L0_bo"], [1, D], "bo")
    ln2r = load_rep(p, aps["L0_ln2_w"], D, consts, ps[3], "ln2")
    xnt = [p.get("xnt", [128, D], F32) for _ in range(2)]
    x1t = [p.get("x1t", [128, D], F32) for _ in range(2)]
    h2b = [p.get("h2b", [128, D], BF16) for _ in range(2)]
    mv1 = p.get("h2mv", [128, 1, 2], F32)
    sc12 = p.get("h2sc", [128, 12], F32)
    rs1 = p.get("h2rs", [128, 1], F32)
    eps6 = p.cval(1e-6)
    h2_fm = p.get("l0h2f", [128, NI, T], BF16)
    onehot = p.get("l0oh", [128, NJ, 8], F32)
    ohT = p.get("l0ohT", [8, T], BF16)
    xn_dview = dram["xn"].rearrange("(j q) c -> q j c", q=128)
    acc_dview = dram["acc"].rearrange("(j q) c -> q j c", q=128)
    for j in range(NJ):
        xv, x1v, bv = xnt[j % 2], x1t[j % 2], h2b[j % 2]
        k.ins("sync", lambda e, j=j, xv=xv: e.dma_start(xv[:, :], xn_dview[:, j, :]),
              rd=[dram["xn"]], wr=[xv], dma=True)
        for nb in range(2):
            for ck in range(8):
                k.ins("tensor", lambda e, j=j, nb=nb, ck=ck: e.matmul(
                    ps[nb][:, :512], gat_fm[:, ck, 128 * j:128 * (j + 1)], Wo[:, ck, 512 * nb:512 * (nb + 1)],
                    start=(ck == 0), stop=False), rd=[gat_fm, Wo, ps[nb]], wr=[ps[nb]])
            k.ins("tensor", lambda e, j=j, nb=nb: e.matmul(
                ps[nb][:, :512], consts["ones_row32"][:, :], bo_row[:, 512 * nb:512 * (nb + 1)],
                start=False, stop=True), rd=[consts["ones_row32"], bo_row, ps[nb]], wr=[ps[nb]])
            k.ins("vector", lambda e, j=j, nb=nb, xv=xv, x1v=x1v: e.tensor_add(
                x1v[:, 512 * nb:512 * (nb + 1)], ps[nb][:, :512], xv[:, 512 * nb:512 * (nb + 1)]),
                rd=[ps[nb], xv], wr=[x1v])
        k.ins("sync", lambda e, j=j, x1v=x1v: e.dma_start(acc_dview[:, j, :], x1v[:, :]),
              rd=[x1v], wr=[dram["acc"]], dma=True)
        k.ins("vector", lambda e, x1v=x1v: e.bn_stats(sc12[:, 0:6], x1v[:, 0:512]), rd=[x1v], wr=[sc12])
        k.ins("vector", lambda e, x1v=x1v: e.bn_stats(sc12[:, 6:12], x1v[:, 512:1024]), rd=[x1v], wr=[sc12])
        k.ins("vector", lambda e: e.bn_aggr(mv1[:, 0, :], sc12[:, 0:12]), rd=[sc12], wr=[mv1])
        k.ins("vector", lambda e: e.tensor_mul(rs1[:, :], mv1[:, 0, 0:1], mv1[:, 0, 0:1]), rd=[mv1], wr=[rs1])
        k.ins("vector", lambda e: e.tensor_add(rs1[:, :], rs1[:, :], mv1[:, 0, 1:2]), rd=[mv1, rs1], wr=[rs1])
        k.ins("scalar", lambda e: e.activation(rs1[:, :], rs1[:, :], AF.Sqrt, bias=eps6), rd=[rs1, eps6], wr=[rs1])
        k.ins("vector", lambda e: e.reciprocal(rs1[:, :], rs1[:, :]), rd=[rs1], wr=[rs1])
        k.ins("vector", lambda e, x1v=x1v: e.tensor_scalar_mul(x1v[:, :], x1v[:, :], rs1[:, 0:1]),
              rd=[x1v], srd=[rs1], wr=[x1v])
        k.ins("vector", lambda e, x1v=x1v, bv=bv: e.tensor_mul(bv[:, :], x1v[:, :], ln2r[:, :]),
              rd=[x1v, ln2r], wr=[bv])
        transpose_pairs(p, [(bv[:, 128 * i:128 * (i + 1)], h2_fm[:, i, 128 * j:128 * (j + 1)], bv, h2_fm)
                            for i in range(NI)], consts, "h2t")
    p.rel(gat_fm, Wo, bo_row, ln2r, xnt[0], xnt[1], x1t[0], x1t[1], h2b[0], h2b[1], mv1, sc12, rs1)

    Wr = load_bf(p, aps["L0_Wr"], [128, 8, 8], "Wr", ("(kc kp) n -> kp kc n", dict(kp=128)))
    brr = load_rep(p, aps["L0_br"], 8, consts, ps[3], "br")
    logits = p.get("l0lg", [128, NJ, 8], F32)
    mxr = p.get("l0mx", [128, NJ], F32)
    oh_bf = p.get("l0ohb", [128, NJ, 8], BF16)
    for j in range(NJ):
        for ck in range(8):
            k.ins("tensor", lambda e, j=j, ck=ck: e.matmul(
                ps[1][:, :8], h2_fm[:, ck, 128 * j:128 * (j + 1)], Wr[:, ck, :],
                start=(ck == 0), stop=(ck == 7)), rd=[h2_fm, Wr, ps[1]], wr=[ps[1]])
        k.ins("vector", lambda e, j=j: e.tensor_add(logits[:, j, :], ps[1][:, :8], brr[:, :8]),
              rd=[ps[1], brr], wr=[logits])
    for j in range(NJ):
        k.ins("vector", lambda e, j=j: e.reduce_max(mxr[:, j:j + 1], logits[:, j, :], AX.X),
              rd=[logits], wr=[mxr])
        k.ins("vector", lambda e, j=j: e.tensor_scalar(
            onehot[:, j, :], logits[:, j, :], mxr[:, j:j + 1], None, ALU.is_equal),
            rd=[logits], srd=[mxr], wr=[onehot])
        k.ins("scalar", lambda e, j=j: e.copy(oh_bf[:, j, :], onehot[:, j, :]), rd=[onehot], wr=[oh_bf])
    transpose_pairs(p, [(oh_bf[:, j, :], ohT[:, 128 * j:128 * (j + 1)], oh_bf, ohT)
                        for j in range(NJ)], consts, "ohT")
    p.rel(Wr, brr, logits, mxr, oh_bf)

    W1a = p.get("m0w1a", [128, 4, 4096], BF16)
    W1b = p.get("m0w1b", [128, 4, 4096], BF16)
    W1ch = [(W1a, 0), (W1a, 1), (W1a, 2), (W1a, 3), (W1b, 0), (W1b, 1), (W1b, 2), (W1b, 3)]
    W2 = p.get("m0w2", [128, 16, 1024], BF16)
    u1 = p.get("m0u1", [128, 16, 512], BF16)
    act = u1
    yscr = [p.get("m0ysA", [128, D], F32), p.get("m0ysB", [128, D], F32)]
    W1ap = aps["L0_W1"]
    W2ap = aps["L0_W2"]

    def load_W1(e_):
        s = W1ap[e_].rearrange("(kc kp) n -> kc kp n", kp=128)
        for ck in range(8):
            t, sub = W1ch[ck]
            p.k.ins("gpsimd", lambda e, s=s, ck=ck, t=t, sub=sub: e.dma_start(t[:, sub, :], s[ck]),
                    rd=[s], wr=[t], dma=True, label=f"w1l{e_}_{ck}")

    def load_W2(e_):
        load_w_chunks(p, W2ap[e_], 2048, 1024, f"w2l{e_}", into=W2)

    load_W1(0)
    load_W2(0)
    for e_ in range(8):
        for tb in range(4):
            t0 = 512 * tb
            for fi in range(16):
                pA, pB = ps[fi % 2], ps[2 + fi % 2]
                for ck in range(8):
                    t, sub = W1ch[ck]
                    k.ins("tensor", lambda e, fi=fi, ck=ck, t0=t0, pA=pA, t_=t, sub=sub: e.matmul(
                        pA[:, :512], t_[:, sub, 128 * fi:128 * (fi + 1)], h2_fm[:, ck, t0:t0 + 512],
                        start=(ck == 0), stop=(ck == 7)), rd=[t, h2_fm, pA], wr=[pA])
                k.ins("scalar", lambda e, fi=fi, pA=pA: e.copy(u1[:, fi, :], pA[:, :512]), rd=[pA], wr=[u1])
                for ck in range(8):
                    t, sub = W1ch[ck]
                    k.ins("tensor", lambda e, fi=fi, ck=ck, t0=t0, pB=pB, t_=t, sub=sub: e.matmul(
                        pB[:, :512], t_[:, sub, 2048 + 128 * fi:2048 + 128 * (fi + 1)],
                        h2_fm[:, ck, t0:t0 + 512],
                        start=(ck == 0), stop=(ck == 7)), rd=[t, h2_fm, pB], wr=[pB])
                k.ins("scalar", lambda e, fi=fi, pB=pB: e.activation(pB[:, :512], pB[:, :512], AF.Sigmoid),
                      rd=[pB], wr=[pB])
                k.ins("vector", lambda e, fi=fi, pB=pB: e.tensor_mul(u1[:, fi, :], u1[:, fi, :], pB[:, :512]),
                      rd=[u1, pB], wr=[u1])
            if tb == 3 and e_ < 7:
                load_W1(e_ + 1)
            for sub in range(4):
                j = 4 * tb + sub
                ysb = yscr[sub % 2]
                for nb in range(2):
                    pY = ps[nb]
                    for fi in range(16):
                        k.ins("tensor", lambda e, fi=fi, sub=sub, nb=nb, pY=pY: e.matmul(
                            pY[:, :512], act[:, fi, 128 * sub:128 * (sub + 1)],
                            W2[:, fi, 512 * nb:512 * (nb + 1)],
                            start=(fi == 0), stop=(fi == 15)), rd=[act, W2, pY], wr=[pY])
                    k.ins("vector", lambda e, j=j, nb=nb, e_=e_, pY=pY, ysb=ysb: e.tensor_scalar_mul(
                        ysb[:, 512 * nb:512 * (nb + 1)], pY[:, :512], onehot[:, j, e_:e_ + 1]),
                        rd=[pY], srd=[onehot], wr=[ysb])
                k.ins("sync", lambda e, j=j, e_=e_, ysb=ysb: e.dma_start(
                    dram["y"][e_].rearrange("(j q) c -> q j c", q=128)[:, j, :], ysb[:, :]),
                    rd=[ysb], wr=[dram["y"][e_]], dma=True)
        if e_ < 7:
            load_W2(e_ + 1)
    p.rel(W2, u1, yscr[0], yscr[1], h2_fm, onehot, W1a, W1b)

    b2sb = load_bf(p, aps["L0_b2"], [8, 1024], "b2")
    ytiles = [p.get(f"c0y{i}", [128, D], F32) for i in range(4)]
    accT = p.get("c0acc", [128, D], F32)
    sT = p.get("c0s", [128, D], F32)
    gT = p.get("c0g", [128, D], F32)
    xnew_t = [p.get("c0xn", [128, D], F32) for _ in range(2)]
    x_dst = dram["x2"]
    xdv = x_dst.rearrange("(j q) c -> q j c", q=128)
    for j in range(NJ):
        k.ins("sync", lambda e, j=j: e.dma_start(
            accT[:, :], dram["acc"].rearrange("(j q) c -> q j c", q=128)[:, j, :]),
            rd=[dram["acc"]], wr=[accT], dma=True)
        for e_ in range(8):
            yt = ytiles[e_ % 4]
            k.ins("sync", lambda e, j=j, e_=e_, yt=yt: e.dma_start(
                yt[:, :], dram["y"][e_].rearrange("(j q) c -> q j c", q=128)[:, j, :]),
                rd=[dram["y"][e_]], wr=[yt], dma=True)
            k.ins("vector", lambda e, yt=yt: e.tensor_add(accT[:, :], accT[:, :], yt[:, :]),
                  rd=[yt, accT], wr=[accT])
        for nb in range(2):
            k.ins("tensor", lambda e, j=j, nb=nb: e.matmul(
                ps[0][:, :512], ohT[:, 128 * j:128 * (j + 1)], b2sb[:, 512 * nb:512 * (nb + 1)],
                start=True, stop=True), rd=[ohT, b2sb, ps[0]], wr=[ps[0]])
            k.ins("vector", lambda e, nb=nb: e.tensor_add(
                accT[:, 512 * nb:512 * (nb + 1)], accT[:, 512 * nb:512 * (nb + 1)], ps[0][:, :512]),
                rd=[ps[0], accT], wr=[accT])
        k.ins("sync", lambda e, j=j: e.dma_start(
            sT[:, :], dram["s"].rearrange("(j q) c -> q j c", q=128)[:, j, :]),
            rd=[dram["s"]], wr=[sT], dma=True)
        xnv = xnew_t[j % 2]
        k.ins("sync", lambda e, j=j, xnv=xnv: e.dma_start(
            xnv[:, :], x_src.rearrange("(j q) c -> q j c", q=128)[:, j, :]),
            rd=[x_src], wr=[xnv], dma=True)
        k.ins("scalar", lambda e: e.activation(gT[:, :], accT[:, :], AF.Silu), rd=[accT], wr=[gT])
        k.ins("vector", lambda e: e.tensor_mul(gT[:, :], gT[:, :], sT[:, :]), rd=[sT, gT], wr=[gT])
        k.ins("vector", lambda e, xnv=xnv: e.tensor_add(xnv[:, :], xnv[:, :], gT[:, :]),
              rd=[gT, xnv], wr=[xnv])
        k.ins("sync", lambda e, j=j, xnv=xnv: e.dma_start(xdv[:, j, :], xnv[:, :]),
              rd=[xnv], wr=[x_dst], dma=True)
    p.rel(b2sb, ytiles[0], ytiles[1], ytiles[2], ytiles[3], accT, sT, gT, ohT,
          xnew_t[0], xnew_t[1], ps[0], ps[1], ps[2], ps[3], pden)
    state["x_src"] = x_dst


def glu_moe_expert(p, aps, consts, xn_fm, Wa, Wb, W2, scale_cb, dst_dram, ps, label, prefetch_cb=None):
    k = p.k
    sbA = p.get(label + "A", [128, 512], F32)
    act = p.get(label + "act", [128, 16, 512], BF16)
    ysb = [p.get(label + "y0", [128, D], F32), p.get(label + "y1", [128, D], F32)]
    dv = dst_dram.rearrange("(j q) c -> q j c", q=128)
    for tb in range(4):
        t0 = 512 * tb
        for fi in range(16):
            pA, pB = ps[fi % 2], ps[2 + fi % 2]
            for ck in range(8):
                k.ins("tensor", lambda e, fi=fi, ck=ck, t0=t0, pA=pA: e.matmul(
                    pA[:, :512], Wa[:, ck, 128 * fi:128 * (fi + 1)], xn_fm[:, ck, t0:t0 + 512],
                    start=(ck == 0), stop=(ck == 7)), rd=[Wa, xn_fm, pA], wr=[pA])
            k.ins("scalar", lambda e, pA=pA: e.activation(sbA[:, :], pA[:, :512], AF.Silu),
                  rd=[pA], wr=[sbA])
            for ck in range(8):
                k.ins("tensor", lambda e, fi=fi, ck=ck, t0=t0, pB=pB: e.matmul(
                    pB[:, :512], Wb[:, ck, 128 * fi:128 * (fi + 1)], xn_fm[:, ck, t0:t0 + 512],
                    start=(ck == 0), stop=(ck == 7)), rd=[Wb, xn_fm, pB], wr=[pB])
            k.ins("vector", lambda e, fi=fi, pB=pB: e.tensor_mul(act[:, fi, :], sbA[:, :], pB[:, :512]),
                  rd=[sbA, pB], wr=[act])
        if tb == 3 and prefetch_cb is not None:
            prefetch_cb()
        for j2 in range(4):
            j = 4 * tb + j2
            yv = ysb[j2 % 2]
            for nb in range(2):
                pY = ps[nb]
                for fi in range(16):
                    k.ins("tensor", lambda e, fi=fi, j2=j2, nb=nb, pY=pY: e.matmul(
                        pY[:, :512], act[:, fi, 128 * j2:128 * (j2 + 1)],
                        W2[:, fi, 512 * nb:512 * (nb + 1)],
                        start=(fi == 0), stop=(fi == 15)), rd=[act, W2, pY], wr=[pY])
                sc = scale_cb(j) if scale_cb is not None else None
                if sc is None:
                    k.ins("vector", lambda e, nb=nb, pY=pY, yv=yv: e.tensor_scalar_mul(
                        yv[:, 512 * nb:512 * (nb + 1)], pY[:, :512], 1.0 / 3.0),
                        rd=[pY], wr=[yv])
                else:
                    k.ins("vector", lambda e, nb=nb, pY=pY, yv=yv, sc=sc: e.tensor_scalar_mul(
                        yv[:, 512 * nb:512 * (nb + 1)], pY[:, :512], sc),
                        rd=[pY], srd=[sc], wr=[yv])
            k.ins("sync", lambda e, j=j, yv=yv: e.dma_start(dv[:, j, :], yv[:, :]),
                  rd=[yv], wr=[dst_dram], dma=True)
    p.rel(sbA, act, ysb[0], ysb[1])


def moe_combine(p, consts, state, dram, srcs, x_src, out_dram, label, base_src=None):
    k = p.k
    accT = p.get(label + "ac", [128, D], F32)
    yt = [p.get(label + "yt" + str(i), [128, D], F32) for i in range(2)]
    sT = p.get(label + "s", [128, D], F32)
    gT = p.get(label + "g", [128, D], F32)
    xT = [p.get(label + "x", [128, D], F32) for _ in range(2)]
    ov = out_dram.rearrange("(j q) c -> q j c", q=128)
    for j in range(NJ):
        first = srcs[0] if base_src is None else base_src
        k.ins("sync", lambda e, j=j, first=first: e.dma_start(
            accT[:, :], first.rearrange("(j q) c -> q j c", q=128)[:, j, :]),
            rd=[first], wr=[accT], dma=True)
        rest = srcs if base_src is not None else srcs[1:]
        for i, sd in enumerate(rest):
            yv = yt[i % 2]
            k.ins("sync", lambda e, j=j, sd=sd, yv=yv: e.dma_start(
                yv[:, :], sd.rearrange("(j q) c -> q j c", q=128)[:, j, :]),
                rd=[sd], wr=[yv], dma=True)
            k.ins("vector", lambda e, yv=yv: e.tensor_add(accT[:, :], accT[:, :], yv[:, :]),
                  rd=[yv, accT], wr=[accT])
        k.ins("sync", lambda e, j=j: e.dma_start(
            sT[:, :], dram["s"].rearrange("(j q) c -> q j c", q=128)[:, j, :]),
            rd=[dram["s"]], wr=[sT], dma=True)
        xv = xT[j % 2]
        k.ins("sync", lambda e, j=j, xv=xv: e.dma_start(
            xv[:, :], x_src.rearrange("(j q) c -> q j c", q=128)[:, j, :]),
            rd=[x_src], wr=[xv], dma=True)
        k.ins("scalar", lambda e: e.activation(gT[:, :], accT[:, :], AF.Silu), rd=[accT], wr=[gT])
        k.ins("vector", lambda e: e.tensor_mul(gT[:, :], gT[:, :], sT[:, :]), rd=[sT, gT], wr=[gT])
        k.ins("vector", lambda e, xv=xv: e.tensor_add(xv[:, :], xv[:, :], gT[:, :]),
              rd=[gT, xv], wr=[xv])
        k.ins("sync", lambda e, j=j, xv=xv: e.dma_start(ov[:, j, :], xv[:, :]),
              rd=[xv], wr=[out_dram], dma=True)
    p.rel(accT, yt[0], yt[1], sT, gT, xT[0], xT[1])
    state["x_src"] = out_dram


def emit_L1(p, aps, consts, state, dram):
    k = p.k
    x_src = state["x_src"]
    ps = [p.gps("l1ps", [128, 512], F32) for _ in range(4)]
    xn_fm = p.get("l1xnf", [128, NI, T], BF16)
    f_all = p.get("l1f", [128, NJ], F32)
    mv2 = p.get("l1mv2", [128, 1, 2], F32)
    sc2 = p.get("l1sc2", [128, 12], F32)

    def fcb(j, nv):
        k.ins("vector", lambda e, nv=nv: e.bn_stats(sc2[:, 0:6], nv[:, 0:512]), rd=[nv], wr=[sc2])
        k.ins("vector", lambda e, nv=nv: e.bn_stats(sc2[:, 6:12], nv[:, 512:1024]), rd=[nv], wr=[sc2])
        k.ins("vector", lambda e: e.bn_aggr(mv2[:, 0, :], sc2[:, 0:12]), rd=[sc2], wr=[mv2])
        k.ins("vector", lambda e, j=j: e.tensor_mul(f_all[:, j:j + 1], mv2[:, 0, 0:1], mv2[:, 0, 0:1]),
              rd=[mv2], wr=[f_all])
        k.ins("vector", lambda e, j=j: e.tensor_add(f_all[:, j:j + 1], f_all[:, j:j + 1], mv2[:, 0, 1:2]),
              rd=[mv2, f_all], wr=[f_all])
        k.ins("scalar", lambda e, j=j: e.activation(f_all[:, j:j + 1], f_all[:, j:j + 1], AF.Sqrt),
              rd=[f_all], wr=[f_all])
        k.ins("vector", lambda e, j=j: e.tensor_scalar_max(f_all[:, j:j + 1], f_all[:, j:j + 1], 3.125e-14),
              rd=[f_all], wr=[f_all])
        k.ins("vector", lambda e, j=j: e.reciprocal(f_all[:, j:j + 1], f_all[:, j:j + 1]),
              rd=[f_all], wr=[f_all])

    stream_ln_to_fm(p, x_src, xn_fm, consts, xn_spill=None, eps=1e-5, label="l1ln", post_cb=fcb)
    p.rel(mv2, sc2)

    gam_row = load_f32(p, aps["L1_gamma"].rearrange("(a c) -> a c", a=1), [1, D], "gamr")
    gam = p.get("l1gam", [128, 8], F32)
    row_to_cols(p, gam, gam_row, D, consts, ps[3], "gamc")
    Wq = load_w_chunks(p, aps["L1_w_q"], D, 128, "l1wq")
    Wkv = load_w_chunks(p, aps["L1_w_kv"], D, 256, "l1wkv")
    for ck in range(8):
        k.ins("vector", lambda e, ck=ck: e.tensor_scalar_mul(Wq[:, ck, :], Wq[:, ck, :], gam[:, ck:ck + 1]),
              rd=[Wq], srd=[gam], wr=[Wq])
        k.ins("vector", lambda e, ck=ck: e.tensor_scalar_mul(Wkv[:, ck, :], Wkv[:, ck, :], gam[:, ck:ck + 1]),
              rd=[Wkv], srd=[gam], wr=[Wkv])
    p.rel(gam_row, gam)
    q_tm = p.get("l1q", [128, NJ, 128], F32)
    kv_tm = p.get("l1kv", [128, NJ, 256], F32)
    for j in range(NJ):
        for ck in range(8):
            k.ins("tensor", lambda e, j=j, ck=ck: e.matmul(
                ps[0][:, :128], xn_fm[:, ck, 128 * j:128 * (j + 1)], Wq[:, ck, :],
                start=(ck == 0), stop=(ck == 7)), rd=[xn_fm, Wq, ps[0]], wr=[ps[0]])
        k.ins("vector", lambda e, j=j: e.tensor_scalar(
            q_tm[:, j, :], ps[0][:, :128], f_all[:, j:j + 1], 0.25, ALU.mult, ALU.mult),
            rd=[ps[0]], srd=[f_all], wr=[q_tm])
        for ck in range(8):
            k.ins("tensor", lambda e, j=j, ck=ck: e.matmul(
                ps[1][:, :256], xn_fm[:, ck, 128 * j:128 * (j + 1)], Wkv[:, ck, :],
                start=(ck == 0), stop=(ck == 7)), rd=[xn_fm, Wkv, ps[1]], wr=[ps[1]])
        k.ins("vector", lambda e, j=j: e.tensor_scalar_mul(
            kv_tm[:, j, :], ps[1][:, :256], f_all[:, j:j + 1]),
            rd=[ps[1]], srd=[f_all], wr=[kv_tm])
    p.rel(Wq, Wkv, f_all, ps[0], ps[1], ps[2], ps[3])

    SQH = 0.5 ** 0.5
    l1pack = p.get("l1pack", [128, 6 * 2048], BF16)
    kexp = [l1pack[:, 2048 * m:2048 * (m + 1)].rearrange("q (j c) -> q j c", j=NJ) for m in range(3)]
    qexp = [l1pack[:, 2048 * (3 + m):2048 * (4 + m)].rearrange("q (j c) -> q j c", j=NJ) for m in range(3)]
    qexp_fm = [p.get("l1qef", [128, 2048], BF16) for m in range(3)]
    vaug = p.get("l1va", [128, NJ, 32], BF16)
    kvs = p.get("l1kvs", [128, 3, 32], BF16)
    pskv = [p.gps("l1pskv", [128, 32], F32) for _ in range(3)]
    psO = p.gps("l1psO", [128, 512], F32)
    psD = p.gps("l1psD", [128, 512], F32)
    psT = consts["ptr"][0]
    denr = p.get("l1den", [1, 512], F32)
    dinvc = p.get("l1dc", [128, 4], F32)
    oT = p.get("l1oT", [128, 512], BF16)
    out_tm = p.get("l1out", [128, NJ, 128], BF16)
    w_out = load_w_chunks(p, aps["L1_w_out"], 128, 1024, "l1wo")
    s_tiles = p.get("l1st", [128, D], F32)
    for h in range(8):
        hc = 16 * h
        for j in range(NJ):
            if h == 0:
                k.ins("vector", lambda e, j=j: e.memset(kexp[0][:, j, 0:1], 1.0), rd=[], wr=[kexp[0]])
                k.ins("vector", lambda e, j=j: e.memset(kexp[0][:, j, 17:128], 0.0), rd=[], wr=[kexp[0]])
                k.ins("vector", lambda e, j=j: e.memset(qexp[0][:, j, 0:1], 1.0), rd=[], wr=[qexp[0]])
                k.ins("vector", lambda e, j=j: e.memset(qexp[0][:, j, 17:128], 0.0), rd=[], wr=[qexp[0]])
                k.ins("vector", lambda e, j=j: e.memset(vaug[:, j, 16:17], 1.0), rd=[], wr=[vaug])
            k.ins("scalar", lambda e, j=j, hc=hc: e.copy(kexp[0][:, j, 1:17], kv_tm[:, j, hc:hc + 16]),
                  rd=[kv_tm], wr=[kexp[0]])
            k.ins("scalar", lambda e, j=j, hc=hc: e.copy(qexp[0][:, j, 1:17], q_tm[:, j, hc:hc + 16]),
                  rd=[q_tm], wr=[qexp[0]])
            k.ins("scalar", lambda e, j=j, hc=hc: e.copy(vaug[:, j, 0:16], kv_tm[:, j, 128 + hc:128 + hc + 16]),
                  rd=[kv_tm], wr=[vaug])
            for i in range(16):
                m, o = divmod(16 * i, 128)
                k.ins("vector", lambda e, j=j, i=i, m=m, o=o, hc=hc: e.tensor_scalar(
                    kexp[1 + m][:, j, o:o + 16], kv_tm[:, j, hc:hc + 16],
                    kv_tm[:, j, hc + i:hc + i + 1], SQH, ALU.mult, ALU.mult),
                    rd=[kv_tm], srd=[kv_tm], wr=[kexp[1 + m]])
                k.ins("vector", lambda e, j=j, i=i, m=m, o=o, hc=hc: e.tensor_scalar(
                    qexp[1 + m][:, j, o:o + 16], q_tm[:, j, hc:hc + 16],
                    q_tm[:, j, hc + i:hc + i + 1], SQH, ALU.mult, ALU.mult),
                    rd=[q_tm], srd=[q_tm], wr=[qexp[1 + m]])
        for m in range(3):
            for j in range(NJ):
                k.ins("tensor", lambda e, m=m, j=j: e.matmul(
                    pskv[m][:, :17], kexp[m][:, j, :], vaug[:, j, 0:17],
                    start=(j == 0), stop=(j == NJ - 1)), rd=[kexp[m], vaug, pskv[m]], wr=[pskv[m]])
            k.ins(k.evac_eng(), lambda e, m=m: ecopy(e, kvs[:, m, 0:17], pskv[m][:, :17]),
                  rd=[pskv[m]], wr=[kvs])
        pairs = []
        for j in range(NJ):
            for m in range(3):
                pairs.append((qexp[m][:, j, :], qexp_fm[m][:, 128 * j:128 * (j + 1)],
                              qexp[m], qexp_fm[m]))
        transpose_pairs(p, pairs, consts, "l1qT")
        for tb in range(4):
            for m in range(3):
                k.ins("tensor", lambda e, m=m, tb=tb: e.matmul(
                    psO[:16, :512], kvs[:, m, 0:16], qexp_fm[m][:, 512 * tb:512 * (tb + 1)],
                    start=(m == 0), stop=(m == 2)), rd=[kvs, qexp_fm[m], psO], wr=[psO])
            for m in range(3):
                k.ins("tensor", lambda e, m=m, tb=tb: e.matmul(
                    psD[0:1, :512], kvs[:, m, 16:17], qexp_fm[m][:, 512 * tb:512 * (tb + 1)],
                    start=(m == 0), stop=(m == 2)), rd=[kvs, qexp_fm[m], psD], wr=[psD])
            k.ins("vector", lambda e: e.tensor_scalar_max(denr[:, :], psD[0:1, :512], 1e-5),
                  rd=[psD], wr=[denr])
            k.ins("vector", lambda e: e.reciprocal(denr[:, :], denr[:, :]), rd=[denr], wr=[denr])
            k.ins(k.evac_eng(), lambda e: ecopy(e, oT[:16, :], psO[:16, :512]), rd=[psO], wr=[oT])
            for sub in range(4):
                j = 4 * tb + sub
                k.ins("tensor", lambda e, sub=sub: e.transpose(
                    psT[:128, 0:16], oT[:16, 128 * sub:128 * (sub + 1)], consts["ident"][:16, :16]),
                    rd=[oT, consts["ident"], psT], wr=[psT])
                k.ins("tensor", lambda e, sub=sub: e.matmul(
                    psD[:, 1:2], denr[0:1, 128 * sub:128 * (sub + 1)],
                    consts["ones_row32"][0:1, 0:1], start=True, stop=True),
                    rd=[denr, consts["ones_row32"], psD], wr=[psD])
                k.ins("vector", lambda e, sub=sub: e.tensor_copy(dinvc[:, sub:sub + 1], psD[:, 1:2]),
                      rd=[psD], wr=[dinvc])
                k.ins("vector", lambda e, j=j, sub=sub, h=h: e.tensor_scalar_mul(
                    out_tm[:, j, 16 * h:16 * (h + 1)], psT[:, 0:16], dinvc[:, sub:sub + 1]),
                    rd=[psT], srd=[dinvc], wr=[out_tm])
    p.rel(q_tm, kv_tm, l1pack, qexp_fm[0], qexp_fm[1], qexp_fm[2], vaug, kvs, oT, denr, dinvc,
          pskv[0], pskv[1], pskv[2], psO, psD)
    ps = [p.gps("l1ps2", [128, 512], F32) for _ in range(4)]
    out_fm = p.get("l1of", [128, T], BF16)
    transpose_pairs(p, [(out_tm[:, j, :], out_fm[:, 128 * j:128 * (j + 1)], out_tm, out_fm)
                        for j in range(NJ)], consts, "l1oT2")
    sdv = dram["s"].rearrange("(j q) c -> q j c", q=128)
    for j in range(NJ):
        for nb in range(2):
            k.ins("tensor", lambda e, j=j, nb=nb: e.matmul(
                ps[nb][:, :512], out_fm[:, 128 * j:128 * (j + 1)], w_out[:, 0, 512 * nb:512 * (nb + 1)],
                start=True, stop=True), rd=[out_fm, w_out, ps[nb]], wr=[ps[nb]])
            k.ins("vector", lambda e, j=j, nb=nb: e.tensor_copy(
                s_tiles[:, 512 * nb:512 * (nb + 1)], ps[nb][:, :512]), rd=[ps[nb]], wr=[s_tiles])
        k.ins("sync", lambda e, j=j: e.dma_start(sdv[:, j, :], s_tiles[:, :]),
              rd=[s_tiles], wr=[dram["s"]], dma=True)
    p.rel(out_tm, out_fm, w_out, s_tiles)

    Wg = load_bf(p, aps["L1_Wg"], [128, 8, 8], "l1Wg", ("(kc kp) n -> kp kc n", dict(kp=128)))
    gates = p.get("l1gt", [128, NJ, 8], F32)
    w3 = p.get("l1w3", [128, NJ, 8], F32)
    t8 = p.get("l1t8", [128, 8], F32)
    m1 = p.get("l1m1", [128, 1], F32)
    m2 = p.get("l1m2", [128, 1], F32)
    for j in range(NJ):
        for ck in range(8):
            k.ins("tensor", lambda e, j=j, ck=ck: e.matmul(
                ps[0][:, :8], xn_fm[:, ck, 128 * j:128 * (j + 1)], Wg[:, ck, :],
                start=(ck == 0), stop=(ck == 7)), rd=[xn_fm, Wg, ps[0]], wr=[ps[0]])
        k.ins("scalar", lambda e, j=j: e.activation(gates[:, j, :], ps[0][:, :8], AF.Sigmoid),
              rd=[ps[0]], wr=[gates])
        k.ins("vector", lambda e, j=j: e.reduce_max(m1[:, :], gates[:, j, :], AX.X),
              rd=[gates], wr=[m1])
        k.ins("vector", lambda e, j=j: e.tensor_scalar(
            t8[:, :], gates[:, j, :], m1[:, 0:1], None, ALU.is_equal), rd=[gates], srd=[m1], wr=[t8])
        k.ins("vector", lambda e, j=j: e.scalar_tensor_tensor(
            w3[:, j, :], t8[:, :], -1e9, gates[:, j, :], ALU.mult, ALU.add),
            rd=[t8, gates], wr=[w3])
        k.ins("vector", lambda e, j=j: e.reduce_max(m2[:, :], w3[:, j, :], AX.X), rd=[w3], wr=[m2])
        k.ins("vector", lambda e, j=j: e.tensor_scalar(
            w3[:, j, :], w3[:, j, :], m2[:, 0:1], None, ALU.is_equal), rd=[w3], srd=[m2], wr=[w3])
        k.ins("vector", lambda e, j=j: e.tensor_add(w3[:, j, :], w3[:, j, :], t8[:, :]),
              rd=[t8, w3], wr=[w3])
        k.ins("vector", lambda e, j=j: e.tensor_add(m1[:, :], m1[:, :], m2[:, :]), rd=[m2, m1], wr=[m1])
        k.ins("vector", lambda e, j=j: e.tensor_scalar_mul(m1[:, :], m1[:, :], 3.0), rd=[m1], wr=[m1])
        k.ins("vector", lambda e, j=j: e.reciprocal(m1[:, :], m1[:, :]), rd=[m1], wr=[m1])
        k.ins("vector", lambda e, j=j: e.tensor_mul(w3[:, j, :], w3[:, j, :], gates[:, j, :]),
              rd=[gates, w3], wr=[w3])
        k.ins("vector", lambda e, j=j: e.tensor_scalar_mul(w3[:, j, :], w3[:, j, :], m1[:, 0:1]),
              rd=[w3], srd=[m1], wr=[w3])
    p.rel(Wg, gates, t8, m1, m2)

    Wa = p.get("l1Wa", [128, 8, 2048], BF16)
    Wb = p.get("l1Wb", [128, 8, 2048], BF16)
    W2 = p.get("l1W2", [128, 16, 1024], BF16)
    load_w_chunks(p, aps["L1_Ws1"], D, 2048, "l1s1", into=Wa)
    load_w_chunks(p, aps["L1_Ws3"], D, 2048, "l1s3", into=Wb)
    load_w_chunks(p, aps["L1_Ws2"], 2048, 1024, "l1s2", into=W2)
    def mk_prefetch(e_next):
        def cb():
            if e_next < 8:
                load_w_chunks(p, aps["L1_We1"][e_next], D, 2048, f"l1e1{e_next}", into=Wa)
                load_w_chunks(p, aps["L1_We3"][e_next], D, 2048, f"l1e3{e_next}", into=Wb)
                load_w_chunks(p, aps["L1_We2"][e_next], 2048, 1024, f"l1e2{e_next}", into=W2)
        return cb

    glu_moe_expert(p, aps, consts, xn_fm, Wa, Wb, W2, None, dram["acc"], ps, "l1sh",
                   prefetch_cb=mk_prefetch(0))
    for e_ in range(8):
        glu_moe_expert(p, aps, consts, xn_fm, Wa, Wb, W2,
                       (lambda j, e_=e_: w3[:, j, e_:e_ + 1]), dram["y"][e_], ps, f"l1x{e_}",
                       prefetch_cb=mk_prefetch(e_ + 1))
    p.rel(Wa, Wb, W2, w3, xn_fm)
    moe_combine(p, consts, state, dram, [dram["acc"]] + dram["y"], x_src, dram["x"], "l1cb")
    p.rel(ps[0], ps[1], ps[2], ps[3])


def emit_L2(p, aps, consts, state, dram):
    k = p.k
    x_src = state["x_src"]
    ps = [p.gps("l2ps", [128, 512], F32) for _ in range(2)]
    xn_fm = p.get("l2xnf", [128, NI, T], BF16)
    stream_ln_to_fm(p, x_src, xn_fm, consts, xn_spill=None, eps=1e-5, label="l2ln")

    wqA = p.get("l2wA", [128, 4, 3072], BF16)
    wqB = p.get("l2wB", [128, 4, 3072], BF16)
    _s = aps["L2_wqkv"].rearrange("(kc kp) n -> kc kp n", kp=128)
    for kc in range(8):
        _t, _sub = (wqA, kc) if kc < 4 else (wqB, kc - 4)
        p.k.ins("gpsimd", lambda e, kc=kc, _t=_t, _sub=_sub: e.dma_start(_t[:, _sub, :], _s[kc]),
                rd=[_s], wr=[_t], dma=True, label=f"l2w{kc}")
    qkvt = [p.get("l2qkv", [128, 3072], F32) for _ in range(2)]
    psC = [p.gps("l2pC", [1, 512], F32) for _ in range(2)]
    psS = p.gps("l2pS", [128, 64], F32)
    carry = p.get("l2cr", [1, 1024], F32)
    v_bf = p.get("l2vb", [128, 1024], BF16)
    sii = p.get("l2sii", [128, 16], F32)
    mm_ = p.get("l2m", [128, 16], F32)
    esb = p.get("l2es", [128, 16], F32)
    emb = p.get("l2em", [128, 16], F32)
    qks = p.get("l2qk", [128, 64], F32)
    esv = p.get("l2esv", [128, 64], F32)
    den = p.get("l2den", [128, 1], F32)
    s_tile = p.get("l2st", [128, D], F32)
    sdv = dram["s"].rearrange("(j q) c -> q j c", q=128)
    k.ins("vector", lambda e: e.memset(carry[:, :], 0.0), rd=[], wr=[carry])
    for cc in range(2):
        k.ins("tensor", lambda e, cc=cc: e.matmul(
            psC[cc][0:1, :512], consts["ones_row_bf"][0:1, 0:1], consts["zeros_row_bf"][:, 0:512],
            start=True, stop=False),
            rd=[consts["ones_row_bf"], consts["zeros_row_bf"], psC[cc]], wr=[psC[cc]])
    for j in range(NJ):
        qv = qkvt[j % 2]
        for nb in range(6):
            k.ins("vector", lambda e: e.engine_nop(), rd=[], wr=[]) if False else None
            for ck in range(8):
                _t, _sub = (wqA, ck) if ck < 4 else (wqB, ck - 4)
                k.ins("tensor", lambda e, j=j, nb=nb, ck=ck, _t=_t, _sub=_sub: e.matmul(
                    ps[nb % 2][:, :512], xn_fm[:, ck, 128 * j:128 * (j + 1)],
                    _t[:, _sub, 512 * nb:512 * (nb + 1)],
                    start=(ck == 0), stop=(ck == 7)), rd=[xn_fm, _t, ps[nb % 2]], wr=[ps[nb % 2]])
            k.ins(k.evac_eng(), lambda e, nb=nb, qv=qv: ecopy(
                e, qv[:, 512 * nb:512 * (nb + 1)], ps[nb % 2][:, :512]), rd=[ps[nb % 2]], wr=[qv])
        for h in range(16):
            b = 192 * h
            k.ins("vector", lambda e, h=h, qv=qv, b=b: e.tensor_mul(
                qks[:, :], qv[:, b:b + 64], qv[:, b + 64:b + 128]), rd=[qv], wr=[qks])
            k.ins("vector", lambda e, h=h: e.reduce_sum(sii[:, h:h + 1], qks[:, :], AX.X),
                  rd=[qks], wr=[sii])
            k.ins("scalar", lambda e, h=h, qv=qv, b=b: e.copy(v_bf[:, 64 * h:64 * (h + 1)], qv[:, b + 128:b + 192]),
                  rd=[qv], wr=[v_bf])
        k.ins("vector", lambda e: e.tensor_scalar_mul(sii[:, :], sii[:, :], 0.125), rd=[sii], wr=[sii])
        k.ins("vector", lambda e: e.tensor_scalar_max(mm_[:, :], sii[:, :], 0.0), rd=[sii], wr=[mm_])
        k.ins("vector", lambda e: e.tensor_sub(esb[:, :], sii[:, :], mm_[:, :]), rd=[sii, mm_], wr=[esb])
        k.ins("scalar", lambda e: e.activation(esb[:, :], esb[:, :], AF.Exp), rd=[esb], wr=[esb])
        k.ins("scalar", lambda e: e.activation(emb[:, :], mm_[:, :], AF.Exp, scale=-1.0), rd=[mm_], wr=[emb])
        for h in range(16):
            k.ins("tensor", lambda e, h=h: e.matmul(
                psS[:, :64], consts["upstrict"][:, :], v_bf[:, 64 * h:64 * (h + 1)],
                start=True, stop=False), rd=[consts["upstrict"], v_bf, psS], wr=[psS])
            k.ins("tensor", lambda e, h=h: e.matmul(
                psS[:, :64], consts["ones_row32"][:, :], carry[0:1, 64 * h:64 * (h + 1)],
                start=False, stop=True), rd=[consts["ones_row32"], carry, psS], wr=[psS])
            k.ins("vector", lambda e, h=h: e.tensor_scalar_mul(
                esv[:, :], v_bf[:, 64 * h:64 * (h + 1)], esb[:, h:h + 1]),
                rd=[v_bf], srd=[esb], wr=[esv])
            k.ins("vector", lambda e, h=h: e.scalar_tensor_tensor(
                esv[:, :], psS[:, :64], emb[:, h:h + 1], esv[:, :], ALU.mult, ALU.add),
                rd=[psS, esv], srd=[emb], wr=[esv])
            k.ins("vector", lambda e, h=h, j=j: e.scalar_tensor_tensor(
                den[:, :], consts["pos"][:, j:j + 1], emb[:, h:h + 1], esb[:, h:h + 1], ALU.mult, ALU.add),
                rd=[consts["pos"], esb], srd=[emb], wr=[den])
            k.ins("vector", lambda e: e.reciprocal(den[:, :], den[:, :]), rd=[den], wr=[den])
            k.ins("vector", lambda e, h=h: e.tensor_scalar_mul(
                s_tile[:, 64 * h:64 * (h + 1)], esv[:, :], den[:, 0:1]),
                rd=[esv], srd=[den], wr=[s_tile])
            cc = h // 8
            k.ins("tensor", lambda e, h=h, cc=cc: e.matmul(
                psC[cc][0:1, 64 * (h % 8):64 * (h % 8) + 64], consts["ones_col_bf"][:, :],
                v_bf[:, 64 * h:64 * (h + 1)], start=False, stop=(j == NJ - 1 and h % 8 == 7)),
                rd=[consts["ones_col_bf"], v_bf, psC[cc]], wr=[psC[cc]])
        for cc in range(2):
            k.ins("scalar", lambda e, cc=cc: e.copy(carry[0:1, 512 * cc:512 * (cc + 1)], psC[cc][0:1, :512]),
                  rd=[psC[cc]], wr=[carry])
        k.ins("sync", lambda e, j=j: e.dma_start(sdv[:, j, :], s_tile[:, :]),
              rd=[s_tile], wr=[dram["s"]], dma=True)
    p.rel(wqA, wqB, qkvt[0], qkvt[1], psC[0], psC[1], psS, carry, v_bf, sii, mm_, esb, emb, qks, esv, den, s_tile)

    rwT = load_w_chunks(p, aps["L2_router_wT"], D, 8, "l2r")
    rb = load_f32(p, aps["L2_router_b"].rearrange("(a c) -> a c", a=1), [1, 8], "l2rb")
    mask = p.get("l2mk", [128, NJ, 8], F32)
    t8 = p.get("l2t8", [128, 8], F32)
    m1 = p.get("l2m1", [128, 1], F32)
    for j in range(NJ):
        for ck in range(8):
            k.ins("tensor", lambda e, j=j, ck=ck: e.matmul(
                ps[0][:, :8], xn_fm[:, ck, 128 * j:128 * (j + 1)], rwT[:, ck, :],
                start=(ck == 0), stop=False), rd=[xn_fm, rwT, ps[0]], wr=[ps[0]])
        k.ins("tensor", lambda e, j=j: e.matmul(
            ps[0][:, :8], consts["ones_row32"][:, :], rb[0:1, :], start=False, stop=True),
            rd=[consts["ones_row32"], rb, ps[0]], wr=[ps[0]])
        k.ins("vector", lambda e, j=j: e.tensor_copy(t8[:, :], ps[0][:, :8]), rd=[ps[0]], wr=[t8])
        k.ins("vector", lambda e, j=j: e.reduce_max(m1[:, :], t8[:, :], AX.X), rd=[t8], wr=[m1])
        k.ins("vector", lambda e, j=j: e.tensor_scalar(
            mask[:, j, :], t8[:, :], m1[:, 0:1], None, ALU.is_equal), rd=[t8], srd=[m1], wr=[mask])
        k.ins("vector", lambda e, j=j: e.scalar_tensor_tensor(
            t8[:, :], mask[:, j, :], -1e9, t8[:, :], ALU.mult, ALU.add), rd=[mask, t8], wr=[t8])
        k.ins("vector", lambda e, j=j: e.reduce_max(m1[:, :], t8[:, :], AX.X), rd=[t8], wr=[m1])
        k.ins("vector", lambda e, j=j: e.tensor_scalar(
            t8[:, :], t8[:, :], m1[:, 0:1], None, ALU.is_equal), rd=[t8], srd=[m1], wr=[t8])
        k.ins("vector", lambda e, j=j: e.tensor_add(mask[:, j, :], mask[:, j, :], t8[:, :]),
              rd=[t8, mask], wr=[mask])
    p.rel(rwT, rb, t8, m1)
    ew2 = [p.get("l2ew", [128, 8, 1024], BF16) for _ in range(2)]
    eb2 = [p.get("l2eb", [1, 1024], F32) for _ in range(2)]
    ysb = [p.get("l2y", [128, D], F32) for _ in range(2)]

    def l2load(e_):
        load_w_chunks(p, aps["L2_expert_wT"][e_], D, 1024, f"l2e{e_}", into=ew2[e_ % 2])
        k.ins("sync", lambda e, e_=e_: e.dma_start(eb2[e_ % 2][:, :], aps["L2_expert_b"][e_:e_ + 1, :]),
              rd=[aps["L2_expert_b"]], wr=[eb2[e_ % 2]], dma=True)

    l2load(0)
    for e_ in range(8):
        ew, eb = ew2[e_ % 2], eb2[e_ % 2]
        if e_ < 7:
            l2load(e_ + 1)
        ydv = dram["y"][e_].rearrange("(j q) c -> q j c", q=128)
        for j in range(NJ):
            yv = ysb[j % 2]
            for nb in range(2):
                for ck in range(8):
                    k.ins("tensor", lambda e, j=j, nb=nb, ck=ck, ew=ew: e.matmul(
                        ps[nb][:, :512], xn_fm[:, ck, 128 * j:128 * (j + 1)],
                        ew[:, ck, 512 * nb:512 * (nb + 1)],
                        start=(ck == 0), stop=False), rd=[xn_fm, ew, ps[nb]], wr=[ps[nb]])
                k.ins("tensor", lambda e, nb=nb, eb=eb: e.matmul(
                    ps[nb][:, :512], consts["ones_row32"][:, :], eb[0:1, 512 * nb:512 * (nb + 1)],
                    start=False, stop=True), rd=[consts["ones_row32"], eb, ps[nb]], wr=[ps[nb]])
                k.ins("vector", lambda e, j=j, nb=nb, e_=e_, yv=yv: e.tensor_scalar_mul(
                    yv[:, 512 * nb:512 * (nb + 1)], ps[nb][:, :512], mask[:, j, e_:e_ + 1]),
                    rd=[ps[nb]], srd=[mask], wr=[yv])
            k.ins("sync", lambda e, j=j, yv=yv, ydv=ydv: e.dma_start(ydv[:, j, :], yv[:, :]),
                  rd=[yv], wr=[dram["y"][e_]], dma=True)
    p.rel(ew2[0], ew2[1], eb2[0], eb2[1], ysb[0], ysb[1], mask, xn_fm)
    moe_combine(p, consts, state, dram, dram["y"][1:], x_src, dram["x2"], "l2cb", base_src=dram["y"][0])
    p.rel(ps[0], ps[1])


def emit_L3(p, aps, consts, state, dram):
    k = p.k
    x_src = state["x_src"]
    ps = [p.gps("l3ps", [128, 512], F32) for _ in range(4)]
    xn_fm = p.get("l3xnf", [128, NI, T], BF16)
    stream_ln_to_fm(p, x_src, xn_fm, consts, xn_spill=dram["xn"], eps=1e-5, label="l3ln")

    gwT = load_w_chunks(p, aps["L3_gate_wT"], D, D, "l3gw")
    gb_row = load_f32(p, aps["L3_gate_b"].rearrange("(a c) -> a c", a=1), [1, D], "l3gb")
    gt_ = p.get("l3gt", [128, D], F32)
    gmean = p.get("l3gm", [128, 1], F32)
    eps5 = p.cval(1e-5)
    gnv = dram["acc"].rearrange("(j q) c -> q j c", q=128)
    for j in range(NJ):
        for nb in range(2):
            for ck in range(8):
                k.ins("tensor", lambda e, j=j, nb=nb, ck=ck: e.matmul(
                    ps[nb][:, :512], xn_fm[:, ck, 128 * j:128 * (j + 1)], gwT[:, ck, 512 * nb:512 * (nb + 1)],
                    start=(ck == 0), stop=False), rd=[xn_fm, gwT, ps[nb]], wr=[ps[nb]])
            k.ins("tensor", lambda e, nb=nb: e.matmul(
                ps[nb][:, :512], consts["ones_row32"][:, :], gb_row[:, 512 * nb:512 * (nb + 1)],
                start=False, stop=True), rd=[consts["ones_row32"], gb_row, ps[nb]], wr=[ps[nb]])
            k.ins("scalar", lambda e, nb=nb: e.activation(gt_[:, 512 * nb:512 * (nb + 1)], ps[nb][:, :512],
                                                          AF.Sigmoid), rd=[ps[nb]], wr=[gt_])
        k.ins("vector", lambda e: e.reduce_sum(gmean[:, :], gt_[:, :], AX.X), rd=[gt_], wr=[gmean])
        k.ins("vector", lambda e: e.tensor_scalar(gmean[:, :], gmean[:, :], 1.0 / D, 1e-5, ALU.mult, ALU.add),
              rd=[gmean], wr=[gmean])
        k.ins("vector", lambda e: e.reciprocal(gmean[:, :], gmean[:, :]), rd=[gmean], wr=[gmean])
        k.ins("vector", lambda e: e.tensor_scalar_mul(gt_[:, :], gt_[:, :], gmean[:, 0:1]),
              rd=[gt_], srd=[gmean], wr=[gt_])
        k.ins("sync", lambda e, j=j: e.dma_start(gnv[:, j, :], gt_[:, :]), rd=[gt_], wr=[dram["acc"]], dma=True)
    p.rel(gwT, gb_row, gt_, gmean, xn_fm)

    lngr = load_rep(p, aps["L3_ln_g"].rearrange("(a c) -> a c", a=1), D, consts, ps[3], "l3lg")
    lnbr = load_rep(p, aps["L3_ln_b"].rearrange("(a c) -> a c", a=1), D, consts, ps[3], "l3lb")
    xnorm_fm = p.get("l3xno", [128, NI, T], BF16)
    stream_ln_to_fm(p, dram["xn"], xnorm_fm, consts, xn_spill=None, eps=1e-5, label="l3ln2",
                    grep=lngr, brep=lnbr)
    p.rel(lngr, lnbr)

    qb_row = load_f32(p, aps["L3_qkv_b"].rearrange("(a c) -> a c", a=1), [1, 3072], "l3qb")
    q_fm = p.get("l3qf", [128, NI, T], BF16)
    k_tm = p.get("l3kt", [128, NJ, D], BF16)
    v_tm = p.get("l3vt", [128, NJ, D], BF16)
    wsbh = p.get("l3wq", [128, 8, 512], BF16)
    scr_a = p.get("l3sa", [128, 512], F32)
    scr_b = p.get("l3sb", [128, 512], F32)
    scr_c = p.get("l3sc", [128, 512], BF16)
    gnt = [p.get("l3gn", [128, 512], F32) for _ in range(2)]
    for part, off in (("q", 0), ("k", 1024), ("v", 2048)):
        for half in range(2):
            o2 = off + 512 * half
            load_w_chunks(p, aps["L3_qkv_wT"][:, o2:o2 + 512], D, 512, "l3w" + part, into=wsbh)
            for j in range(NJ):
                pp = ps[j % 2]
                for ck in range(8):
                    k.ins("tensor", lambda e, j=j, ck=ck, pp=pp: e.matmul(
                        pp[:, :512], xnorm_fm[:, ck, 128 * j:128 * (j + 1)], wsbh[:, ck, :],
                        start=(ck == 0), stop=False), rd=[xnorm_fm, wsbh, pp], wr=[pp])
                k.ins("tensor", lambda e, o2=o2, pp=pp: e.matmul(
                    pp[:, :512], consts["ones_row32"][:, :], qb_row[:, o2:o2 + 512],
                    start=False, stop=True), rd=[consts["ones_row32"], qb_row, pp], wr=[pp])
                if part == "v":
                    k.ins("scalar", lambda e, j=j, half=half, pp=pp: e.copy(
                        v_tm[:, j, 512 * half:512 * (half + 1)], pp[:, :512]), rd=[pp], wr=[v_tm])
                    continue
                gv = gnt[j % 2]
                k.ins("sync", lambda e, j=j, half=half, gv=gv: e.dma_start(
                    gv[:, :], gnv[:, j, 512 * half:512 * (half + 1)]),
                    rd=[dram["acc"]], wr=[gv], dma=True)
                k.ins("vector", lambda e, gv=gv, pp=pp: e.tensor_mul(scr_a[:, :], pp[:, :512], gv[:, :]),
                      rd=[pp, gv], wr=[scr_a])
                k.ins("vector", lambda e: e.tensor_scalar_min(scr_b[:, :], scr_a[:, :], 0.0),
                      rd=[scr_a], wr=[scr_b])
                k.ins("scalar", lambda e: e.activation(scr_b[:, :], scr_b[:, :], AF.Exp), rd=[scr_b], wr=[scr_b])
                k.ins("vector", lambda e: e.tensor_relu(scr_a[:, :], scr_a[:, :]), rd=[scr_a], wr=[scr_a])
                if part == "k":
                    k.ins("vector", lambda e, j=j, half=half: e.tensor_add(
                        k_tm[:, j, 512 * half:512 * (half + 1)], scr_a[:, :], scr_b[:, :]),
                        rd=[scr_a, scr_b], wr=[k_tm])
                else:
                    k.ins("vector", lambda e: e.tensor_add(scr_c[:, :], scr_a[:, :], scr_b[:, :]),
                          rd=[scr_a, scr_b], wr=[scr_c])
                    pairs = [(scr_c[:, 128 * i:128 * (i + 1)], q_fm[:, 4 * half + i, 128 * j:128 * (j + 1)],
                              scr_c, q_fm) for i in range(4)]
                    transpose_pairs(p, pairs, consts, "l3q")
    p.rel(wsbh, qb_row, scr_a, scr_b, scr_c, gnt[0], gnt[1], xnorm_fm, ps[0], ps[1], ps[2], ps[3])

    pkT = consts["ptr"][0]
    psS = p.gps("l3pS", [128, 512], F32)
    psD = p.gps("l3pD", [128, 512], F32)
    psN = p.gps("l3pN", [128, 512], F32)
    psK = p.gps("l3pK", [128, 512], F32)
    psT2 = consts["ptr"][1]
    kf_sb = p.get("l3kf", [128, 128], BF16)
    STb = p.get("l3STb", [128, 128], BF16)
    numb = p.get("l3num", [128, 128], BF16)
    KV_sb = p.get("l3KV", [128, 65], BF16)
    den_sb = p.get("l3dn", [128, 1], F32)
    out_tm = p.get("l3ot", [128, NJ, D], BF16)
    triu32 = p.get("l3triu32", [128, 128], F32)
    k.ins("sync", lambda e: e.dma_start(triu32[:, :], aps["c_triu01"][:, :]), rd=[aps["c_triu01"]],
          wr=[triu32], dma=True)
    for h in range(16):
        b = (h % 2) * 64
        ci = h // 2
        k.ins("vector", lambda e, b=b: e.memset(KV_sb[b:b + 64, :], 0.0), rd=[], wr=[KV_sb])
        k.ins("tensor", lambda e, b=b: e.matmul(
            psK[b:b + 64, 0:65], consts["ones_row_bf"][0:1, 0:64], consts["zeros_row_bf"][:, 0:65],
            start=True, stop=False), rd=[consts["ones_row_bf"], consts["zeros_row_bf"], psK], wr=[psK])
        for j in range(NJ):
            k.ins("tensor", lambda e, j=j, h=h: e.transpose(
                pkT[:64, :128], k_tm[:, j, 64 * h:64 * (h + 1)], consts["ident"][:, :]),
                rd=[k_tm, consts["ident"], pkT], wr=[pkT])
            k.ins(k.evac_eng(), lambda e, b=b: ecopy(e, kf_sb[b:b + 64, :], pkT[:64, :128]),
                  rd=[pkT], wr=[kf_sb])
            k.ins("tensor", lambda e, j=j, b=b, ci=ci: e.matmul(
                psS[:, :128], kf_sb[b:b + 64, :], q_fm[b:b + 64, ci, 128 * j:128 * (j + 1)],
                start=True, stop=True), rd=[kf_sb, q_fm, psS], wr=[psS])
            k.ins("vector", lambda e: e.tensor_mul(STb[:, :], psS[:, :128], triu32[:, :]),
                  rd=[psS, triu32], wr=[STb])
            k.ins("tensor", lambda e: e.matmul(psD[:, 0:1], STb[:, :], consts["ones_col_bf"][:, :],
                                               start=True, stop=False),
                  rd=[STb, consts["ones_col_bf"], psD], wr=[psD])
            k.ins("tensor", lambda e, j=j, b=b, ci=ci: e.matmul(
                psD[:, 0:1], q_fm[b:b + 64, ci, 128 * j:128 * (j + 1)], KV_sb[b:b + 64, 64:65],
                start=False, stop=True), rd=[q_fm, KV_sb, psD], wr=[psD])
            k.ins("tensor", lambda e, j=j, h=h: e.matmul(
                psN[:64, :128], v_tm[:, j, 64 * h:64 * (h + 1)], STb[:, :], start=True, stop=False),
                rd=[v_tm, STb, psN], wr=[psN])
            k.ins("tensor", lambda e, j=j, b=b, ci=ci: e.matmul(
                psN[:64, :128], KV_sb[b:b + 64, 0:64], q_fm[b:b + 64, ci, 128 * j:128 * (j + 1)],
                start=False, stop=True), rd=[KV_sb, q_fm, psN], wr=[psN])
            k.ins("tensor", lambda e, j=j, h=h, b=b: e.matmul(
                psK[b:b + 64, 0:64], k_tm[:, j, 64 * h:64 * (h + 1)], v_tm[:, j, 64 * h:64 * (h + 1)],
                start=False, stop=False), rd=[k_tm, v_tm, psK], wr=[psK])
            k.ins("tensor", lambda e, j=j, h=h, b=b: e.matmul(
                psK[b:b + 64, 64:65], k_tm[:, j, 64 * h:64 * (h + 1)], consts["ones_col_bf"][:, :],
                start=False, stop=(j == NJ - 1)), rd=[k_tm, consts["ones_col_bf"], psK], wr=[psK])
            k.ins("scalar", lambda e, b=b: e.copy(KV_sb[b:b + 64, :], psK[b:b + 64, 0:65]),
                  rd=[psK], wr=[KV_sb])
            k.ins("vector", lambda e: e.tensor_scalar(den_sb[:, :], psD[:, 0:1], 1e-5, None, ALU.add),
                  rd=[psD], wr=[den_sb])
            k.ins("vector", lambda e: e.reciprocal(den_sb[:, :], den_sb[:, :]), rd=[den_sb], wr=[den_sb])
            k.ins(k.evac_eng(), lambda e: ecopy(e, numb[:64, :], psN[:64, :128]), rd=[psN], wr=[numb])
            k.ins("tensor", lambda e: e.transpose(psT2[:128, :64], numb[:64, :], consts["ident"][:64, :64]),
                  rd=[numb, consts["ident"], psT2], wr=[psT2])
            k.ins("vector", lambda e, j=j, h=h: e.tensor_scalar_mul(
                out_tm[:, j, 64 * h:64 * (h + 1)], psT2[:, :64], den_sb[:, 0:1]),
                rd=[psT2], srd=[den_sb], wr=[out_tm])
    p.rel(psS, psD, psN, psK, kf_sb, STb, numb, KV_sb, den_sb, triu32, q_fm, k_tm, v_tm)

    ps = [p.gps("l3ps2", [128, 512], F32) for _ in range(2)]
    out_fm = p.get("l3ofm", [128, NI, T], BF16)
    tm_to_fm(p, out_fm, out_tm, consts, label="l3o")
    p.rel(out_tm)
    pwT = load_w_chunks(p, aps["L3_proj_wT"], D, D, "l3pw")
    pb_row = load_f32(p, aps["L3_proj_b"].rearrange("(a c) -> a c", a=1), [1, D], "l3pb")
    s_tile = p.get("l3st", [128, D], F32)
    sdv = dram["s"].rearrange("(j q) c -> q j c", q=128)
    for j in range(NJ):
        for nb in range(2):
            for ck in range(8):
                k.ins("tensor", lambda e, j=j, nb=nb, ck=ck: e.matmul(
                    ps[nb][:, :512], out_fm[:, ck, 128 * j:128 * (j + 1)], pwT[:, ck, 512 * nb:512 * (nb + 1)],
                    start=(ck == 0), stop=False), rd=[out_fm, pwT, ps[nb]], wr=[ps[nb]])
            k.ins("tensor", lambda e, nb=nb: e.matmul(
                ps[nb][:, :512], consts["ones_row32"][:, :], pb_row[:, 512 * nb:512 * (nb + 1)],
                start=False, stop=True), rd=[consts["ones_row32"], pb_row, ps[nb]], wr=[ps[nb]])
            k.ins("vector", lambda e, j=j, nb=nb: e.tensor_copy(
                s_tile[:, 512 * nb:512 * (nb + 1)], ps[nb][:, :512]), rd=[ps[nb]], wr=[s_tile])
        k.ins("sync", lambda e, j=j: e.dma_start(sdv[:, j, :], s_tile[:, :]),
              rd=[s_tile], wr=[dram["s"]], dma=True)
    p.rel(out_fm, pwT, pb_row, s_tile, ps[0], ps[1])

    xi = dram["xn"]
    bufs = [dram["acc"], dram["y"][0], dram["y"][1], dram["y"][2]]
    for l in range(2):
        xi = _l3_sublayer(p, aps, consts, dram, xi, bufs[2 * l], bufs[2 * l + 1], l)
    moe_combine(p, consts, state, dram, [xi], x_src, dram["x2"] if x_src is not dram["x2"] else dram["x"],
                "l3cb")


def _l3_shift_load(p, xi, j, xv, label):
    k = p.k
    k.ins("sync", lambda e, j=j, xv=xv: e.dma_start(
        xv[:, 0:512], xi.rearrange("(j q) c -> q j c", q=128)[:, j, 0:512]),
        rd=[xi], wr=[xv], dma=True, label=label)
    if j == 0:
        k.ins("vector", lambda e, xv=xv: e.memset(xv[0:1, 512:1024], 0.0), rd=[], wr=[xv], label=label)
        k.ins("sync", lambda e, xv=xv: e.dma_start(xv[1:128, 512:1024], xi[0:127, 512:1024]),
              rd=[xi], wr=[xv], dma=True, label=label)
    else:
        k.ins("sync", lambda e, j=j, xv=xv: e.dma_start(
            xv[:, 512:1024], xi[128 * j - 1:128 * j + 127, 512:1024]),
            rd=[xi], wr=[xv], dma=True, label=label)


def _l3_sublayer(p, aps, consts, dram, xi, xa_buf, xb_buf, l):
    k = p.k
    HID = 2730
    ps = [p.gps("f3ps", [128, 512], F32) for _ in range(4)]
    h_fm = p.get("f3hf", [128, NI, T], BF16)
    xt = [p.get("f3x", [128, D], F32) for _ in range(2)]
    bt = [p.get("f3b", [128, D], BF16) for _ in range(2)]
    for j in range(NJ):
        xv, bv = xt[j % 2], bt[j % 2]
        _l3_shift_load(p, xi, j, xv, "f3sh")
        k.ins("scalar", lambda e, xv=xv, bv=bv: e.copy(bv[:, :], xv[:, :]), rd=[xv], wr=[bv])
        transpose_pairs(p, [(bv[:, 128 * i:128 * (i + 1)], h_fm[:, i, 128 * j:128 * (j + 1)], bv, h_fm)
                            for i in range(NI)], consts, "f3h")
    wq = load_w_chunks(p, aps["L3f_wqkv"][l], D, 1536, "f3wq")
    qsr = p.get("f3qsr", [128, 512], F32)
    ksr = p.get("f3ksr", [128, 512], F32)
    qs_row = load_f32(p, aps["L3f_q_scale"][l:l + 1, :], [1, 64], "f3qs")
    ks_row = load_f32(p, aps["L3f_k_scale"][l:l + 1, :], [1, 64], "f3ks")
    for i in range(8):
        k.ins("tensor", lambda e, i=i: e.matmul(ps[2][:, 64 * i:64 * (i + 1)], consts["ones_row32"][:, :],
                                                qs_row[:, :], start=(i == 0), stop=(i == 7)),
              rd=[consts["ones_row32"], qs_row, ps[2]], wr=[ps[2]])
    k.ins("vector", lambda e: e.tensor_copy(qsr[:, :], ps[2][:, :512]), rd=[ps[2]], wr=[qsr])
    for i in range(8):
        k.ins("tensor", lambda e, i=i: e.matmul(ps[2][:, 64 * i:64 * (i + 1)], consts["ones_row32"][:, :],
                                                ks_row[:, :], start=(i == 0), stop=(i == 7)),
              rd=[consts["ones_row32"], ks_row, ps[2]], wr=[ps[2]])
    k.ins("vector", lambda e: e.tensor_copy(ksr[:, :], ps[2][:, :512]), rd=[ps[2]], wr=[ksr])
    p.rel(qs_row, ks_row)
    q_fm = p.get("f3qf", [128, 4, T], BF16)
    k_fm = p.get("f3kf", [128, 4, T], BF16)
    v_tm = p.get("f3vt", [128, NJ, 512], BF16)
    sa = p.get("f3sa", [128, 512], F32)
    sb_ = p.get("f3sb", [128, 512], F32)
    nco = p.get("f3nc", [128, 8], F32)
    tC = p.get("f3tC", [128, 512], F32)
    tS = p.get("f3tS", [128, 512], F32)
    cvq = aps["c_xqC"].rearrange("(j q) w -> j q w", q=128)
    svq = aps["c_xqS"].rearrange("(j q) w -> j q w", q=128)
    cvk = aps["c_xkC"].rearrange("(j q) w -> j q w", q=128)
    svk = aps["c_xkS"].rearrange("(j q) w -> j q w", q=128)
    for part, off in (("q", 0), ("k", 512), ("v", 1024)):
        for j in range(NJ):
            pp = ps[j % 2]
            for ck in range(8):
                k.ins("tensor", lambda e, j=j, ck=ck, off=off, pp=pp: e.matmul(
                    pp[:, :512], h_fm[:, ck, 128 * j:128 * (j + 1)], wq[:, ck, off:off + 512],
                    start=(ck == 0), stop=(ck == 7)), rd=[h_fm, wq, pp], wr=[pp])
            if part == "v":
                k.ins("scalar", lambda e, j=j, pp=pp: e.copy(v_tm[:, j, :], pp[:, :512]), rd=[pp], wr=[v_tm])
                continue
            k.ins("vector", lambda e, pp=pp: e.tensor_copy(sa[:, :], pp[:, :512]), rd=[pp], wr=[sa])
            k.ins("scalar", lambda e: e.activation(sb_[:, :], sa[:, :], AF.Square), rd=[sa], wr=[sb_])
            k.ins("vector", lambda e: e.reduce_sum(nco[:, :], sb_[:, :].rearrange("q (g d) -> q g d", g=8), AX.X),
                  rd=[sb_], wr=[nco])
            k.ins("scalar", lambda e: e.activation(nco[:, :], nco[:, :], AF.Sqrt), rd=[nco], wr=[nco])
            k.ins("vector", lambda e: e.tensor_scalar_max(nco[:, :], nco[:, :], 1e-12), rd=[nco], wr=[nco])
            k.ins("vector", lambda e: e.reciprocal(nco[:, :], nco[:, :]), rd=[nco], wr=[nco])
            for hh in range(8):
                k.ins("vector", lambda e, hh=hh: e.tensor_scalar_mul(
                    sa[:, 64 * hh:64 * (hh + 1)], sa[:, 64 * hh:64 * (hh + 1)], nco[:, hh:hh + 1]),
                    rd=[sa], srd=[nco], wr=[sa])
            sc_rep = qsr if part == "q" else ksr
            k.ins("vector", lambda e, sc_rep=sc_rep: e.tensor_mul(sa[:, :], sa[:, :], sc_rep[:, :]),
                  rd=[sa, sc_rep], wr=[sa])
            cv, sv = (cvq, svq) if part == "q" else (cvk, svk)
            k.ins("sync", lambda e, j=j, cv=cv: e.dma_start(tC[:, :], cv[j]), rd=["c_xqC" if part == "q" else "c_xkC"], wr=[tC], dma=True)
            k.ins("sync", lambda e, j=j, sv=sv: e.dma_start(tS[:, :], sv[j]), rd=["c_xqS" if part == "q" else "c_xkS"], wr=[tS], dma=True)
            k.ins("vector", lambda e: e.tensor_mul(sb_[:, :], sa[:, :], tC[:, :]), rd=[sa, tC], wr=[sb_])
            av = sa[:, :].rearrange("q (h s d) -> q h s d", h=8, s=2)
            sv2 = tS[:, :].rearrange("q (h s d) -> q h s d", h=8, s=2)
            ov = tC[:, :].rearrange("q (h s d) -> q h s d", h=8, s=2)
            k.ins("vector", lambda e, av=av, sv2=sv2, ov=ov: e.tensor_tensor(
                ov[:, :, 0, :], av[:, :, 1, :], sv2[:, :, 0, :], ALU.mult), rd=[sa, tS], wr=[tC])
            k.ins("vector", lambda e, av=av, sv2=sv2, ov=ov: e.tensor_tensor(
                ov[:, :, 1, :], av[:, :, 0, :], sv2[:, :, 1, :], ALU.mult), rd=[sa, tS], wr=[tC])
            k.ins("vector", lambda e: e.tensor_add(sb_[:, :], sb_[:, :], tC[:, :]), rd=[tC, sb_], wr=[sb_])
            dst = q_fm if part == "q" else k_fm
            k.ins("scalar", lambda e: e.copy(sa[:, :].bitcast(F32) if False else sb_[:, :], sb_[:, :]),
                  rd=[sb_], wr=[sb_]) if False else None
            cb = p.get("f3cb", [128, 512], BF16)
            k.ins("scalar", lambda e, cb=cb: e.copy(cb[:, :], sb_[:, :]), rd=[sb_], wr=[cb])
            transpose_pairs(p, [(cb[:, 128 * i:128 * (i + 1)], dst[:, i, 128 * j:128 * (j + 1)], cb, dst)
                                for i in range(4)], consts, "f3t")
            p.rel(cb)
    p.rel(wq, sa, sb_, nco, tC, tS, qsr, ksr, h_fm)

    psA, psB, psD = ps[0], ps[1], ps[2]
    psN = p.gps("f3pN", [128, 512], F32)
    prb = p.get("f3prb", [128, 640], BF16)
    den2 = p.get("f3dn", [128, 1], F32)
    o_tm = p.get("f3ot", [128, NJ, 512], BF16)
    for h in range(8):
        b = (h % 2) * 64
        ci = h // 2
        for a in range(NJ):
            w_ = a // 4
            al = a % 4
            kts = []
            if w_ > 0:
                for beta in range(al, 4):
                    kts.append((4 * (w_ - 1) + beta, "prev_diag" if beta == al else "full"))
            for beta in range(0, al + 1):
                kts.append((4 * w_ + beta, "own_diag" if beta == al else "full"))
            nn = len(kts)

            def lqk(n_):
                kt, kind = kts[n_]
                pp = psA if n_ < 4 else psB
                colo = 128 * (n_ % 4)
                k.ins("tensor", lambda e, kt=kt, a=a, b=b, ci=ci, pp=pp, colo=colo: e.matmul(
                    pp[:, colo:colo + 128], k_fm[b:b + 64, ci, 128 * kt:128 * (kt + 1)],
                    q_fm[b:b + 64, ci, 128 * a:128 * (a + 1)], start=True, stop=True),
                    rd=[k_fm, q_fm, pp], wr=[pp], label="f3qk")
                k.ins("scalar", lambda e, pp=pp, colo=colo, n_=n_: e.activation(
                    prb[:, 128 * n_:128 * (n_ + 1)], pp[:, colo:colo + 128], AF.Exp, scale=8.0),
                    rd=[pp], wr=[prb], label="f3ex")
                if kind == "own_diag":
                    k.ins("vector", lambda e, n_=n_: e.tensor_mul(
                        prb[:, 128 * n_:128 * (n_ + 1)], prb[:, 128 * n_:128 * (n_ + 1)], consts["triu"][:, :]),
                        rd=[consts["triu"]], wr=[prb])
                elif kind == "prev_diag":
                    k.ins("vector", lambda e, n_=n_: e.tensor_mul(
                        prb[:, 128 * n_:128 * (n_ + 1)], prb[:, 128 * n_:128 * (n_ + 1)], consts["tril"][:, :]),
                        rd=[consts["tril"]], wr=[prb])

            def ldnpv(n_):
                kt, kind = kts[n_]
                k.ins("tensor", lambda e, n_=n_, nn=nn: e.matmul(
                    psD[:, 0:1], prb[:, 128 * n_:128 * (n_ + 1)], consts["ones_col_bf"][:, :],
                    start=(n_ == 0), stop=(n_ == nn - 1)),
                    rd=[prb, consts["ones_col_bf"], psD], wr=[psD], label="f3dn")
                k.ins("tensor", lambda e, n_=n_, kt=kt, h=h, nn=nn: e.matmul(
                    psN[:, 0:64], prb[:, 128 * n_:128 * (n_ + 1)], v_tm[:, kt, 64 * h:64 * (h + 1)],
                    start=(n_ == 0), stop=(n_ == nn - 1)),
                    rd=[prb, v_tm, psN], wr=[psN], label="f3pv")

            lqk(0)
            for n_ in range(1, nn):
                lqk(n_)
                ldnpv(n_ - 1)
            ldnpv(nn - 1)
            k.ins("vector", lambda e: e.reciprocal(den2[:, :], psD[:, 0:1]), rd=[psD], wr=[den2])
            k.ins("vector", lambda e, a=a, h=h: e.tensor_scalar_mul(
                o_tm[:, a, 64 * h:64 * (h + 1)], psN[:, 0:64], den2[:, 0:1]),
                rd=[psN], srd=[den2], wr=[o_tm])
    p.rel(psN, prb, den2, q_fm, k_fm, v_tm)

    o_fm = p.get("f3of", [128, 4, T], BF16)
    transpose_pairs(p, [(o_tm[:, j, 128 * i:128 * (i + 1)], o_fm[:, i, 128 * j:128 * (j + 1)], o_tm, o_fm)
                        for j in range(NJ) for i in range(4)], consts, "f3oT")
    p.rel(o_tm)
    wo = load_w_chunks(p, aps["L3f_wo"][l], 512, 1024, "f3wo")
    xat = [p.get("f3xa", [128, D], F32) for _ in range(2)]
    xav = xa_buf.rearrange("(j q) c -> q j c", q=128)
    xiv = xi.rearrange("(j q) c -> q j c", q=128)
    for j in range(NJ):
        xv = xat[j % 2]
        k.ins("sync", lambda e, j=j, xv=xv: e.dma_start(xv[:, :], xiv[:, j, :]), rd=[xi], wr=[xv], dma=True)
        for nb in range(2):
            pp = ps[nb]
            for ck in range(4):
                k.ins("tensor", lambda e, j=j, nb=nb, ck=ck, pp=pp: e.matmul(
                    pp[:, :512], o_fm[:, ck, 128 * j:128 * (j + 1)], wo[:, ck, 512 * nb:512 * (nb + 1)],
                    start=(ck == 0), stop=(ck == 3)), rd=[o_fm, wo, pp], wr=[pp])
            k.ins("vector", lambda e, nb=nb, pp=pp, xv=xv: e.tensor_add(
                xv[:, 512 * nb:512 * (nb + 1)], xv[:, 512 * nb:512 * (nb + 1)], pp[:, :512]),
                rd=[pp, xv], wr=[xv])
        k.ins("sync", lambda e, j=j, xv=xv: e.dma_start(xav[:, j, :], xv[:, :]), rd=[xv], wr=[xa_buf], dma=True)
    p.rel(o_fm, wo, xat[0], xat[1])

    lngr = load_rep(p, aps["L3f_ln_g"][l:l + 1, :], D, consts, ps[2], "f3lg")
    lnbr = load_rep(p, aps["L3f_ln_b"][l:l + 1, :], D, consts, ps[2], "f3lb")
    hln_fm = p.get("f3hln", [128, NI, T], BF16)
    mv = p.get("f3mv", [128, 1, 2], F32)
    s12 = p.get("f3s12", [128, 12], F32)
    rstd = p.get("f3rs", [128, 1], F32)
    nmean = p.get("f3nm", [128, 1], F32)
    epsap = p.cval(1e-5)
    for j in range(NJ):
        xv, bv = xt[j % 2], bt[j % 2]
        _l3_shift_load(p, xa_buf, j, xv, "f3sh2")
        k.ins("vector", lambda e, xv=xv: e.bn_stats(s12[:, 0:6], xv[:, 0:512]), rd=[xv], wr=[s12])
        k.ins("vector", lambda e, xv=xv: e.bn_stats(s12[:, 6:12], xv[:, 512:1024]), rd=[xv], wr=[s12])
        k.ins("vector", lambda e: e.bn_aggr(mv[:, 0, :], s12[:, 0:12]), rd=[s12], wr=[mv])
        k.ins("scalar", lambda e: e.activation(rstd[:, :], mv[:, 0, 1:2], AF.Sqrt, bias=epsap),
              rd=[mv, epsap], wr=[rstd])
        k.ins("vector", lambda e: e.reciprocal(rstd[:, :], rstd[:, :]), rd=[rstd], wr=[rstd])
        k.ins("vector", lambda e: e.tensor_scalar_mul(nmean[:, :], mv[:, 0, 0:1], -1.0), rd=[mv], wr=[nmean])
        k.ins("vector", lambda e, xv=xv: e.tensor_scalar(
            xv[:, :], xv[:, :], nmean[:, 0:1], rstd[:, 0:1], ALU.add, ALU.mult),
            rd=[xv], srd=[nmean, rstd], wr=[xv])
        k.ins("vector", lambda e, xv=xv: e.tensor_mul(xv[:, :], xv[:, :], lngr[:, :]), rd=[lngr, xv], wr=[xv])
        k.ins("vector", lambda e, xv=xv, bv=bv: e.tensor_add(bv[:, :].bitcast(F32) if False else xv[:, :],
                                                             xv[:, :], lnbr[:, :]), rd=[lnbr, xv], wr=[xv])
        k.ins("scalar", lambda e, xv=xv, bv=bv: e.copy(bv[:, :], xv[:, :]), rd=[xv], wr=[bv])
        transpose_pairs(p, [(bv[:, 128 * i:128 * (i + 1)], hln_fm[:, i, 128 * j:128 * (j + 1)], bv, hln_fm)
                            for i in range(NI)], consts, "f3hl")
    p.rel(lngr, lnbr, mv, s12, rstd, nmean, xt[0], xt[1], bt[0], bt[1])

    w2sb = [p.get("f3w2a", [128, 11, 1024], BF16), p.get("f3w2b", [128, 11, 1024], BF16)]
    w2s = aps["L3f_w2"][l]
    for kc in range(21):
        t2, s2 = w2sb[kc // 11], kc % 11
        k.ins("gpsimd", lambda e, kc=kc, t2=t2, s2=s2: e.dma_start(
            t2[:, s2, :], w2s[128 * kc:128 * (kc + 1), :]), rd=[w2s], wr=[t2], dma=True)
    k.ins("gpsimd", lambda e: e.dma_start(w2sb[1][0:42, 10, :], w2s[2688:2730, :]),
          rd=[w2s], wr=[w2sb[1]], dma=True)
    b1row = load_f32(p, aps["L3f_b1"][l:l + 1, :], [1, 5460], "f3b1")
    b2row = load_f32(p, aps["L3f_b2"][l:l + 1, :], [1, D], "f3b2")
    bca = p.get("f3bca", [128, 22], F32)
    bcg = p.get("f3bcg", [128, 22], F32)
    row_to_cols(p, bca, b1row[:, 0:HID], HID, consts, ps[2], "f3ba")
    row_to_cols(p, bcg, b1row[:, HID:2 * HID], HID, consts, ps[2], "f3bg")
    w1v = aps["L3f_w1"][l].rearrange("(kc kp) n -> kc kp n", kp=128)
    w1t = [p.get("f3w1s", [128, 8, 128], BF16) for _ in range(4)]
    act_ci = [p.get("f3aci", [128, T], BF16) for _ in range(2)]
    gf = p.get("f3gf", [128, 2048], F32)
    af = p.get("f3af", [128, 2048], BF16)
    ysb = [p.get("f3y", [128, D], F32) for _ in range(2)]
    xbv = xb_buf.rearrange("(j q) c -> q j c", q=128)
    for ci2 in range(22):
        w_rows = 128 if ci2 < 21 else 42
        wa, wg = w1t[2 * (ci2 % 2)], w1t[2 * (ci2 % 2) + 1]
        for ck in range(8):
            k.ins("gpsimd", lambda e, ck=ck, ci2=ci2, wa=wa, w_rows=w_rows: e.dma_start(
                wa[:, ck, 0:w_rows], w1v[ck, :, 128 * ci2:128 * ci2 + w_rows]),
                rd=[w1v], wr=[wa], dma=True)
            k.ins("gpsimd", lambda e, ck=ck, ci2=ci2, wg=wg, w_rows=w_rows: e.dma_start(
                wg[:, ck, 0:w_rows], w1v[ck, :, HID + 128 * ci2:HID + 128 * ci2 + w_rows]),
                rd=[w1v], wr=[wg], dma=True)
        at = act_ci[ci2 % 2]
        for tb in range(8):
            t0 = 256 * tb
            pA, pB = ps[tb % 2], ps[2 + tb % 2]
            for ck in range(8):
                k.ins("tensor", lambda e, ck=ck, t0=t0, w_rows=w_rows, pA=pA, wa=wa: e.matmul(
                    pA[:w_rows, :256], wa[:, ck, 0:w_rows], hln_fm[:, ck, t0:t0 + 256],
                    start=(ck == 0), stop=(ck == 7)), rd=[wa, hln_fm, pA], wr=[pA])
            k.ins("scalar", lambda e, ci2=ci2, w_rows=w_rows, pA=pA, t0=t0: e.activation(
                af[:w_rows, t0:t0 + 256], pA[:w_rows, :256], AF.Identity, bias=bca[:w_rows, ci2:ci2 + 1]),
                rd=[pA, bca], wr=[af])
            for ck in range(8):
                k.ins("tensor", lambda e, ck=ck, t0=t0, w_rows=w_rows, pB=pB, wg=wg: e.matmul(
                    pB[:w_rows, :256], wg[:, ck, 0:w_rows], hln_fm[:, ck, t0:t0 + 256],
                    start=(ck == 0), stop=(ck == 7)), rd=[wg, hln_fm, pB], wr=[pB])
            k.ins("scalar", lambda e, ci2=ci2, w_rows=w_rows, pB=pB, t0=t0: e.activation(
                gf[:w_rows, t0:t0 + 256], pB[:w_rows, :256], AF.Gelu, bias=bcg[:w_rows, ci2:ci2 + 1]),
                rd=[pB, bcg], wr=[gf])
        k.ins("vector", lambda e, w_rows=w_rows, at=at: e.tensor_mul(
            at[:w_rows, :], af[:w_rows, :], gf[:w_rows, :]),
            rd=[af, gf], wr=[at])
        k.ins("sync", lambda e, ci2=ci2, at=at, w_rows=w_rows: e.dma_start(
            dram["act"][ci2, 0:w_rows, :], at[:w_rows, :]),
            rd=[at], wr=[dram["act"]], dma=True)
    p.rel(w1t[0], w1t[1], w1t[2], w1t[3], gf, af, hln_fm, bca, bcg, b1row, act_ci[0], act_ci[1])
    actj = [p.get("f3aj", [128, 22, 128], BF16) for _ in range(2)]
    actv = dram["act"].transpose([1, 0, 2])
    for j in range(NJ):
        yv = ysb[j % 2]
        aj = actj[j % 2]
        k.ins("sync", lambda e, j=j, aj=aj: e.dma_start(aj[:, :, :], actv[:, :, 128 * j:128 * (j + 1)]),
              rd=[dram["act"]], wr=[aj], dma=True)
        k.ins("sync", lambda e, j=j, yv=yv: e.dma_start(
            yv[:, :], xa_buf.rearrange("(j q) c -> q j c", q=128)[:, j, :]),
            rd=[xa_buf], wr=[yv], dma=True)
        for nb in range(2):
            pY = ps[nb]
            for ci2 in range(22):
                w_rows = 128 if ci2 < 21 else 42
                t2, s2 = (w2sb[ci2 // 11], ci2 % 11)
                k.ins("tensor", lambda e, ci2=ci2, j=j, nb=nb, w_rows=w_rows, pY=pY,
                      aj=aj, t2=t2, s2=s2: e.matmul(
                    pY[:, :512], aj[:w_rows, ci2, :],
                    t2[:w_rows, s2, 512 * nb:512 * (nb + 1)],
                    start=(ci2 == 0), stop=False), rd=[aj, t2, pY], wr=[pY])
            k.ins("tensor", lambda e, nb=nb, pY=pY: e.matmul(
                pY[:, :512], consts["ones_row32"][:, :], b2row[:, 512 * nb:512 * (nb + 1)],
                start=False, stop=True), rd=[consts["ones_row32"], b2row, pY], wr=[pY])
            k.ins("vector", lambda e, nb=nb, pY=pY, yv=yv: e.tensor_add(
                yv[:, 512 * nb:512 * (nb + 1)], yv[:, 512 * nb:512 * (nb + 1)], pY[:, :512]),
                rd=[pY, yv], wr=[yv])
        k.ins("sync", lambda e, j=j, yv=yv: e.dma_start(xbv[:, j, :], yv[:, :]),
              rd=[yv], wr=[xb_buf], dma=True)
    p.rel(w2sb[0], w2sb[1], b2row, actj[0], actj[1], ysb[0], ysb[1],
          ps[0], ps[1], ps[2], ps[3])
    return xb_buf



def build_program(w, layers=(0, 1, 2, 3), dbg=()):
    nc = bass.Bass("TRN2", target_bir_lowering=False)
    aps = {"x": nc.dram_tensor("x", [T, D], F32, kind="ExternalInput").ap()}
    for n, a in w.items():
        aps[n] = nc.dram_tensor(n, list(a.shape), F32, kind="ExternalInput").ap()
    out = nc.dram_tensor("out", [T, D], F32, kind="ExternalOutput").ap()
    dram = {
        "x": nc.dram_tensor("ds_x", [T, D], F32).ap(),
        "x2": nc.dram_tensor("ds_x2", [T, D], F32).ap(),
        "xn": nc.dram_tensor("ds_xn", [T, D], F32).ap(),
        "s": nc.dram_tensor("ds_s", [T, D], F32).ap(),
        "acc": nc.dram_tensor("ds_acc", [T, D], F32).ap(),
        "y": [nc.dram_tensor(f"ds_y{i}", [T, D], F32).ap() for i in range(8)],
        "act": nc.dram_tensor("ds_act", [22, 128, T], BF16).ap(),
    }
    k = K(nc)
    with ExitStack() as ctx:
        prog = {e: ctx.enter_context(nc.semaphore(f"pr_{e}")) for e in K.ENGINES}
        dsem = {e: ctx.enter_context(nc.semaphore(f"dm_{e}")) for e in K.ENGINES}
        blk = ctx.enter_context(nc.Block())
        p = P(nc, k, ctx)
        consts = setup_consts(p, aps)
        state = {"x_src": aps["x"], "flip": 0}
        emitters = {0: emit_L0, 1: emit_L1, 2: emit_L2, 3: emit_L3}
        import os as _os
        for L in layers:
            emitters[L](p, aps, consts, state, dram)
            if _os.environ.get("KPOOL"):
                inv = {}
                for kk, vv in p.freel.items():
                    if vv:
                        inv[str(kk)] = len(vv)
                print(f"after L{L}: free={inv} used={sorted(str(u[0]) for u in p.used.values())}")
        for dn in dbg:
            dsrc = dram[dn] if not dn.startswith("y") else dram["y"][int(dn[1:])]
            dout = nc.dram_tensor("dbg_" + dn, [T, D], F32, kind="ExternalOutput").ap()
            dt_ = p.get("dbgT", [128, D], F32)
            dsv = dsrc.rearrange("(j q) c -> q j c", q=128)
            dov = dout.rearrange("(j q) c -> q j c", q=128)
            for j in range(NJ):
                k.ins("sync", lambda e, j=j, dt_=dt_, dsv=dsv: e.dma_start(dt_[:, :], dsv[:, j, :]),
                      rd=[dsrc], wr=[dt_], dma=True)
                k.ins("sync", lambda e, j=j, dt_=dt_, dov=dov: e.dma_start(dov[:, j, :], dt_[:, :]),
                      rd=[dt_], wr=["dbg_" + dn], dma=True)
            p.rel(dt_)
        ot = [p.get("outT", [128, D], F32) for _ in range(2)]
        xsv = state["x_src"].rearrange("(j q) c -> q j c", q=128)
        ov = out.rearrange("(j q) c -> q j c", q=128)
        for j in range(NJ):
            tcur = ot[j % 2]
            k.ins("sync", lambda e, j=j, tc=tcur: e.dma_start(tc[:, :], xsv[:, j, :]),
                  rd=[state["x_src"]], wr=[tcur], dma=True)
            k.ins("sync", lambda e, j=j, tc=tcur: e.dma_start(ov[:, j, :], tc[:, :]),
                  rd=[tcur], wr=["out"], dma=True)
        k.finalize(blk, prog, dsem)
    return nc


def kernel(**inputs):
    w = prep_inputs(inputs)
    x = np.ascontiguousarray(np.asarray(inputs["L0_attn_x"], np.float32))
    nc = build_program(w, layers=(0, 1, 2, 3))
    in_maps = []
    for core in range(8):
        m = {"x": np.ascontiguousarray(x[core])}
        m.update(w)
        in_maps.append(m)
    res = run_bass_kernel_spmd(nc, in_maps, core_ids=list(range(8)))
    out = np.stack([np.asarray(r["out"]) for r in res.results], axis=0)
    return out.astype(np.float32)


# version 05
# speedup vs baseline: 1.0091x; 1.0091x over previous
import numpy as np
from contextlib import ExitStack
import concourse.bass as bass
import concourse.mybir as mybir
from concourse.bass_utils import run_bass_kernel_spmd

F32 = mybir.dt.float32
BF16 = mybir.dt.bfloat16
AF = mybir.ActivationFunctionType
ALU = mybir.AluOpType
AX = mybir.AxisListType

T = 2048
NJ = 16
D = 1024
NI = 8


class Rec:
    __slots__ = ("eng", "fn", "idx", "dma", "marked", "deps", "label", "waits", "dmaord")

class K:
    ENGINES = ("tensor", "vector", "scalar", "gpsimd", "sync")

    def __init__(self, nc):
        self.nc = nc
        self.recs = {e: [] for e in self.ENGINES}
        self.last_w = {}
        self.readers = {}
        self.ndma = {e: 0 for e in self.ENGINES}
        self.evac_tog = 0

    def ins(self, eng, fn, rd=(), wr=(), dma=False, label="", srd=()):
        r = Rec()
        r.eng = eng; r.fn = fn; r.idx = len(self.recs[eng]); r.dma = dma
        r.marked = False; r.deps = set(); r.label = label; r.waits = None
        r.dmaord = self.ndma[eng] + 1 if dma else 0
        if dma:
            self.ndma[eng] += 1
        rdk = [a if isinstance(a, str) else a.name for a in rd]
        wrk = [a if isinstance(a, str) else a.name for a in wr]
        for a in srd:
            key = a if isinstance(a, str) else a.name
            rdk.append(key)
            w = self.last_w.get(key)
            if w is not None and w is not r:
                r.deps.add(w)
        raw_same = eng in ("vector", "scalar", "gpsimd")
        for key in rdk:
            w = self.last_w.get(key)
            if w is not None and w is not r and (w.eng != eng or w.dma or raw_same):
                r.deps.add(w)
        for key in wrk:
            w = self.last_w.get(key)
            if w is not None and (w.eng != eng or w.dma):
                r.deps.add(w)
            for rr in self.readers.get(key, ()):
                if rr is not r and (rr.eng != eng or rr.dma):
                    r.deps.add(rr)
        for key in rdk:
            self.readers.setdefault(key, []).append(r)
        for key in wrk:
            self.last_w[key] = r
            self.readers[key] = []
        self.recs[eng].append(r)
        return r

    def barrier(self):
        lasts = [self.recs[e][-1] for e in self.ENGINES if self.recs[e]]
        for e in self.ENGINES:
            r = self.ins(e, lambda eng: eng.nop(), label="barrier")
            for l in lasts:
                if l is not r and (l.eng != e or l.dma):
                    r.deps.add(l)
        self.last_w = {}
        self.readers = {}

    def evac_eng(self):
        self.evac_tog ^= 1
        return "vector" if self.evac_tog else "scalar"

    def finalize(self, blk, prog, dsem):
        for e in self.ENGINES:
            for r in self.recs[e]:
                for d in r.deps:
                    if not d.dma:
                        d.marked = True
        cpre = {}
        for e in self.ENGINES:
            c = 0; pc = []
            for r in self.recs[e]:
                if r.marked and not r.dma:
                    c += 1
                pc.append(c)
            cpre[e] = pc
        for e in self.ENGINES:
            hw = {}
            for r in self.recs[e]:
                wl = {}
                for d in r.deps:
                    if d.dma:
                        sem, val = dsem[d.eng], 16 * d.dmaord
                    else:
                        sem, val = prog[d.eng], cpre[d.eng][d.idx]
                    if val > wl.get(sem, 0):
                        wl[sem] = val
                r.waits = [(s, v) for s, v in wl.items() if v > hw.get(s, 0)]
                for s, v in wl.items():
                    if v > hw.get(s, 0):
                        hw[s] = v

        def mkbody(e):
            items = self.recs[e]
            tot = self.ndma[e]

            def body(eng):
                for r in items:
                    for s, v in r.waits:
                        eng.wait_ge(s, v)
                    i = r.fn(eng)
                    if r.dma:
                        i.then_inc(dsem[e], 16)
                    elif r.marked:
                        i.then_inc(prog[e], 1)
                if tot:
                    eng.wait_ge(dsem[e], 16 * tot)
            return body

        for e in self.ENGINES:
            getattr(blk, e)(mkbody(e))



def _host_consts():
    c = {}
    c["c_ident"] = np.eye(128, dtype=np.float32)
    i = np.arange(128)
    c["c_triu01"] = (i[:, None] <= i[None, :]).astype(np.float32)
    c["c_tril01"] = (i[:, None] >= i[None, :]).astype(np.float32)
    c["c_upstrict"] = (i[:, None] < i[None, :]).astype(np.float32)
    c["c_ones_col"] = np.ones((128, 1), np.float32)
    c["c_ones_row"] = np.ones((1, 128), np.float32)
    c["c_pos"] = (np.arange(NJ, dtype=np.float32)[None, :] * 128
                  + np.arange(128, dtype=np.float32)[:, None]).astype(np.float32)
    inv = 1.0 / 10000.0 ** (np.arange(0, 64, 2, dtype=np.float32) / 64)
    tt = np.arange(T, dtype=np.float32)
    f = tt[:, None] * inv[None, :]
    cos1 = np.cos(f); sin1 = np.sin(f)
    C64 = np.concatenate([cos1, cos1], 1)
    S64 = np.concatenate([-sin1, sin1], 1)
    c["c_ropeC"] = np.tile(C64, (1, 16)).astype(np.float32)
    c["c_ropeS"] = np.tile(S64, (1, 16)).astype(np.float32)
    half_idx = np.arange(0, 64, 2, dtype=np.float32)
    invf = 1.0 / 10000.0 ** (half_idx / 64)
    fr = tt[:, None] * invf[None, :]
    fr2 = np.concatenate([fr, fr], axis=-1)
    base = (half_idx + 0.4 * 64) / (1.4 * 64)
    power = (tt - T // 2) / 256.0
    sc = base[None, :] ** power[:, None]
    sc2 = np.concatenate([sc, sc], axis=-1)
    cos2, sin2 = np.cos(fr2), np.sin(fr2)
    sgn = np.where(np.arange(64) < 32, -1.0, 1.0).astype(np.float32)[None, :]
    c["c_xqC"] = np.tile(cos2 * sc2, (1, 8)).astype(np.float32)
    c["c_xqS"] = np.tile(sin2 * sc2 * sgn, (1, 8)).astype(np.float32)
    c["c_xkC"] = np.tile(cos2 / sc2, (1, 8)).astype(np.float32)
    c["c_xkS"] = np.tile(sin2 / sc2 * sgn, (1, 8)).astype(np.float32)
    return c

def prep_inputs(d):
    w = {}
    g = lambda n: np.ascontiguousarray(np.asarray(d[n], np.float32))
    tr = lambda n: np.ascontiguousarray(np.asarray(d[n], np.float32).T)
    w["L0_W_dq"] = g("L0_attn_W_dq")
    w["L0_W_dqT"] = tr("L0_attn_W_dq")
    w["L0_W_uq"] = g("L0_attn_W_uq")
    w["L0_W_uqT"] = tr("L0_attn_W_uq")
    w["L0_W_dkvT"] = tr("L0_attn_W_dkv")
    w["L0_W_uk"] = g("L0_attn_W_uk")
    w["L0_W_uv"] = g("L0_attn_W_uv")
    w["L0_W_oT"] = tr("L0_attn_W_o")
    for n in ["ln1_w", "ln2_w", "bq", "bk", "bv", "qn_w", "kn_w", "bg", "bo", "br", "b1", "b2",
              "Wq", "Wk", "Wv", "Wg", "Wo", "Wr", "W1", "W2"]:
        w["L0_" + n] = g("L0_ffn_" + n)
    w["L1_gamma"] = g("L1_attn_gamma")
    w["L1_w_q"] = g("L1_attn_w_q")
    w["L1_w_kv"] = g("L1_attn_w_kv")
    w["L1_w_out"] = g("L1_attn_w_out")
    for n in ["Wg", "Ws1", "Ws3", "Ws2", "We1", "We3", "We2"]:
        w["L1_" + n] = g("L1_ffn_" + n)
    w["L2_wqkv"] = g("L2_attn_wqkv")
    w["L2_router_wT"] = tr("L2_ffn_router_w")
    w["L2_router_b"] = g("L2_ffn_router_b")
    w["L2_expert_wT"] = np.ascontiguousarray(
        np.asarray(d["L2_ffn_expert_w"], np.float32).transpose(0, 2, 1))
    w["L2_expert_b"] = g("L2_ffn_expert_b")
    w["L3_ln_g"] = g("L3_attn_ln_g"); w["L3_ln_b"] = g("L3_attn_ln_b")
    w["L3_qkv_wT"] = tr("L3_attn_qkv_w"); w["L3_qkv_b"] = g("L3_attn_qkv_b")
    w["L3_gate_wT"] = tr("L3_attn_gate_w"); w["L3_gate_b"] = g("L3_attn_gate_b")
    w["L3_proj_wT"] = tr("L3_attn_proj_w"); w["L3_proj_b"] = g("L3_attn_proj_b")
    for n in ["wqkv", "q_scale", "k_scale", "wo", "ln_g", "ln_b", "w1", "b1", "w2", "b2"]:
        w["L3f_" + n] = g("L3_ffn_" + n)
    w.update(_host_consts())
    return w





BCLS = [16, 32, 64, 128, 256, 512, 1024, 2048, 4096, 8192, 12288, 16384, 24576, 32768, 49152, 65536]
I8 = mybir.dt.int8

class P:

    def __init__(self, nc, k, ctx):
        self.nc, self.k, self.ctx = nc, k, ctx
        self.uid = 0
        self.freel = {}
        self.used = {}

    def name(self, base):
        self.uid += 1
        return f"{base}_{self.uid}"

    def sb(self, base, shape, dt=F32):
        return self.ctx.enter_context(self.nc.sbuf_tensor(self.name(base), shape, dt))

    def _cls(self, n):
        for c in BCLS:
            if n <= c:
                return c
        raise ValueError(f"too big: {n}")

    def _view(self, t, shape, dt):
        nel = int(np.prod(shape[1:])) if len(shape) > 1 else 1
        esz = 4 if dt in (F32, mybir.dt.int32) else 2
        v = t[:, :nel * esz].bitcast(dt)
        if len(shape) == 3:
            return v.rearrange("q (a b) -> q a b", a=shape[1])
        if shape[0] != 128:
            return v[:shape[0], :]
        return v

    def get(self, base, shape, dt=F32):
        esz = 4 if dt in (F32, mybir.dt.int32) else 2
        nby = int(np.prod(shape[1:])) * esz
        c = self._cls(nby)
        key = ("sb", c)
        lst = self.freel.get(key)
        if lst:
            t = lst.pop()
        else:
            try:
                t = self.ctx.enter_context(self.nc.sbuf_tensor(self.name("pl_" + base), [128, c], I8))
                import os as _o
                if _o.environ.get("KPOOL"):
                    print(f"NEWBLK {t.name} cls={c}")
            except AssertionError:
                for c2 in BCLS:
                    if c2 > c and self.freel.get(("sb", c2)):
                        t = self.freel[("sb", c2)].pop()
                        key = ("sb", c2)
                        break
                else:
                    fr = {}
                    us = {}
                    for kk, vv in self.freel.items():
                        if vv:
                            fr[str(kk)] = len(vv)
                    for nmm, (kk2, _) in self.used.items():
                        us.setdefault(str(kk2), []).append(nmm)
                    print("POOL free:", fr)
                    print("POOL used:", {kk: vv for kk, vv in sorted(us.items())})
                    raise
        self.used[t.name] = (key, t)
        return self._view(t, shape, dt)

    def gps(self, base, shape, dt=F32):
        c = 512 if dt == F32 else 1024
        assert int(np.prod(shape[1:])) <= c, shape
        key = ("ps", str(dt), c)
        lst = self.freel.get(key)
        if lst:
            t = lst.pop()
        else:
            t = self.ctx.enter_context(self.nc.psum_tensor(self.name("pp_" + base), [128, c], dt))
        self.used[t.name] = (key, t)
        return self._view_ps(t, shape)

    def _view_ps(self, t, shape):
        nel = int(np.prod(shape[1:])) if len(shape) > 1 else 1
        v = t[:, :nel]
        if len(shape) == 3:
            return v.rearrange("q (a b) -> q a b", a=shape[1])
        if shape[0] != 128:
            return t[:shape[0], :shape[1]]
        return v

    def rel(self, *views):
        for v in views:
            nm = v.name
            key, t = self.used.pop(nm)
            self.freel.setdefault(key, []).append(t)

    _cvals = None

    def cval(self, value):
        if self._cvals is None:
            self._cvals = {}
        if value not in self._cvals:
            t = self.sb(f"cv_{len(self._cvals)}", [128, 1], F32)
            self.k.ins("vector", lambda e, t=t, v=value: e.memset(t[:, :], v), rd=[], wr=[t])
            self._cvals[value] = t
        return self._cvals[value][:, :]


def ecopy(e, out, in_):
    if hasattr(e, "tensor_copy"):
        return e.tensor_copy(out, in_)
    return e.copy(out, in_)


def load_bf(p, src, shape_sb, label, rearr=None):
    t = p.get(label, shape_sb, BF16)
    s = src if rearr is None else src.rearrange(rearr[0], **rearr[1])
    p.k.ins("gpsimd", lambda e, d=t, s=s: e.dma_start(d, s), rd=[s], wr=[t], dma=True, label=label)
    return t

def load_f32(p, src, shape_sb, label, rearr=None):
    t = p.get(label, shape_sb, F32)
    s = src if rearr is None else src.rearrange(rearr[0], **rearr[1])
    p.k.ins("sync", lambda e, d=t, s=s: e.dma_start(d, s), rd=[s], wr=[t], dma=True, label=label)
    return t

def load_w_chunks(p, src, Kdim, N, label, into=None):
    nk = Kdim // 128
    s = src.rearrange("(kc kp) n -> kc kp n", kp=128)
    t = into if into is not None else p.get(label, [128, nk, N], BF16)
    for kc in range(nk):
        p.k.ins("gpsimd", lambda e, d=t, s=s, kc=kc: e.dma_start(d[:, kc, :], s[kc]),
                rd=[s], wr=[t], dma=True, label=f"{label}k{kc}")
    return t


def transpose_pairs(p, pairs, consts, label=""):
    k = p.k
    ptr = consts["ptr"]
    for n, (s_ap, d_ap, s_t, d_t) in enumerate(pairs):
        ps = ptr[n % len(ptr)]
        k.ins("tensor", lambda e, o=ps, i=s_ap: e.transpose(o[:i.shape[1], :i.shape[0]], i, consts["ident"][:, :]),
              rd=[s_t, consts["ident"], ps], wr=[ps], label="tr" + label)
        k.ins(k.evac_eng(), lambda e, o=d_ap, i=ps: ecopy(e, o, i[:o.shape[0], :o.shape[1]]),
              rd=[ps], wr=[d_t], label="te" + label)


def tm_to_fm(p, dst_fm, src_tm, consts, nj=NJ, ni=NI, label="t2f"):
    pairs = []
    for j in range(nj):
        for i in range(ni):
            pairs.append((src_tm[:, j, 128 * i:128 * (i + 1)],
                          dst_fm[:, i, 128 * j:128 * (j + 1)], src_tm, dst_fm))
    transpose_pairs(p, pairs, consts, label)


def tm_to_fm_cast(p, dst_fm, src_tm_f32, consts, nj=NJ, ni=NI, label="t2fc"):
    k = p.k
    cs = [p.get(label + "c", [128, ni * 128], BF16) for _ in range(2)]
    for j in range(nj):
        c = cs[j % 2]
        k.ins("scalar", lambda e, j=j, c=c: e.copy(c[:, :], src_tm_f32[:, j, :]),
              rd=[src_tm_f32], wr=[c])
        pairs = [(c[:, 128 * i:128 * (i + 1)], dst_fm[:, i, 128 * j:128 * (j + 1)], c, dst_fm)
                 for i in range(ni)]
        transpose_pairs(p, pairs, consts, label)
    p.rel(cs[0], cs[1])


def bn_meanvar(p, mv, src_tm, j, width, scratch, label="bn"):
    k = p.k
    if width <= 512:
        k.ins("vector", lambda e: e.bn_stats(scratch[:, 0:6], src_tm[:, j, 0:width]), rd=[src_tm], wr=[scratch])
        k.ins("vector", lambda e: e.bn_aggr(mv[:, j, :], scratch[:, 0:6]), rd=[scratch], wr=[mv])
    else:
        k.ins("vector", lambda e: e.bn_stats(scratch[:, 0:6], src_tm[:, j, 0:512]), rd=[src_tm], wr=[scratch])
        k.ins("vector", lambda e: e.bn_stats(scratch[:, 6:12], src_tm[:, j, 512:1024]), rd=[src_tm], wr=[scratch])
        k.ins("vector", lambda e: e.bn_aggr(mv[:, j, :], scratch[:, 0:12]), rd=[scratch], wr=[mv])


def layernorm_tm(p, xn_tm, x_tm, consts, eps=1e-5, label="ln"):
    k = p.k
    mv = p.get(label + "mv", [128, NJ, 2], F32)
    rstd = p.get(label + "rs", [128, NJ], F32)
    nmean = p.get(label + "nm", [128, NJ], F32)
    scr = p.get(label + "sc", [128, 12], F32)
    for j in range(NJ):
        bn_meanvar(p, mv, x_tm, j, D, scr, label)
    epsap = p.cval(float(eps))
    k.ins("scalar", lambda e: e.activation(rstd[:, :], mv[:, :, 1], AF.Sqrt, bias=epsap),
          rd=[mv, epsap], wr=[rstd])
    k.ins("vector", lambda e: e.reciprocal(rstd[:, :], rstd[:, :]), rd=[rstd], wr=[rstd])
    k.ins("vector", lambda e: e.tensor_scalar_mul(nmean[:, :], mv[:, :, 0], -1.0), rd=[mv], wr=[nmean])
    for j in range(NJ):
        k.ins("vector", lambda e, j=j: e.tensor_scalar(
            xn_tm[:, j, :], x_tm[:, j, :], nmean[:, j:j + 1], rstd[:, j:j + 1], ALU.add, ALU.mult),
            rd=[x_tm], srd=[nmean, rstd], wr=[xn_tm])
    p.rel(mv, rstd, nmean, scr)


def rms_apply(p, dst_bf, src_tm, wrep, eps, label):
    k = p.k
    mv = p.get(label + "m", [128, NJ, 2], F32)
    rs = p.get(label + "r", [128, NJ], F32)
    tmp = p.get(label + "t", [128, D], F32)
    scr12 = p.get(label + "sc", [128, 12], F32)
    for j in range(NJ):
        bn_meanvar(p, mv, src_tm, j, D, scr12, label)
    k.ins("vector", lambda e: e.tensor_mul(rs[:, :], mv[:, :, 0], mv[:, :, 0]), rd=[mv], wr=[rs])
    k.ins("vector", lambda e: e.tensor_add(rs[:, :], rs[:, :], mv[:, :, 1]), rd=[mv, rs], wr=[rs])
    epsap = p.cval(float(eps))
    k.ins("scalar", lambda e: e.activation(rs[:, :], rs[:, :], AF.Sqrt, bias=epsap), rd=[rs, epsap], wr=[rs])
    k.ins("vector", lambda e: e.reciprocal(rs[:, :], rs[:, :]), rd=[rs], wr=[rs])
    for j in range(NJ):
        k.ins("vector", lambda e, j=j: e.tensor_scalar_mul(tmp[:], src_tm[:, j, :], rs[:, j:j + 1]),
              rd=[src_tm], srd=[rs], wr=[tmp])
        k.ins("vector", lambda e, j=j: e.tensor_mul(dst_bf[:, j, :], tmp[:], wrep[:]),
              rd=[tmp, wrep], wr=[dst_bf])
    p.rel(mv, rs, tmp, scr12)


def cast_tm(p, dst, src, nj=NJ):
    for j in range(nj):
        p.k.ins("scalar", lambda e, j=j: e.copy(dst[:, j, :], src[:, j, :]), rd=[src], wr=[dst])


def replicate_row(p, dst_sb, row_sb, ncols, consts, ps, label="rep"):
    k = p.k
    for b in range(0, ncols, 512):
        n = min(512, ncols - b)
        k.ins("tensor", lambda e, b=b, n=n: e.matmul(
            ps[:, :n], consts["ones_row32"][:, :], row_sb[:, b:b + n], start=True, stop=True),
            rd=[consts["ones_row32"], row_sb, ps], wr=[ps], label=label)
        k.ins(k.evac_eng(), lambda e, b=b, n=n: ecopy(e, dst_sb[:, b:b + n], ps[:, :n]),
              rd=[ps], wr=[dst_sb], label=label + "e")

def load_rep(p, src, ncols, consts, ps, label):
    row = load_f32(p, src, [1, ncols], label + "r")
    rep = p.get(label, [128, ncols], F32)
    replicate_row(p, rep, row, ncols, consts, ps, label)
    p.rel(row)
    return rep


def setup_consts(p, aps):
    c = {}
    c["ident"] = load_bf(p, aps["c_ident"], [128, 128], "ident")
    c["triu"] = load_bf(p, aps["c_triu01"], [128, 128], "triu")
    c["tril"] = load_bf(p, aps["c_tril01"], [128, 128], "tril")
    c["upstrict"] = load_bf(p, aps["c_upstrict"], [128, 128], "upstrict")
    c["ones_col_bf"] = load_bf(p, aps["c_ones_col"], [128, 1], "onescolb")
    c["zeros_row_bf"] = p.sb("zrow", [1, 1024], BF16)
    p.k.ins("vector", lambda e: e.memset(c["zeros_row_bf"][:, :], 0.0), rd=[], wr=[c["zeros_row_bf"]])
    c["ones_row_bf"] = p.sb("orow", [1, 128], BF16)
    p.k.ins("vector", lambda e: e.memset(c["ones_row_bf"][:, :], 1.0), rd=[], wr=[c["ones_row_bf"]])
    c["ones_row32"] = load_f32(p, aps["c_ones_row"], [1, 128], "onesrow32")
    c["pos"] = load_f32(p, aps["c_pos"], [128, NJ], "pos")
    c["ptr"] = [p.gps("ptr", [128, 128], BF16) for _ in range(2)]
    return c


def causal_attention(p, consts, nheads, dv, getK, getQ, getV, epilogue, scale, pools, label):
    k = p.k
    psc_pool, pctx, pden, probs_pool, dinv = pools
    denr = p.get(label + "dr", [1, 512], F32)
    for h in range(nheads):
        for tb in range(4):
            nkb = 4 * (tb + 1)

            def qk_exp(kb):
                pr = probs_pool[kb % len(probs_pool)]
                psc = psc_pool[kb % len(psc_pool)]
                ksl = getK(h, kb)
                qsl = getQ(h, tb)
                k.ins("tensor", lambda e, ksl=ksl, qsl=qsl, psc=psc: e.matmul(
                    psc[:, :512], ksl, qsl, start=True, stop=True),
                    rd=[ksl, qsl, psc], wr=[psc], label=label + "qk")
                k.ins("scalar", lambda e, pr=pr, psc=psc: e.activation(pr[:, :], psc[:, :512], AF.Exp, scale=scale),
                      rd=[psc], wr=[pr], label=label + "exp")
                for sub in range(4):
                    tq0 = 512 * tb + 128 * sub
                    if 128 * kb > tq0:
                        k.ins("vector", lambda e, pr=pr, sub=sub: e.memset(pr[:, 128 * sub:128 * (sub + 1)], 0.0),
                              rd=[], wr=[pr], label=label + "z")
                    elif 128 * kb == tq0:
                        k.ins("vector", lambda e, pr=pr, sub=sub: e.tensor_mul(
                            pr[:, 128 * sub:128 * (sub + 1)], pr[:, 128 * sub:128 * (sub + 1)],
                            consts["triu"][:, :]), rd=[consts["triu"]], wr=[pr], label=label + "m")

            def den_pv(kb):
                pr = probs_pool[kb % len(probs_pool)]
                k.ins("tensor", lambda e, pr=pr, kb=kb, nkb=nkb: e.matmul(
                    pden[0:1, 0:512], consts["ones_col_bf"][:, :], pr[:, :],
                    start=(kb == 0), stop=(kb == nkb - 1)),
                    rd=[pr, consts["ones_col_bf"], pden], wr=[pden], label=label + "dn")
                vsl = getV(h, kb)
                k.ins("tensor", lambda e, vsl=vsl, pr=pr, kb=kb, nkb=nkb: e.matmul(
                    pctx[:dv, :512], vsl, pr[:, :], start=(kb == 0), stop=(kb == nkb - 1)),
                    rd=[vsl, pr, pctx], wr=[pctx], label=label + "pv")

            qk_exp(0)
            for kb in range(1, nkb):
                qk_exp(kb)
                den_pv(kb - 1)
            den_pv(nkb - 1)
            k.ins("vector", lambda e: e.reciprocal(denr[0:1, :], pden[0:1, 0:512]), rd=[pden], wr=[denr],
                  label=label + "di")
            for sub in range(4):
                k.ins("tensor", lambda e, sub=sub: e.matmul(
                    pden[:, 1:2], denr[0:1, 128 * sub:128 * (sub + 1)], consts["ones_row32"][0:1, 0:1],
                    start=True, stop=True), rd=[denr, consts["ones_row32"], pden], wr=[pden],
                    label=label + "dt")
                k.ins("vector", lambda e, sub=sub: e.tensor_copy(dinv[:, sub:sub + 1], pden[:, 1:2]),
                      rd=[pden], wr=[dinv], label=label + "dc")
            epilogue(h, tb, pctx, dinv)
    p.rel(denr)


def stream_ln_to_fm(p, x_src_dram, xn_fm, consts, xn_spill=None, eps=1e-5, label="sln",
                    grep=None, brep=None, post_cb=None, load_cb=None):
    k = p.k
    xsv = x_src_dram.rearrange("(j q) c -> q j c", q=128) if x_src_dram is not None else None
    spv = xn_spill.rearrange("(j q) c -> q j c", q=128) if xn_spill is not None else None
    xt = [p.get(label + "x", [128, D], F32) for _ in range(2)]
    nt = [p.get(label + "n", [128, D], F32) for _ in range(2)]
    bt = [p.get(label + "b", [128, D], BF16) for _ in range(2)]
    mv = p.get(label + "mv", [128, 1, 2], F32)
    scr12 = p.get(label + "sc", [128, 12], F32)
    rstd = p.get(label + "rs", [128, 1], F32)
    nmean = p.get(label + "nm", [128, 1], F32)
    epsap = p.cval(float(eps))
    for j in range(NJ):
        xv, nv, bv = xt[j % 2], nt[j % 2], bt[j % 2]
        if load_cb is not None:
            load_cb(j, xv, nv)
        else:
            k.ins("sync", lambda e, j=j, xv=xv: e.dma_start(xv[:, :], xsv[:, j, :]),
                  rd=[x_src_dram], wr=[xv], dma=True)
        k.ins("vector", lambda e, xv=xv: e.bn_stats(scr12[:, 0:6], xv[:, 0:512]), rd=[xv], wr=[scr12])
        k.ins("vector", lambda e, xv=xv: e.bn_stats(scr12[:, 6:12], xv[:, 512:1024]), rd=[xv], wr=[scr12])
        k.ins("vector", lambda e: e.bn_aggr(mv[:, 0, :], scr12[:, 0:12]), rd=[scr12], wr=[mv])
        k.ins("scalar", lambda e: e.activation(rstd[:, :], mv[:, 0, 1:2], AF.Sqrt, bias=epsap),
              rd=[mv, epsap], wr=[rstd])
        k.ins("vector", lambda e: e.reciprocal(rstd[:, :], rstd[:, :]), rd=[rstd], wr=[rstd])
        k.ins("vector", lambda e: e.tensor_scalar_mul(nmean[:, :], mv[:, 0, 0:1], -1.0), rd=[mv], wr=[nmean])
        k.ins("vector", lambda e, xv=xv, nv=nv: e.tensor_scalar(
            nv[:, :], xv[:, :], nmean[:, 0:1], rstd[:, 0:1], ALU.add, ALU.mult),
            rd=[xv], srd=[nmean, rstd], wr=[nv])
        if grep is not None:
            k.ins("vector", lambda e, nv=nv: e.tensor_mul(nv[:, :], nv[:, :], grep[:, :]),
                  rd=[grep, nv], wr=[nv])
        if brep is not None:
            k.ins("vector", lambda e, nv=nv: e.tensor_add(nv[:, :], nv[:, :], brep[:, :]),
                  rd=[brep, nv], wr=[nv])
        if spv is not None:
            k.ins("sync", lambda e, j=j, nv=nv: e.dma_start(spv[:, j, :], nv[:, :]),
                  rd=[nv], wr=[xn_spill], dma=True)
        if post_cb is not None:
            post_cb(j, nv)
        k.ins("scalar", lambda e, nv=nv, bv=bv: e.copy(bv[:, :], nv[:, :]), rd=[nv], wr=[bv])
        pairs = [(bv[:, 128 * i:128 * (i + 1)], xn_fm[:, i, 128 * j:128 * (j + 1)], bv, xn_fm)
                 for i in range(NI)]
        transpose_pairs(p, pairs, consts, label)
    p.rel(xt[0], xt[1], nt[0], nt[1], bt[0], bt[1], mv, scr12, rstd, nmean)


def row_to_cols(p, dst_col, row_sb, n, consts, ps, label="r2c"):
    k = p.k
    one11 = consts["ones_row32"][0:1, 0:1]
    nck = (n + 127) // 128
    for ci in range(nck):
        w = min(128, n - 128 * ci)
        k.ins("tensor", lambda e, ci=ci, w=w: e.matmul(
            ps[:w, 0:1], row_sb[0:1, 128 * ci:128 * ci + w], one11, start=True, stop=True),
            rd=[row_sb, consts["ones_row32"], ps], wr=[ps], label=label)
        k.ins(k.evac_eng(), lambda e, ci=ci, w=w: ecopy(e, dst_col[:w, ci:ci + 1], ps[:w, 0:1]),
              rd=[ps], wr=[dst_col], label=label + "e")


def stream_rms_to_fm(p, src_dram, dst_fm, wrep, eps, consts, label="srms"):
    k = p.k
    xsv = src_dram.rearrange("(j q) c -> q j c", q=128)
    xt = [p.get(label + "x", [128, D], F32) for _ in range(2)]
    bt = [p.get(label + "b", [128, D], BF16) for _ in range(2)]
    mv = p.get(label + "mv", [128, 1, 2], F32)
    scr12 = p.get(label + "sc", [128, 12], F32)
    rs = p.get(label + "rs", [128, 1], F32)
    epsap = p.cval(float(eps))
    for j in range(NJ):
        xv, bv = xt[j % 2], bt[j % 2]
        k.ins("sync", lambda e, j=j, xv=xv: e.dma_start(xv[:, :], xsv[:, j, :]),
              rd=[src_dram], wr=[xv], dma=True)
        k.ins("vector", lambda e, xv=xv: e.bn_stats(scr12[:, 0:6], xv[:, 0:512]), rd=[xv], wr=[scr12])
        k.ins("vector", lambda e, xv=xv: e.bn_stats(scr12[:, 6:12], xv[:, 512:1024]), rd=[xv], wr=[scr12])
        k.ins("vector", lambda e: e.bn_aggr(mv[:, 0, :], scr12[:, 0:12]), rd=[scr12], wr=[mv])
        k.ins("vector", lambda e: e.tensor_mul(rs[:, :], mv[:, 0, 0:1], mv[:, 0, 0:1]), rd=[mv], wr=[rs])
        k.ins("vector", lambda e: e.tensor_add(rs[:, :], rs[:, :], mv[:, 0, 1:2]), rd=[mv, rs], wr=[rs])
        k.ins("scalar", lambda e: e.activation(rs[:, :], rs[:, :], AF.Sqrt, bias=epsap), rd=[rs, epsap], wr=[rs])
        k.ins("vector", lambda e: e.reciprocal(rs[:, :], rs[:, :]), rd=[rs], wr=[rs])
        k.ins("vector", lambda e, xv=xv: e.tensor_scalar_mul(xv[:, :], xv[:, :], rs[:, 0:1]),
              rd=[xv], srd=[rs], wr=[xv])
        k.ins("vector", lambda e, xv=xv, bv=bv: e.tensor_mul(bv[:, :], xv[:, :], wrep[:, :]),
              rd=[xv, wrep], wr=[bv])
        pairs = [(bv[:, 128 * i:128 * (i + 1)], dst_fm[:, i, 128 * j:128 * (j + 1)], bv, dst_fm)
                 for i in range(NI)]
        transpose_pairs(p, pairs, consts, label)
    p.rel(xt[0], xt[1], bt[0], bt[1], mv, scr12, rs)



def emit_L0(p, aps, consts, state, dram):
    k = p.k
    x_src = state["x_src"]

    ps = [p.gps("aps", [128, 512], F32) for _ in range(4)]
    pden = p.gps("apd", [128, 512], F32)
    xn_fm = p.get("l0_xnf", [128, NI, T], BF16)
    stream_ln_to_fm(p, x_src, xn_fm, consts, xn_spill=dram["xn"], eps=1e-5, label="l0ln")

    W_uq = load_w_chunks(p, aps["L0_W_uq"], 1024, 128, "wuq")
    W_uk = load_w_chunks(p, aps["L0_W_uk"], 1024, 128, "wuk")
    M1 = p.get("M1", [128, 128], BF16)
    for ck in range(8):
        k.ins("tensor", lambda e, ck=ck: e.matmul(ps[0][:, :128], W_uq[:, ck, :], W_uk[:, ck, :],
                                                  start=(ck == 0), stop=(ck == 7)),
              rd=[W_uq, W_uk, ps[0]], wr=[ps[0]])
    k.ins("scalar", lambda e: e.copy(M1[:, :], ps[0][:, :128]), rd=[ps[0]], wr=[M1])
    p.rel(W_uq, W_uk)
    W_dq = load_bf(p, aps["L0_W_dq"], [128, 1024], "wdq")
    keff = p.get("keff", [128, 8, 128], BF16)
    for ci in range(8):
        k.ins("tensor", lambda e, ci=ci: e.matmul(ps[1][:, :128], W_dq[:, 128 * ci:128 * (ci + 1)],
                                                  M1[:, :], start=True, stop=True),
              rd=[W_dq, M1, ps[1]], wr=[ps[1]])
        k.ins(k.evac_eng(), lambda e, ci=ci: ecopy(e, keff[:, ci, :], ps[1][:, :128]),
              rd=[ps[1]], wr=[keff])
    p.rel(W_dq, M1)
    W_uv = load_w_chunks(p, aps["L0_W_uv"], 1024, 128, "wuv")
    W_oT = load_w_chunks(p, aps["L0_W_oT"], 1024, 1024, "woT")
    Vf = p.get("Vf", [128, 1024], BF16)
    for nb in range(2):
        for ck in range(8):
            k.ins("tensor", lambda e, nb=nb, ck=ck: e.matmul(
                ps[0][:, :512], W_uv[:, ck, :], W_oT[:, ck, 512 * nb:512 * (nb + 1)],
                start=(ck == 0), stop=(ck == 7)), rd=[W_uv, W_oT, ps[0]], wr=[ps[0]])
        k.ins(k.evac_eng(), lambda e, nb=nb: ecopy(e, Vf[:, 512 * nb:512 * (nb + 1)], ps[0][:, :512]),
              rd=[ps[0]], wr=[Vf])
    p.rel(W_uv, W_oT)
    W_dqT = load_w_chunks(p, aps["L0_W_dqT"], 1024, 128, "wdqT")
    W_uqT = load_bf(p, aps["L0_W_uqT"], [128, 1024], "wuqT")
    W_dkvT = load_w_chunks(p, aps["L0_W_dkvT"], 1024, 128, "wdkvT")
    ckv_fm = p.get("ckvf", [128, T], BF16)
    qlat_fm = p.get("qlatf", [128, T], BF16)
    for tb in range(4):
        for (dst, wch) in ((ckv_fm, W_dkvT), (qlat_fm, W_dqT)):
            for ck in range(8):
                k.ins("tensor", lambda e, tb=tb, wch=wch, ck=ck: e.matmul(
                    ps[0][:, :512], wch[:, ck, :], xn_fm[:, ck, 512 * tb:512 * (tb + 1)],
                    start=(ck == 0), stop=(ck == 7)), rd=[wch, xn_fm, ps[0]], wr=[ps[0]])
            k.ins(k.evac_eng(), lambda e, dst=dst, tb=tb: ecopy(e,
                dst[:, 512 * tb:512 * (tb + 1)], ps[0][:, :512]), rd=[ps[0]], wr=[dst])
    ckv_tm = p.get("ckvt", [128, NJ, 128], BF16)
    transpose_pairs(p, [(ckv_fm[:, 128 * j:128 * (j + 1)], ckv_tm[:, j, :], ckv_fm, ckv_tm)
                        for j in range(NJ)], consts, "ckvT")
    q_fm = p.get("l0qf", [128, 8, T], BF16)
    for ci in range(8):
        for tb in range(4):
            k.ins("tensor", lambda e, ci=ci, tb=tb: e.matmul(
                ps[0][:, :512], W_uqT[:, 128 * ci:128 * (ci + 1)],
                qlat_fm[:, 512 * tb:512 * (tb + 1)], start=True, stop=True),
                rd=[W_uqT, qlat_fm, ps[0]], wr=[ps[0]])
            k.ins(k.evac_eng(), lambda e, ci=ci, tb=tb: ecopy(e,
                q_fm[:, ci, 512 * tb:512 * (tb + 1)], ps[0][:, :512]), rd=[ps[0]], wr=[q_fm])
    p.rel(qlat_fm, W_dqT, W_dkvT, W_uqT)

    ql_h = p.get("l0qlh", [128, T], BF16)
    probs_pool = [p.get("apr", [128, 512], BF16) for _ in range(4)]
    ctx_bf = p.get("l0ctx", [128, 512], BF16)
    dinv = p.get("l0dinv", [128, 4], F32)
    py = p.gps("l0py", [128, 64], F32)
    s_stripe = [p.get("sstA", [128, 4, 64], F32), p.get("sstB", [128, 4, 64], F32)]
    s_dview = dram["s"].rearrange("(j q) c -> q j c", q=128)
    cur_h = [None]

    def mla_getK(h, kb):
        if cur_h[0] != h:
            ci, po = h // 2, (h % 2) * 64
            for tb in range(4):
                k.ins("tensor", lambda e, tb=tb, ci=ci, po=po: e.matmul(
                    ps[2][:, :512], keff[po:po + 64, ci, :], q_fm[po:po + 64, ci, 512 * tb:512 * (tb + 1)],
                    start=True, stop=True), rd=[keff, q_fm, ps[2]], wr=[ps[2]])
                k.ins(k.evac_eng(), lambda e, tb=tb: ecopy(e, ql_h[:, 512 * tb:512 * (tb + 1)], ps[2][:, :512]),
                      rd=[ps[2]], wr=[ql_h])
            cur_h[0] = h
        return ckv_fm[:, 128 * kb:128 * (kb + 1)]

    def mla_getQ(h, tb):
        return ql_h[:, 512 * tb:512 * (tb + 1)]

    def mla_getV(h, kb):
        return ckv_tm[:, kb, :]

    def mla_epi(h, tb, pctx, dinv_):
        sstb = s_stripe[tb % 2]
        k.ins(k.evac_eng(), lambda e: ecopy(e, ctx_bf[:, :], pctx[:, :512]), rd=[pctx], wr=[ctx_bf])
        for sub in range(4):
            k.ins("tensor", lambda e, sub=sub, h=h: e.matmul(
                py[:, :64], ctx_bf[:, 128 * sub:128 * (sub + 1)], Vf[:, 64 * h:64 * (h + 1)],
                start=True, stop=True), rd=[ctx_bf, Vf, py], wr=[py])
            k.ins("vector", lambda e, sub=sub, sstb=sstb: e.tensor_scalar_mul(
                sstb[:, sub, :], py[:, :64], dinv_[:, sub:sub + 1]),
                rd=[py], srd=[dinv_], wr=[sstb])
        k.ins("sync", lambda e, h=h, tb=tb, sstb=sstb: e.dma_start(
            s_dview[:, 4 * tb:4 * (tb + 1), 64 * h:64 * (h + 1)], sstb[:, :, :]),
            rd=[sstb], wr=[dram["s"]], dma=True)

    causal_attention(p, consts, 16, 128, mla_getK, mla_getQ, mla_getV, mla_epi, 0.125,
                     ([ps[0], ps[3]], ps[1], pden, probs_pool, dinv), "mla")
    p.rel(xn_fm, keff, Vf, ckv_fm, ckv_tm, q_fm, ql_h,
          *probs_pool, ctx_bf, dinv, py, s_stripe[0], s_stripe[1],
          ps[0], ps[1], ps[2], ps[3], pden)

    ps = [p.gps("fps", [128, 512], F32) for _ in range(4)]
    pden = p.gps("fpd", [128, 512], F32)
    ln1r = load_rep(p, aps["L0_ln1_w"], D, consts, ps[3], "ln1")
    h_fm = p.get("l0hf", [128, NI, T], BF16)
    stream_rms_to_fm(p, dram["xn"], h_fm, ln1r, 1e-6, consts, "rms1")
    p.rel(ln1r)

    q_tm2 = [p.get("l0qA", [128, NJ, 512], F32), p.get("l0qB", [128, NJ, 512], F32)]
    kv_tm = p.get("l0kv", [128, NJ, 512], F32)
    qnr = load_rep(p, aps["L0_qn_w"], 64, consts, ps[3], "qn")
    knr = load_rep(p, aps["L0_kn_w"], 64, consts, ps[3], "kn")
    Wq = load_w_chunks(p, aps["L0_Wq"], D, D, "Wq")
    Wk = load_w_chunks(p, aps["L0_Wk"], D, 256, "Wk")
    Wv = load_w_chunks(p, aps["L0_Wv"], D, 256, "Wv")
    bq_row = load_f32(p, aps["L0_bq"], [1, D], "bq")
    bk_row = load_f32(p, aps["L0_bk"], [1, 256], "bk")
    bv_row = load_f32(p, aps["L0_bv"], [1, 256], "bv")
    for j in range(NJ):
        for nb in range(2):
            for ck in range(8):
                k.ins("tensor", lambda e, j=j, nb=nb, ck=ck: e.matmul(
                    ps[0][:, :512], h_fm[:, ck, 128 * j:128 * (j + 1)], Wq[:, ck, 512 * nb:512 * (nb + 1)],
                    start=(ck == 0), stop=False), rd=[h_fm, Wq, ps[0]], wr=[ps[0]])
            k.ins("tensor", lambda e, nb=nb: e.matmul(
                ps[0][:, :512], consts["ones_row32"][:, :], bq_row[:, 512 * nb:512 * (nb + 1)],
                start=False, stop=True), rd=[consts["ones_row32"], bq_row, ps[0]], wr=[ps[0]])
            k.ins("vector", lambda e, j=j, nb=nb: e.tensor_copy(
                q_tm2[nb][:, j, :], ps[0][:, :512]), rd=[ps[0]], wr=[q_tm2[nb]])
        for (wch, brow, off, pp) in ((Wk, bk_row, 0, ps[1]), (Wv, bv_row, 256, ps[2])):
            for ck in range(8):
                k.ins("tensor", lambda e, j=j, wch=wch, ck=ck, pp=pp: e.matmul(
                    pp[:, :256], h_fm[:, ck, 128 * j:128 * (j + 1)], wch[:, ck, :],
                    start=(ck == 0), stop=False), rd=[h_fm, wch, pp], wr=[pp])
            k.ins("tensor", lambda e, brow=brow, pp=pp: e.matmul(
                pp[:, :256], consts["ones_row32"][:, :], brow[:, :],
                start=False, stop=True), rd=[consts["ones_row32"], brow, pp], wr=[pp])
            k.ins("vector", lambda e, j=j, off=off, pp=pp: e.tensor_copy(
                kv_tm[:, j, off:off + 256], pp[:, :256]), rd=[pp], wr=[kv_tm])
    p.rel(h_fm, Wq, Wk, Wv, bq_row, bk_row, bv_row)

    ropeC_t = [p.get("rC", [128, D], F32)]
    ropeS_t = [p.get("rS", [128, D], F32)]
    rCview = aps["c_ropeC"].rearrange("(j q) w -> j q w", q=128)
    rSview = aps["c_ropeS"].rearrange("(j q) w -> j q w", q=128)
    scr = p.get("l0scr", [128, 2 * D], F32)
    sqv = p.get("l0sq", [128, NJ, 16], F32)
    skv = p.get("l0sk", [128, NJ, 4], F32)
    qsq = scr[:, D:2 * D]

    def rope2_tab(buf, j, width, Cs, Ss):
        k.ins("vector", lambda e, buf=buf, j=j, width=width, Cs=Cs: e.tensor_mul(
            scr[:, 0:width], buf[:, j, 0:width], Cs), rd=[buf, ropeC_t[0]], wr=[scr])
        nh = width // 64
        bv = buf[:, j, 0:width].rearrange("q (h s d) -> q h s d", h=nh, s=2)
        sv = Ss.rearrange("q (h s d) -> q h s d", h=nh, s=2)
        ov = scr[:, D:D + width].rearrange("q (h s d) -> q h s d", h=nh, s=2)
        k.ins("vector", lambda e, bv=bv, sv=sv, ov=ov: e.tensor_tensor(
            ov[:, :, 0, :], bv[:, :, 1, :], sv[:, :, 0, :], ALU.mult), rd=[buf, ropeS_t[0]], wr=[scr])
        k.ins("vector", lambda e, bv=bv, sv=sv, ov=ov: e.tensor_tensor(
            ov[:, :, 1, :], bv[:, :, 0, :], sv[:, :, 1, :], ALU.mult), rd=[buf, ropeS_t[0]], wr=[scr])
        k.ins("vector", lambda e, buf=buf, j=j, width=width: e.tensor_add(
            buf[:, j, 0:width], scr[:, 0:width], scr[:, D:D + width]), rd=[scr], wr=[buf])

    def rope2(buf, j, width, Ct, St):
        rope2_tab(buf, j, width, Ct[:, 0:width], St[:, 0:width])

    def postnorm(s_):
        k.ins("vector", lambda e, s_=s_: e.tensor_scalar(s_[:, :, :], s_[:, :, :], 1.0 / 64.0, 1e-6, ALU.mult, ALU.add),
              rd=[s_], wr=[s_])
        k.ins("scalar", lambda e, s_=s_: e.activation(s_[:, :, :], s_[:, :, :], AF.Sqrt), rd=[s_], wr=[s_])
        k.ins("vector", lambda e, s_=s_: e.reciprocal(s_[:, :, :], s_[:, :, :]), rd=[s_], wr=[s_])

    Ct, St = ropeC_t[0], ropeS_t[0]
    k_bf = p.get("l0kb", [128, NJ, 256], BF16)
    v_bf = p.get("l0vb", [128, NJ, 256], BF16)
    for j in range(NJ):
        k.ins("sync", lambda e, j=j, Ct=Ct: e.dma_start(Ct[:, 0:256], rCview[j, :, 0:256]),
              rd=[rCview], wr=[Ct], dma=True)
        k.ins("sync", lambda e, j=j, St=St: e.dma_start(St[:, 0:256], rSview[j, :, 0:256]),
              rd=[rSview], wr=[St], dma=True)
        rope2(kv_tm, j, 256, Ct, St)
        k.ins("scalar", lambda e, j=j: e.activation(qsq[:, :256], kv_tm[:, j, 0:256], AF.Square),
              rd=[kv_tm], wr=[scr])
        k.ins("vector", lambda e, j=j: e.reduce_sum(skv[:, j, :], qsq[:, :256].rearrange("q (g d) -> q g d", g=4), AX.X),
              rd=[scr], wr=[skv])
    postnorm(skv)
    for j in range(NJ):
        for hh in range(4):
            k.ins("vector", lambda e, j=j, hh=hh: e.tensor_scalar_mul(
                scr[:, 0:64], kv_tm[:, j, 64 * hh:64 * (hh + 1)], skv[:, j, hh:hh + 1]),
                rd=[kv_tm], srd=[skv], wr=[scr])
            k.ins("vector", lambda e, j=j, hh=hh: e.tensor_mul(
                k_bf[:, j, 64 * hh:64 * (hh + 1)], scr[:, 0:64], knr[:, :]), rd=[scr, knr], wr=[k_bf])
        k.ins("scalar", lambda e, j=j: e.copy(v_bf[:, j, :], kv_tm[:, j, 256:512]), rd=[kv_tm], wr=[v_bf])
    p.rel(kv_tm, knr, skv)

    q_bf = p.get("l0qb", [128, NJ, D], BF16)
    for j in range(NJ):
        k.ins("sync", lambda e, j=j, Ct=Ct: e.dma_start(Ct[:, :], rCview[j]), rd=[rCview], wr=[Ct], dma=True)
        k.ins("sync", lambda e, j=j, St=St: e.dma_start(St[:, :], rSview[j]), rd=[rSview], wr=[St], dma=True)
        for half in range(2):
            qt = q_tm2[half]
            rope2_tab(qt, j, 512, Ct[:, 512 * half:512 * (half + 1)], St[:, 512 * half:512 * (half + 1)])
            k.ins("scalar", lambda e, j=j, qt=qt: e.activation(qsq[:, 0:512], qt[:, j, :], AF.Square),
                  rd=[qt], wr=[scr])
            k.ins("vector", lambda e, j=j, half=half: e.reduce_sum(
                sqv[:, j, 8 * half:8 * (half + 1)], qsq[:, 0:512].rearrange("q (g d) -> q g d", g=8), AX.X),
                rd=[scr], wr=[sqv])
    postnorm(sqv)
    for j in range(NJ):
        for hh in range(16):
            qt = q_tm2[hh // 8]
            co = 64 * (hh % 8)
            k.ins("vector", lambda e, j=j, hh=hh, qt=qt, co=co: e.tensor_scalar_mul(
                scr[:, 0:64], qt[:, j, co:co + 64], sqv[:, j, hh:hh + 1]),
                rd=[qt], srd=[sqv], wr=[scr])
            k.ins("vector", lambda e, j=j, hh=hh: e.tensor_mul(
                q_bf[:, j, 64 * hh:64 * (hh + 1)], scr[:, 0:64], qnr[:, :]), rd=[scr, qnr], wr=[q_bf])
    p.rel(q_tm2[0], q_tm2[1], qnr, ropeC_t[0], ropeS_t[0], scr, sqv)

    Wg = load_bf(p, aps["L0_Wg"], [64, 64], "Wg0")
    bgr = load_rep(p, aps["L0_bg"], 64, consts, ps[3], "bg")
    q_fm2 = p.get("l0qf2", [128, NI, T], BF16)
    tm_to_fm(p, q_fm2, q_bf, consts)
    p.rel(q_bf)
    k_fm2 = p.get("l0kf2", [128, 2, T], BF16)
    transpose_pairs(p, [(k_bf[:, j, 128 * i:128 * (i + 1)], k_fm2[:, i, 128 * j:128 * (j + 1)], k_bf, k_fm2)
                        for j in range(NJ) for i in range(2)], consts, "kT")
    p.rel(k_bf)
    gat_tm = p.get("l0gat", [128, NJ, D], BF16)
    probs_pool = [p.get("fpr", [128, 512], BF16) for _ in range(4)]
    ctx_bf2 = p.get("fctx", [128, 512], BF16)
    dinv2 = p.get("fdi", [128, 4], F32)
    pg = p.gps("fpg", [128, 64], F32)
    gt = p.get("fgt", [128, 64], F32)
    ct = p.get("fct", [128, 64], F32)

    qsc = p.get("l0qsc", [128, 512], BF16)
    qsc_cur = [None]

    def gqa_getK(h, kb):
        kvh = h // 4
        ki, ko = kvh // 2, (kvh % 2) * 64
        return k_fm2[ko:ko + 64, ki, 128 * kb:128 * (kb + 1)]

    def gqa_getQ(h, tb):
        qi, qo = h // 2, (h % 2) * 64
        ko = ((h // 4) % 2) * 64
        if qo == ko:
            return q_fm2[qo:qo + 64, qi, 512 * tb:512 * (tb + 1)]
        if qsc_cur[0] != (h, tb):
            k.ins("vector", lambda e, qi=qi, qo=qo, ko=ko, tb=tb: e.tensor_copy(
                qsc[ko:ko + 64, :], q_fm2[qo:qo + 64, qi, 512 * tb:512 * (tb + 1)]),
                rd=[q_fm2], wr=[qsc])
            qsc_cur[0] = (h, tb)
        return qsc[ko:ko + 64, :]

    def gqa_getV(h, kb):
        kvh = h // 4
        return v_bf[:, kb, 64 * kvh:64 * (kvh + 1)]

    def gqa_epi(h, tb, pctx, dinv_):
        k.ins(k.evac_eng(), lambda e: ecopy(e, ctx_bf2[:64, :], pctx[:64, :512]), rd=[pctx], wr=[ctx_bf2])
        for sub in range(4):
            j = 4 * tb + sub
            k.ins("tensor", lambda e, sub=sub: e.matmul(
                pg[:, :64], ctx_bf2[:64, 128 * sub:128 * (sub + 1)], Wg[:, :], start=True, stop=True),
                rd=[ctx_bf2, Wg, pg], wr=[pg])
            k.ins("vector", lambda e, sub=sub: e.tensor_scalar_mul(gt[:, :], pg[:, :64], dinv_[:, sub:sub + 1]),
                  rd=[pg], srd=[dinv_], wr=[gt])
            k.ins("vector", lambda e: e.tensor_add(gt[:, :], gt[:, :], bgr[:, :]), rd=[bgr, gt], wr=[gt])
            k.ins("scalar", lambda e: e.activation(gt[:, :], gt[:, :], AF.Sigmoid), rd=[gt], wr=[gt])
            k.ins("tensor", lambda e, sub=sub: e.transpose(
                consts["ptr"][0][:, :64], ctx_bf2[:64, 128 * sub:128 * (sub + 1)], consts["ident"][:64, :64]),
                rd=[ctx_bf2, consts["ident"], consts["ptr"][0]], wr=[consts["ptr"][0]])
            k.ins("vector", lambda e, sub=sub: e.tensor_scalar_mul(
                ct[:, :], consts["ptr"][0][:, :64], dinv_[:, sub:sub + 1]),
                rd=[consts["ptr"][0]], srd=[dinv_], wr=[ct])
            k.ins("vector", lambda e, j=j, h=h: e.tensor_mul(
                gat_tm[:, j, 64 * h:64 * (h + 1)], ct[:, :], gt[:, :]), rd=[ct, gt], wr=[gat_tm])

    causal_attention(p, consts, 16, 64, gqa_getK, gqa_getQ, gqa_getV, gqa_epi, 0.125,
                     ([ps[0], ps[3]], ps[1], pden, probs_pool, dinv2), "gqa")
    p.rel(q_fm2, k_fm2, v_bf, *probs_pool, ctx_bf2, dinv2, pg, gt, ct, Wg, bgr, qsc)

    gat_fm = p.get("l0gatf", [128, NI, T], BF16)
    tm_to_fm(p, gat_fm, gat_tm, consts)
    p.rel(gat_tm)
    Wo = load_w_chunks(p, aps["L0_Wo"], D, D, "Wo0")
    bo_row = load_f32(p, aps["L0_bo"], [1, D], "bo")
    ln2r = load_rep(p, aps["L0_ln2_w"], D, consts, ps[3], "ln2")
    xnt = [p.get("xnt", [128, D], F32) for _ in range(2)]
    x1t = [p.get("x1t", [128, D], F32) for _ in range(2)]
    h2b = [p.get("h2b", [128, D], BF16) for _ in range(2)]
    mv1 = p.get("h2mv", [128, 1, 2], F32)
    sc12 = p.get("h2sc", [128, 12], F32)
    rs1 = p.get("h2rs", [128, 1], F32)
    eps6 = p.cval(1e-6)
    h2_fm = p.get("l0h2f", [128, NI, T], BF16)
    onehot = p.get("l0oh", [128, NJ, 8], F32)
    ohT = p.get("l0ohT", [8, T], BF16)
    xn_dview = dram["xn"].rearrange("(j q) c -> q j c", q=128)
    acc_dview = dram["acc"].rearrange("(j q) c -> q j c", q=128)
    for j in range(NJ):
        xv, x1v, bv = xnt[j % 2], x1t[j % 2], h2b[j % 2]
        k.ins("sync", lambda e, j=j, xv=xv: e.dma_start(xv[:, :], xn_dview[:, j, :]),
              rd=[dram["xn"]], wr=[xv], dma=True)
        for nb in range(2):
            for ck in range(8):
                k.ins("tensor", lambda e, j=j, nb=nb, ck=ck: e.matmul(
                    ps[nb][:, :512], gat_fm[:, ck, 128 * j:128 * (j + 1)], Wo[:, ck, 512 * nb:512 * (nb + 1)],
                    start=(ck == 0), stop=False), rd=[gat_fm, Wo, ps[nb]], wr=[ps[nb]])
            k.ins("tensor", lambda e, j=j, nb=nb: e.matmul(
                ps[nb][:, :512], consts["ones_row32"][:, :], bo_row[:, 512 * nb:512 * (nb + 1)],
                start=False, stop=True), rd=[consts["ones_row32"], bo_row, ps[nb]], wr=[ps[nb]])
            k.ins("vector", lambda e, j=j, nb=nb, xv=xv, x1v=x1v: e.tensor_add(
                x1v[:, 512 * nb:512 * (nb + 1)], ps[nb][:, :512], xv[:, 512 * nb:512 * (nb + 1)]),
                rd=[ps[nb], xv], wr=[x1v])
        k.ins("sync", lambda e, j=j, x1v=x1v: e.dma_start(acc_dview[:, j, :], x1v[:, :]),
              rd=[x1v], wr=[dram["acc"]], dma=True)
        k.ins("vector", lambda e, x1v=x1v: e.bn_stats(sc12[:, 0:6], x1v[:, 0:512]), rd=[x1v], wr=[sc12])
        k.ins("vector", lambda e, x1v=x1v: e.bn_stats(sc12[:, 6:12], x1v[:, 512:1024]), rd=[x1v], wr=[sc12])
        k.ins("vector", lambda e: e.bn_aggr(mv1[:, 0, :], sc12[:, 0:12]), rd=[sc12], wr=[mv1])
        k.ins("vector", lambda e: e.tensor_mul(rs1[:, :], mv1[:, 0, 0:1], mv1[:, 0, 0:1]), rd=[mv1], wr=[rs1])
        k.ins("vector", lambda e: e.tensor_add(rs1[:, :], rs1[:, :], mv1[:, 0, 1:2]), rd=[mv1, rs1], wr=[rs1])
        k.ins("scalar", lambda e: e.activation(rs1[:, :], rs1[:, :], AF.Sqrt, bias=eps6), rd=[rs1, eps6], wr=[rs1])
        k.ins("vector", lambda e: e.reciprocal(rs1[:, :], rs1[:, :]), rd=[rs1], wr=[rs1])
        k.ins("vector", lambda e, x1v=x1v: e.tensor_scalar_mul(x1v[:, :], x1v[:, :], rs1[:, 0:1]),
              rd=[x1v], srd=[rs1], wr=[x1v])
        k.ins("vector", lambda e, x1v=x1v, bv=bv: e.tensor_mul(bv[:, :], x1v[:, :], ln2r[:, :]),
              rd=[x1v, ln2r], wr=[bv])
        transpose_pairs(p, [(bv[:, 128 * i:128 * (i + 1)], h2_fm[:, i, 128 * j:128 * (j + 1)], bv, h2_fm)
                            for i in range(NI)], consts, "h2t")
    p.rel(gat_fm, Wo, bo_row, ln2r, xnt[0], xnt[1], x1t[0], x1t[1], h2b[0], h2b[1], mv1, sc12, rs1)

    Wr = load_bf(p, aps["L0_Wr"], [128, 8, 8], "Wr", ("(kc kp) n -> kp kc n", dict(kp=128)))
    brr = load_rep(p, aps["L0_br"], 8, consts, ps[3], "br")
    logits = p.get("l0lg", [128, NJ, 8], F32)
    mxr = p.get("l0mx", [128, NJ], F32)
    oh_bf = p.get("l0ohb", [128, NJ, 8], BF16)
    for j in range(NJ):
        for ck in range(8):
            k.ins("tensor", lambda e, j=j, ck=ck: e.matmul(
                ps[1][:, :8], h2_fm[:, ck, 128 * j:128 * (j + 1)], Wr[:, ck, :],
                start=(ck == 0), stop=(ck == 7)), rd=[h2_fm, Wr, ps[1]], wr=[ps[1]])
        k.ins("vector", lambda e, j=j: e.tensor_add(logits[:, j, :], ps[1][:, :8], brr[:, :8]),
              rd=[ps[1], brr], wr=[logits])
    for j in range(NJ):
        k.ins("vector", lambda e, j=j: e.reduce_max(mxr[:, j:j + 1], logits[:, j, :], AX.X),
              rd=[logits], wr=[mxr])
        k.ins("vector", lambda e, j=j: e.tensor_scalar(
            onehot[:, j, :], logits[:, j, :], mxr[:, j:j + 1], None, ALU.is_equal),
            rd=[logits], srd=[mxr], wr=[onehot])
        k.ins("scalar", lambda e, j=j: e.copy(oh_bf[:, j, :], onehot[:, j, :]), rd=[onehot], wr=[oh_bf])
    transpose_pairs(p, [(oh_bf[:, j, :], ohT[:, 128 * j:128 * (j + 1)], oh_bf, ohT)
                        for j in range(NJ)], consts, "ohT")
    p.rel(Wr, brr, logits, mxr, oh_bf)

    W1a = p.get("m0w1a", [128, 4, 4096], BF16)
    W1b = p.get("m0w1b", [128, 4, 4096], BF16)
    W1ch = [(W1a, 0), (W1a, 1), (W1a, 2), (W1a, 3), (W1b, 0), (W1b, 1), (W1b, 2), (W1b, 3)]
    W2 = p.get("m0w2", [128, 16, 1024], BF16)
    u1 = p.get("m0u1", [128, 16, 512], BF16)
    act = u1
    yscr = [p.get("m0ysA", [128, D], F32), p.get("m0ysB", [128, D], F32)]
    W1ap = aps["L0_W1"]
    W2ap = aps["L0_W2"]

    def load_W1(e_):
        s = W1ap[e_].rearrange("(kc kp) n -> kc kp n", kp=128)
        for ck in range(8):
            t, sub = W1ch[ck]
            p.k.ins("gpsimd", lambda e, s=s, ck=ck, t=t, sub=sub: e.dma_start(t[:, sub, :], s[ck]),
                    rd=[s], wr=[t], dma=True, label=f"w1l{e_}_{ck}")

    def load_W2(e_):
        load_w_chunks(p, W2ap[e_], 2048, 1024, f"w2l{e_}", into=W2)

    load_W1(0)
    load_W2(0)
    for e_ in range(8):
        for tb in range(4):
            t0 = 512 * tb
            for fi in range(16):
                pA, pB = ps[fi % 2], ps[2 + fi % 2]
                for ck in range(8):
                    t, sub = W1ch[ck]
                    k.ins("tensor", lambda e, fi=fi, ck=ck, t0=t0, pA=pA, t_=t, sub=sub: e.matmul(
                        pA[:, :512], t_[:, sub, 128 * fi:128 * (fi + 1)], h2_fm[:, ck, t0:t0 + 512],
                        start=(ck == 0), stop=(ck == 7)), rd=[t, h2_fm, pA], wr=[pA])
                k.ins("scalar", lambda e, fi=fi, pA=pA: e.copy(u1[:, fi, :], pA[:, :512]), rd=[pA], wr=[u1])
                for ck in range(8):
                    t, sub = W1ch[ck]
                    k.ins("tensor", lambda e, fi=fi, ck=ck, t0=t0, pB=pB, t_=t, sub=sub: e.matmul(
                        pB[:, :512], t_[:, sub, 2048 + 128 * fi:2048 + 128 * (fi + 1)],
                        h2_fm[:, ck, t0:t0 + 512],
                        start=(ck == 0), stop=(ck == 7)), rd=[t, h2_fm, pB], wr=[pB])
                k.ins("scalar", lambda e, fi=fi, pB=pB: e.activation(pB[:, :512], pB[:, :512], AF.Sigmoid),
                      rd=[pB], wr=[pB])
                k.ins("vector", lambda e, fi=fi, pB=pB: e.tensor_mul(u1[:, fi, :], u1[:, fi, :], pB[:, :512]),
                      rd=[u1, pB], wr=[u1])
            if tb == 3 and e_ < 7:
                load_W1(e_ + 1)
            for sub in range(4):
                j = 4 * tb + sub
                ysb = yscr[sub % 2]
                for nb in range(2):
                    pY = ps[nb]
                    for fi in range(16):
                        k.ins("tensor", lambda e, fi=fi, sub=sub, nb=nb, pY=pY: e.matmul(
                            pY[:, :512], act[:, fi, 128 * sub:128 * (sub + 1)],
                            W2[:, fi, 512 * nb:512 * (nb + 1)],
                            start=(fi == 0), stop=(fi == 15)), rd=[act, W2, pY], wr=[pY])
                    k.ins("vector", lambda e, j=j, nb=nb, e_=e_, pY=pY, ysb=ysb: e.tensor_scalar_mul(
                        ysb[:, 512 * nb:512 * (nb + 1)], pY[:, :512], onehot[:, j, e_:e_ + 1]),
                        rd=[pY], srd=[onehot], wr=[ysb])
                k.ins("sync", lambda e, j=j, e_=e_, ysb=ysb: e.dma_start(
                    dram["y"][e_].rearrange("(j q) c -> q j c", q=128)[:, j, :], ysb[:, :]),
                    rd=[ysb], wr=[dram["y"][e_]], dma=True)
        if e_ < 7:
            load_W2(e_ + 1)
    p.rel(W2, u1, yscr[0], yscr[1], h2_fm, onehot, W1a, W1b)

    b2sb = load_bf(p, aps["L0_b2"], [8, 1024], "b2")
    ytiles = [p.get(f"c0y{i}", [128, D], F32) for i in range(2)]
    accA = [p.get("c0aA" + str(i), [128, D], F32) for i in range(2)]
    sT2 = [p.get("c0s0", [128, D], F32)]
    x_dst = dram["x2"]
    xdv = x_dst.rearrange("(j q) c -> q j c", q=128)
    lst = [dram["acc"]] + dram["y"]
    ha, hb = lst[0::2], lst[1::2]

    def l0cb(j, xnv, nv=None):
        accT, acc2, sT = accA[j % 2], nv, sT2[0]
        for half, (accX, hl) in enumerate(((accT, ha), (acc2, hb))):
            k.ins("sync", lambda e, j=j, accX=accX, sd=hl[0]: e.dma_start(
                accX[:, :], sd.rearrange("(j q) c -> q j c", q=128)[:, j, :]),
                rd=[hl[0]], wr=[accX], dma=True)
            for i, sd in enumerate(hl[1:]):
                yt = ytiles[half]
                k.ins("sync", lambda e, j=j, sd=sd, yt=yt: e.dma_start(
                    yt[:, :], sd.rearrange("(j q) c -> q j c", q=128)[:, j, :]),
                    rd=[sd], wr=[yt], dma=True)
                k.ins("vector", lambda e, yt=yt, accX=accX: e.tensor_add(
                    accX[:, :], accX[:, :], yt[:, :]), rd=[yt, accX], wr=[accX])
        k.ins("vector", lambda e, accT=accT, acc2=acc2: e.tensor_add(
            accT[:, :], accT[:, :], acc2[:, :]), rd=[accT, acc2], wr=[accT])
        for nb in range(2):
            k.ins("tensor", lambda e, j=j, nb=nb: e.matmul(
                ps[0][:, :512], ohT[:, 128 * j:128 * (j + 1)], b2sb[:, 512 * nb:512 * (nb + 1)],
                start=True, stop=True), rd=[ohT, b2sb, ps[0]], wr=[ps[0]])
            k.ins("vector", lambda e, nb=nb, accT=accT: e.tensor_add(
                accT[:, 512 * nb:512 * (nb + 1)], accT[:, 512 * nb:512 * (nb + 1)], ps[0][:, :512]),
                rd=[ps[0], accT], wr=[accT])
        k.ins("sync", lambda e, j=j, sT=sT: e.dma_start(
            sT[:, :], dram["s"].rearrange("(j q) c -> q j c", q=128)[:, j, :]),
            rd=[dram["s"]], wr=[sT], dma=True)
        k.ins("sync", lambda e, j=j, xnv=xnv: e.dma_start(
            xnv[:, :], x_src.rearrange("(j q) c -> q j c", q=128)[:, j, :]),
            rd=[x_src], wr=[xnv], dma=True)
        k.ins("scalar", lambda e, accT=accT: e.activation(accT[:, :], accT[:, :], AF.Silu),
              rd=[accT], wr=[accT])
        k.ins("vector", lambda e, sT=sT, accT=accT: e.tensor_mul(accT[:, :], accT[:, :], sT[:, :]),
              rd=[sT, accT], wr=[accT])
        k.ins("vector", lambda e, xnv=xnv, accT=accT: e.tensor_add(xnv[:, :], xnv[:, :], accT[:, :]),
              rd=[accT, xnv], wr=[xnv])
        k.ins("sync", lambda e, j=j, xnv=xnv: e.dma_start(xdv[:, j, :], xnv[:, :]),
              rd=[xnv], wr=[x_dst], dma=True)

    p.rel(ps[1], ps[2], ps[3], pden)
    state["x_src"] = x_dst
    state["combine_cb"] = l0cb
    state["combine_rel"] = lambda: p.rel(
        b2sb, ohT, ps[0], ytiles[0], ytiles[1], accA[0], accA[1], sT2[0])


def glu_moe_expert(p, aps, consts, xn_fm, Wa, Wb, W2, scale_cb, dst_dram, ps, label, prefetch_cb=None):
    k = p.k
    sbA = p.get(label + "A", [128, 512], F32)
    act = p.get(label + "act", [128, 16, 512], BF16)
    ysb = [p.get(label + "y0", [128, D], F32), p.get(label + "y1", [128, D], F32)]
    dv = dst_dram.rearrange("(j q) c -> q j c", q=128)
    for tb in range(4):
        t0 = 512 * tb
        for fi in range(16):
            pA, pB = ps[fi % 2], ps[2 + fi % 2]
            for ck in range(8):
                k.ins("tensor", lambda e, fi=fi, ck=ck, t0=t0, pA=pA: e.matmul(
                    pA[:, :512], Wa[:, ck, 128 * fi:128 * (fi + 1)], xn_fm[:, ck, t0:t0 + 512],
                    start=(ck == 0), stop=(ck == 7)), rd=[Wa, xn_fm, pA], wr=[pA])
            k.ins("scalar", lambda e, pA=pA: e.activation(sbA[:, :], pA[:, :512], AF.Silu),
                  rd=[pA], wr=[sbA])
            for ck in range(8):
                k.ins("tensor", lambda e, fi=fi, ck=ck, t0=t0, pB=pB: e.matmul(
                    pB[:, :512], Wb[:, ck, 128 * fi:128 * (fi + 1)], xn_fm[:, ck, t0:t0 + 512],
                    start=(ck == 0), stop=(ck == 7)), rd=[Wb, xn_fm, pB], wr=[pB])
            k.ins("vector", lambda e, fi=fi, pB=pB: e.tensor_mul(act[:, fi, :], sbA[:, :], pB[:, :512]),
                  rd=[sbA, pB], wr=[act])
        if tb == 3 and prefetch_cb is not None:
            prefetch_cb()
        for j2 in range(4):
            j = 4 * tb + j2
            yv = ysb[j2 % 2]
            for nb in range(2):
                pY = ps[nb]
                for fi in range(16):
                    k.ins("tensor", lambda e, fi=fi, j2=j2, nb=nb, pY=pY: e.matmul(
                        pY[:, :512], act[:, fi, 128 * j2:128 * (j2 + 1)],
                        W2[:, fi, 512 * nb:512 * (nb + 1)],
                        start=(fi == 0), stop=(fi == 15)), rd=[act, W2, pY], wr=[pY])
                sc = scale_cb(j) if scale_cb is not None else None
                if sc is None:
                    k.ins("vector", lambda e, nb=nb, pY=pY, yv=yv: e.tensor_scalar_mul(
                        yv[:, 512 * nb:512 * (nb + 1)], pY[:, :512], 1.0 / 3.0),
                        rd=[pY], wr=[yv])
                else:
                    k.ins("vector", lambda e, nb=nb, pY=pY, yv=yv, sc=sc: e.tensor_scalar_mul(
                        yv[:, 512 * nb:512 * (nb + 1)], pY[:, :512], sc),
                        rd=[pY], srd=[sc], wr=[yv])
            k.ins("sync", lambda e, j=j, yv=yv: e.dma_start(dv[:, j, :], yv[:, :]),
                  rd=[yv], wr=[dst_dram], dma=True)
    p.rel(sbA, act, ysb[0], ysb[1])


def moe_combine(p, consts, state, dram, srcs, x_src, out_dram, label, base_src=None,
                defer=False):
    k = p.k
    accA = [p.get(label + "aA" + str(i), [128, D], F32) for i in range(2)]
    yt = [p.get(label + "yt" + str(i), [128, D], F32) for i in range(4)]
    sT2 = [p.get(label + "s0", [128, D], F32)]
    ov = out_dram.rearrange("(j q) c -> q j c", q=128) if out_dram is not None else None
    lst = ([base_src] if base_src is not None else []) + list(srcs)
    ha, hb = lst[0::2], lst[1::2]

    def cb(j, xv, nv=None):
        accT, acc2, sT = accA[j % 2], nv, sT2[0]
        for half, (accX, hl) in enumerate(((accT, ha), (acc2, hb))):
            if not hl:
                continue
            k.ins("sync", lambda e, j=j, accX=accX, sd=hl[0]: e.dma_start(
                accX[:, :], sd.rearrange("(j q) c -> q j c", q=128)[:, j, :]),
                rd=[hl[0]], wr=[accX], dma=True)
            for i, sd in enumerate(hl[1:]):
                yv = yt[2 * half + i % 2]
                k.ins("sync", lambda e, j=j, sd=sd, yv=yv: e.dma_start(
                    yv[:, :], sd.rearrange("(j q) c -> q j c", q=128)[:, j, :]),
                    rd=[sd], wr=[yv], dma=True)
                k.ins("vector", lambda e, yv=yv, accX=accX: e.tensor_add(
                    accX[:, :], accX[:, :], yv[:, :]), rd=[yv, accX], wr=[accX])
        if hb:
            k.ins("vector", lambda e, accT=accT, acc2=acc2: e.tensor_add(
                accT[:, :], accT[:, :], acc2[:, :]), rd=[accT, acc2], wr=[accT])
        k.ins("sync", lambda e, j=j, sT=sT: e.dma_start(
            sT[:, :], dram["s"].rearrange("(j q) c -> q j c", q=128)[:, j, :]),
            rd=[dram["s"]], wr=[sT], dma=True)
        k.ins("sync", lambda e, j=j, xv=xv: e.dma_start(
            xv[:, :], x_src.rearrange("(j q) c -> q j c", q=128)[:, j, :]),
            rd=[x_src], wr=[xv], dma=True)
        k.ins("scalar", lambda e, accT=accT: e.activation(accT[:, :], accT[:, :], AF.Silu),
              rd=[accT], wr=[accT])
        k.ins("vector", lambda e, sT=sT, accT=accT: e.tensor_mul(accT[:, :], accT[:, :], sT[:, :]),
              rd=[sT, accT], wr=[accT])
        k.ins("vector", lambda e, xv=xv, accT=accT: e.tensor_add(xv[:, :], xv[:, :], accT[:, :]),
              rd=[accT, xv], wr=[xv])
        if ov is not None:
            k.ins("sync", lambda e, j=j, xv=xv: e.dma_start(ov[:, j, :], xv[:, :]),
                  rd=[xv], wr=[out_dram], dma=True)

    def relall():
        p.rel(accA[0], accA[1], yt[0], yt[1], yt[2], yt[3], sT2[0])

    state["x_src"] = out_dram
    if defer:
        state["combine_cb"] = cb
        state["combine_rel"] = relall
        return
    xT = [p.get(label + "x", [128, D], F32) for _ in range(2)]
    nvT = p.get(label + "nv", [128, D], F32)
    for j in range(NJ):
        cb(j, xT[j % 2], nvT)
    relall()
    p.rel(xT[0], xT[1], nvT)


def _take_combine(state):
    return state.pop("combine_cb", None), state.pop("combine_rel", None)


def emit_L1(p, aps, consts, state, dram):
    k = p.k
    x_src = state["x_src"]
    ps = [p.gps("l1ps", [128, 512], F32) for _ in range(4)]
    xn_fm = p.get("l1xnf", [128, NI, T], BF16)
    f_all = p.get("l1f", [128, NJ], F32)
    mv2 = p.get("l1mv2", [128, 1, 2], F32)
    sc2 = p.get("l1sc2", [128, 12], F32)

    def fcb(j, nv):
        k.ins("vector", lambda e, nv=nv: e.bn_stats(sc2[:, 0:6], nv[:, 0:512]), rd=[nv], wr=[sc2])
        k.ins("vector", lambda e, nv=nv: e.bn_stats(sc2[:, 6:12], nv[:, 512:1024]), rd=[nv], wr=[sc2])
        k.ins("vector", lambda e: e.bn_aggr(mv2[:, 0, :], sc2[:, 0:12]), rd=[sc2], wr=[mv2])
        k.ins("vector", lambda e, j=j: e.tensor_mul(f_all[:, j:j + 1], mv2[:, 0, 0:1], mv2[:, 0, 0:1]),
              rd=[mv2], wr=[f_all])
        k.ins("vector", lambda e, j=j: e.tensor_add(f_all[:, j:j + 1], f_all[:, j:j + 1], mv2[:, 0, 1:2]),
              rd=[mv2, f_all], wr=[f_all])
        k.ins("scalar", lambda e, j=j: e.activation(f_all[:, j:j + 1], f_all[:, j:j + 1], AF.Sqrt),
              rd=[f_all], wr=[f_all])
        k.ins("vector", lambda e, j=j: e.tensor_scalar_max(f_all[:, j:j + 1], f_all[:, j:j + 1], 3.125e-14),
              rd=[f_all], wr=[f_all])
        k.ins("vector", lambda e, j=j: e.reciprocal(f_all[:, j:j + 1], f_all[:, j:j + 1]),
              rd=[f_all], wr=[f_all])

    lcb, lrel = _take_combine(state)
    stream_ln_to_fm(p, x_src, xn_fm, consts, xn_spill=None, eps=1e-5, label="l1ln", post_cb=fcb,
                    load_cb=lcb)
    if lrel is not None:
        lrel()
    p.rel(mv2, sc2)

    gam_row = load_f32(p, aps["L1_gamma"].rearrange("(a c) -> a c", a=1), [1, D], "gamr")
    gam = p.get("l1gam", [128, 8], F32)
    row_to_cols(p, gam, gam_row, D, consts, ps[3], "gamc")
    Wq = load_w_chunks(p, aps["L1_w_q"], D, 128, "l1wq")
    Wkv = load_w_chunks(p, aps["L1_w_kv"], D, 256, "l1wkv")
    for ck in range(8):
        k.ins("vector", lambda e, ck=ck: e.tensor_scalar_mul(Wq[:, ck, :], Wq[:, ck, :], gam[:, ck:ck + 1]),
              rd=[Wq], srd=[gam], wr=[Wq])
        k.ins("vector", lambda e, ck=ck: e.tensor_scalar_mul(Wkv[:, ck, :], Wkv[:, ck, :], gam[:, ck:ck + 1]),
              rd=[Wkv], srd=[gam], wr=[Wkv])
    p.rel(gam_row, gam)
    q_tm = p.get("l1q", [128, NJ, 128], F32)
    kv_tm = p.get("l1kv", [128, NJ, 256], F32)
    for j in range(NJ):
        for ck in range(8):
            k.ins("tensor", lambda e, j=j, ck=ck: e.matmul(
                ps[0][:, :128], xn_fm[:, ck, 128 * j:128 * (j + 1)], Wq[:, ck, :],
                start=(ck == 0), stop=(ck == 7)), rd=[xn_fm, Wq, ps[0]], wr=[ps[0]])
        k.ins("vector", lambda e, j=j: e.tensor_scalar(
            q_tm[:, j, :], ps[0][:, :128], f_all[:, j:j + 1], 0.25, ALU.mult, ALU.mult),
            rd=[ps[0]], srd=[f_all], wr=[q_tm])
        for ck in range(8):
            k.ins("tensor", lambda e, j=j, ck=ck: e.matmul(
                ps[1][:, :256], xn_fm[:, ck, 128 * j:128 * (j + 1)], Wkv[:, ck, :],
                start=(ck == 0), stop=(ck == 7)), rd=[xn_fm, Wkv, ps[1]], wr=[ps[1]])
        k.ins("vector", lambda e, j=j: e.tensor_scalar_mul(
            kv_tm[:, j, :], ps[1][:, :256], f_all[:, j:j + 1]),
            rd=[ps[1]], srd=[f_all], wr=[kv_tm])
    p.rel(Wq, Wkv, f_all, ps[0], ps[1], ps[2], ps[3])

    SQH = 0.5 ** 0.5
    l1pack = p.get("l1pack", [128, 6 * 2048], BF16)
    kexp = [l1pack[:, 2048 * m:2048 * (m + 1)].rearrange("q (j c) -> q j c", j=NJ) for m in range(3)]
    qexp = [l1pack[:, 2048 * (3 + m):2048 * (4 + m)].rearrange("q (j c) -> q j c", j=NJ) for m in range(3)]
    qexp_fm = [p.get("l1qef", [128, 2048], BF16) for m in range(3)]
    vaug = p.get("l1va", [128, NJ, 32], BF16)
    kvs = p.get("l1kvs", [128, 3, 32], BF16)
    pskv = [p.gps("l1pskv", [128, 32], F32) for _ in range(3)]
    psO = p.gps("l1psO", [128, 512], F32)
    psD = p.gps("l1psD", [128, 512], F32)
    psT = consts["ptr"][0]
    denr = p.get("l1den", [1, 512], F32)
    dinvc = p.get("l1dc", [128, 4], F32)
    oT = p.get("l1oT", [128, 512], BF16)
    out_tm = p.get("l1out", [128, NJ, 128], BF16)
    w_out = load_w_chunks(p, aps["L1_w_out"], 128, 1024, "l1wo")
    s_tiles = p.get("l1st", [128, D], F32)
    for h in range(8):
        hc = 16 * h
        for j in range(NJ):
            if h == 0:
                k.ins("vector", lambda e, j=j: e.memset(kexp[0][:, j, 0:1], 1.0), rd=[], wr=[kexp[0]])
                k.ins("vector", lambda e, j=j: e.memset(kexp[0][:, j, 17:128], 0.0), rd=[], wr=[kexp[0]])
                k.ins("vector", lambda e, j=j: e.memset(qexp[0][:, j, 0:1], 1.0), rd=[], wr=[qexp[0]])
                k.ins("vector", lambda e, j=j: e.memset(qexp[0][:, j, 17:128], 0.0), rd=[], wr=[qexp[0]])
                k.ins("vector", lambda e, j=j: e.memset(vaug[:, j, 16:17], 1.0), rd=[], wr=[vaug])
            k.ins("scalar", lambda e, j=j, hc=hc: e.copy(kexp[0][:, j, 1:17], kv_tm[:, j, hc:hc + 16]),
                  rd=[kv_tm], wr=[kexp[0]])
            k.ins("scalar", lambda e, j=j, hc=hc: e.copy(qexp[0][:, j, 1:17], q_tm[:, j, hc:hc + 16]),
                  rd=[q_tm], wr=[qexp[0]])
            k.ins("scalar", lambda e, j=j, hc=hc: e.copy(vaug[:, j, 0:16], kv_tm[:, j, 128 + hc:128 + hc + 16]),
                  rd=[kv_tm], wr=[vaug])
            for i in range(16):
                m, o = divmod(16 * i, 128)
                k.ins("vector", lambda e, j=j, i=i, m=m, o=o, hc=hc: e.tensor_scalar(
                    kexp[1 + m][:, j, o:o + 16], kv_tm[:, j, hc:hc + 16],
                    kv_tm[:, j, hc + i:hc + i + 1], SQH, ALU.mult, ALU.mult),
                    rd=[kv_tm], srd=[kv_tm], wr=[kexp[1 + m]])
                k.ins("vector", lambda e, j=j, i=i, m=m, o=o, hc=hc: e.tensor_scalar(
                    qexp[1 + m][:, j, o:o + 16], q_tm[:, j, hc:hc + 16],
                    q_tm[:, j, hc + i:hc + i + 1], SQH, ALU.mult, ALU.mult),
                    rd=[q_tm], srd=[q_tm], wr=[qexp[1 + m]])
        for m in range(3):
            for j in range(NJ):
                k.ins("tensor", lambda e, m=m, j=j: e.matmul(
                    pskv[m][:, :17], kexp[m][:, j, :], vaug[:, j, 0:17],
                    start=(j == 0), stop=(j == NJ - 1)), rd=[kexp[m], vaug, pskv[m]], wr=[pskv[m]])
            k.ins(k.evac_eng(), lambda e, m=m: ecopy(e, kvs[:, m, 0:17], pskv[m][:, :17]),
                  rd=[pskv[m]], wr=[kvs])
        pairs = []
        for j in range(NJ):
            for m in range(3):
                pairs.append((qexp[m][:, j, :], qexp_fm[m][:, 128 * j:128 * (j + 1)],
                              qexp[m], qexp_fm[m]))
        transpose_pairs(p, pairs, consts, "l1qT")
        for tb in range(4):
            for m in range(3):
                k.ins("tensor", lambda e, m=m, tb=tb: e.matmul(
                    psO[:16, :512], kvs[:, m, 0:16], qexp_fm[m][:, 512 * tb:512 * (tb + 1)],
                    start=(m == 0), stop=(m == 2)), rd=[kvs, qexp_fm[m], psO], wr=[psO])
            for m in range(3):
                k.ins("tensor", lambda e, m=m, tb=tb: e.matmul(
                    psD[0:1, :512], kvs[:, m, 16:17], qexp_fm[m][:, 512 * tb:512 * (tb + 1)],
                    start=(m == 0), stop=(m == 2)), rd=[kvs, qexp_fm[m], psD], wr=[psD])
            k.ins("vector", lambda e: e.tensor_scalar_max(denr[:, :], psD[0:1, :512], 1e-5),
                  rd=[psD], wr=[denr])
            k.ins("vector", lambda e: e.reciprocal(denr[:, :], denr[:, :]), rd=[denr], wr=[denr])
            k.ins(k.evac_eng(), lambda e: ecopy(e, oT[:16, :], psO[:16, :512]), rd=[psO], wr=[oT])
            for sub in range(4):
                j = 4 * tb + sub
                k.ins("tensor", lambda e, sub=sub: e.transpose(
                    psT[:128, 0:16], oT[:16, 128 * sub:128 * (sub + 1)], consts["ident"][:16, :16]),
                    rd=[oT, consts["ident"], psT], wr=[psT])
                k.ins("tensor", lambda e, sub=sub: e.matmul(
                    psD[:, 1:2], denr[0:1, 128 * sub:128 * (sub + 1)],
                    consts["ones_row32"][0:1, 0:1], start=True, stop=True),
                    rd=[denr, consts["ones_row32"], psD], wr=[psD])
                k.ins("vector", lambda e, sub=sub: e.tensor_copy(dinvc[:, sub:sub + 1], psD[:, 1:2]),
                      rd=[psD], wr=[dinvc])
                k.ins("vector", lambda e, j=j, sub=sub, h=h: e.tensor_scalar_mul(
                    out_tm[:, j, 16 * h:16 * (h + 1)], psT[:, 0:16], dinvc[:, sub:sub + 1]),
                    rd=[psT], srd=[dinvc], wr=[out_tm])
    p.rel(q_tm, kv_tm, l1pack, qexp_fm[0], qexp_fm[1], qexp_fm[2], vaug, kvs, oT, denr, dinvc,
          pskv[0], pskv[1], pskv[2], psO, psD)
    ps = [p.gps("l1ps2", [128, 512], F32) for _ in range(4)]
    out_fm = p.get("l1of", [128, T], BF16)
    transpose_pairs(p, [(out_tm[:, j, :], out_fm[:, 128 * j:128 * (j + 1)], out_tm, out_fm)
                        for j in range(NJ)], consts, "l1oT2")
    sdv = dram["s"].rearrange("(j q) c -> q j c", q=128)
    for j in range(NJ):
        for nb in range(2):
            k.ins("tensor", lambda e, j=j, nb=nb: e.matmul(
                ps[nb][:, :512], out_fm[:, 128 * j:128 * (j + 1)], w_out[:, 0, 512 * nb:512 * (nb + 1)],
                start=True, stop=True), rd=[out_fm, w_out, ps[nb]], wr=[ps[nb]])
            k.ins("vector", lambda e, j=j, nb=nb: e.tensor_copy(
                s_tiles[:, 512 * nb:512 * (nb + 1)], ps[nb][:, :512]), rd=[ps[nb]], wr=[s_tiles])
        k.ins("sync", lambda e, j=j: e.dma_start(sdv[:, j, :], s_tiles[:, :]),
              rd=[s_tiles], wr=[dram["s"]], dma=True)
    p.rel(out_tm, out_fm, w_out, s_tiles)

    Wg = load_bf(p, aps["L1_Wg"], [128, 8, 8], "l1Wg", ("(kc kp) n -> kp kc n", dict(kp=128)))
    gates = p.get("l1gt", [128, NJ, 8], F32)
    w3 = p.get("l1w3", [128, NJ, 8], F32)
    t8 = p.get("l1t8", [128, 8], F32)
    m1 = p.get("l1m1", [128, 1], F32)
    m2 = p.get("l1m2", [128, 1], F32)
    for j in range(NJ):
        for ck in range(8):
            k.ins("tensor", lambda e, j=j, ck=ck: e.matmul(
                ps[0][:, :8], xn_fm[:, ck, 128 * j:128 * (j + 1)], Wg[:, ck, :],
                start=(ck == 0), stop=(ck == 7)), rd=[xn_fm, Wg, ps[0]], wr=[ps[0]])
        k.ins("scalar", lambda e, j=j: e.activation(gates[:, j, :], ps[0][:, :8], AF.Sigmoid),
              rd=[ps[0]], wr=[gates])
        k.ins("vector", lambda e, j=j: e.reduce_max(m1[:, :], gates[:, j, :], AX.X),
              rd=[gates], wr=[m1])
        k.ins("vector", lambda e, j=j: e.tensor_scalar(
            t8[:, :], gates[:, j, :], m1[:, 0:1], None, ALU.is_equal), rd=[gates], srd=[m1], wr=[t8])
        k.ins("vector", lambda e, j=j: e.scalar_tensor_tensor(
            w3[:, j, :], t8[:, :], -1e9, gates[:, j, :], ALU.mult, ALU.add),
            rd=[t8, gates], wr=[w3])
        k.ins("vector", lambda e, j=j: e.reduce_max(m2[:, :], w3[:, j, :], AX.X), rd=[w3], wr=[m2])
        k.ins("vector", lambda e, j=j: e.tensor_scalar(
            w3[:, j, :], w3[:, j, :], m2[:, 0:1], None, ALU.is_equal), rd=[w3], srd=[m2], wr=[w3])
        k.ins("vector", lambda e, j=j: e.tensor_add(w3[:, j, :], w3[:, j, :], t8[:, :]),
              rd=[t8, w3], wr=[w3])
        k.ins("vector", lambda e, j=j: e.tensor_add(m1[:, :], m1[:, :], m2[:, :]), rd=[m2, m1], wr=[m1])
        k.ins("vector", lambda e, j=j: e.tensor_scalar_mul(m1[:, :], m1[:, :], 3.0), rd=[m1], wr=[m1])
        k.ins("vector", lambda e, j=j: e.reciprocal(m1[:, :], m1[:, :]), rd=[m1], wr=[m1])
        k.ins("vector", lambda e, j=j: e.tensor_mul(w3[:, j, :], w3[:, j, :], gates[:, j, :]),
              rd=[gates, w3], wr=[w3])
        k.ins("vector", lambda e, j=j: e.tensor_scalar_mul(w3[:, j, :], w3[:, j, :], m1[:, 0:1]),
              rd=[w3], srd=[m1], wr=[w3])
    p.rel(Wg, gates, t8, m1, m2)

    Wa = p.get("l1Wa", [128, 8, 2048], BF16)
    Wb = p.get("l1Wb", [128, 8, 2048], BF16)
    W2 = p.get("l1W2", [128, 16, 1024], BF16)
    load_w_chunks(p, aps["L1_Ws1"], D, 2048, "l1s1", into=Wa)
    load_w_chunks(p, aps["L1_Ws3"], D, 2048, "l1s3", into=Wb)
    load_w_chunks(p, aps["L1_Ws2"], 2048, 1024, "l1s2", into=W2)
    def mk_prefetch(e_next):
        def cb():
            if e_next < 8:
                load_w_chunks(p, aps["L1_We1"][e_next], D, 2048, f"l1e1{e_next}", into=Wa)
                load_w_chunks(p, aps["L1_We3"][e_next], D, 2048, f"l1e3{e_next}", into=Wb)
                load_w_chunks(p, aps["L1_We2"][e_next], 2048, 1024, f"l1e2{e_next}", into=W2)
        return cb

    glu_moe_expert(p, aps, consts, xn_fm, Wa, Wb, W2, None, dram["acc"], ps, "l1sh",
                   prefetch_cb=mk_prefetch(0))
    for e_ in range(8):
        glu_moe_expert(p, aps, consts, xn_fm, Wa, Wb, W2,
                       (lambda j, e_=e_: w3[:, j, e_:e_ + 1]), dram["y"][e_], ps, f"l1x{e_}",
                       prefetch_cb=mk_prefetch(e_ + 1))
    p.rel(Wa, Wb, W2, w3, xn_fm)
    moe_combine(p, consts, state, dram, [dram["acc"]] + dram["y"], x_src, dram["x"], "l1cb",
                defer=True)
    p.rel(ps[0], ps[1], ps[2], ps[3])


def emit_L2(p, aps, consts, state, dram):
    k = p.k
    x_src = state["x_src"]
    ps = [p.gps("l2ps", [128, 512], F32) for _ in range(2)]
    xn_fm = p.get("l2xnf", [128, NI, T], BF16)
    lcb, lrel = _take_combine(state)
    stream_ln_to_fm(p, x_src, xn_fm, consts, xn_spill=None, eps=1e-5, label="l2ln", load_cb=lcb)
    if lrel is not None:
        lrel()

    wqA = p.get("l2wA", [128, 4, 3072], BF16)
    wqB = p.get("l2wB", [128, 4, 3072], BF16)
    _s = aps["L2_wqkv"].rearrange("(kc kp) n -> kc kp n", kp=128)
    for kc in range(8):
        _t, _sub = (wqA, kc) if kc < 4 else (wqB, kc - 4)
        p.k.ins("gpsimd", lambda e, kc=kc, _t=_t, _sub=_sub: e.dma_start(_t[:, _sub, :], _s[kc]),
                rd=[_s], wr=[_t], dma=True, label=f"l2w{kc}")
    qkvt = [p.get("l2qkv", [128, 3072], F32) for _ in range(2)]
    psC = [p.gps("l2pC", [1, 512], F32) for _ in range(2)]
    psS = p.gps("l2pS", [128, 64], F32)
    carry = p.get("l2cr", [1, 1024], F32)
    v_bf = p.get("l2vb", [128, 1024], BF16)
    sii = p.get("l2sii", [128, 16], F32)
    mm_ = p.get("l2m", [128, 16], F32)
    esb = p.get("l2es", [128, 16], F32)
    emb = p.get("l2em", [128, 16], F32)
    qks = p.get("l2qk", [128, 64], F32)
    esv = p.get("l2esv", [128, 64], F32)
    den = p.get("l2den", [128, 1], F32)
    s_tile = p.get("l2st", [128, D], F32)
    sdv = dram["s"].rearrange("(j q) c -> q j c", q=128)
    k.ins("vector", lambda e: e.memset(carry[:, :], 0.0), rd=[], wr=[carry])
    for cc in range(2):
        k.ins("tensor", lambda e, cc=cc: e.matmul(
            psC[cc][0:1, :512], consts["ones_row_bf"][0:1, 0:1], consts["zeros_row_bf"][:, 0:512],
            start=True, stop=False),
            rd=[consts["ones_row_bf"], consts["zeros_row_bf"], psC[cc]], wr=[psC[cc]])
    for j in range(NJ):
        qv = qkvt[j % 2]
        for nb in range(6):
            k.ins("vector", lambda e: e.engine_nop(), rd=[], wr=[]) if False else None
            for ck in range(8):
                _t, _sub = (wqA, ck) if ck < 4 else (wqB, ck - 4)
                k.ins("tensor", lambda e, j=j, nb=nb, ck=ck, _t=_t, _sub=_sub: e.matmul(
                    ps[nb % 2][:, :512], xn_fm[:, ck, 128 * j:128 * (j + 1)],
                    _t[:, _sub, 512 * nb:512 * (nb + 1)],
                    start=(ck == 0), stop=(ck == 7)), rd=[xn_fm, _t, ps[nb % 2]], wr=[ps[nb % 2]])
            k.ins(k.evac_eng(), lambda e, nb=nb, qv=qv: ecopy(
                e, qv[:, 512 * nb:512 * (nb + 1)], ps[nb % 2][:, :512]), rd=[ps[nb % 2]], wr=[qv])
        for h in range(16):
            b = 192 * h
            k.ins("vector", lambda e, h=h, qv=qv, b=b: e.tensor_mul(
                qks[:, :], qv[:, b:b + 64], qv[:, b + 64:b + 128]), rd=[qv], wr=[qks])
            k.ins("vector", lambda e, h=h: e.reduce_sum(sii[:, h:h + 1], qks[:, :], AX.X),
                  rd=[qks], wr=[sii])
            k.ins("scalar", lambda e, h=h, qv=qv, b=b: e.copy(v_bf[:, 64 * h:64 * (h + 1)], qv[:, b + 128:b + 192]),
                  rd=[qv], wr=[v_bf])
        k.ins("vector", lambda e: e.tensor_scalar_mul(sii[:, :], sii[:, :], 0.125), rd=[sii], wr=[sii])
        k.ins("vector", lambda e: e.tensor_scalar_max(mm_[:, :], sii[:, :], 0.0), rd=[sii], wr=[mm_])
        k.ins("vector", lambda e: e.tensor_sub(esb[:, :], sii[:, :], mm_[:, :]), rd=[sii, mm_], wr=[esb])
        k.ins("scalar", lambda e: e.activation(esb[:, :], esb[:, :], AF.Exp), rd=[esb], wr=[esb])
        k.ins("scalar", lambda e: e.activation(emb[:, :], mm_[:, :], AF.Exp, scale=-1.0), rd=[mm_], wr=[emb])
        for h in range(16):
            k.ins("tensor", lambda e, h=h: e.matmul(
                psS[:, :64], consts["upstrict"][:, :], v_bf[:, 64 * h:64 * (h + 1)],
                start=True, stop=False), rd=[consts["upstrict"], v_bf, psS], wr=[psS])
            k.ins("tensor", lambda e, h=h: e.matmul(
                psS[:, :64], consts["ones_row32"][:, :], carry[0:1, 64 * h:64 * (h + 1)],
                start=False, stop=True), rd=[consts["ones_row32"], carry, psS], wr=[psS])
            k.ins("vector", lambda e, h=h: e.tensor_scalar_mul(
                esv[:, :], v_bf[:, 64 * h:64 * (h + 1)], esb[:, h:h + 1]),
                rd=[v_bf], srd=[esb], wr=[esv])
            k.ins("vector", lambda e, h=h: e.scalar_tensor_tensor(
                esv[:, :], psS[:, :64], emb[:, h:h + 1], esv[:, :], ALU.mult, ALU.add),
                rd=[psS, esv], srd=[emb], wr=[esv])
            k.ins("vector", lambda e, h=h, j=j: e.scalar_tensor_tensor(
                den[:, :], consts["pos"][:, j:j + 1], emb[:, h:h + 1], esb[:, h:h + 1], ALU.mult, ALU.add),
                rd=[consts["pos"], esb], srd=[emb], wr=[den])
            k.ins("vector", lambda e: e.reciprocal(den[:, :], den[:, :]), rd=[den], wr=[den])
            k.ins("vector", lambda e, h=h: e.tensor_scalar_mul(
                s_tile[:, 64 * h:64 * (h + 1)], esv[:, :], den[:, 0:1]),
                rd=[esv], srd=[den], wr=[s_tile])
            cc = h // 8
            k.ins("tensor", lambda e, h=h, cc=cc: e.matmul(
                psC[cc][0:1, 64 * (h % 8):64 * (h % 8) + 64], consts["ones_col_bf"][:, :],
                v_bf[:, 64 * h:64 * (h + 1)], start=False, stop=(j == NJ - 1 and h % 8 == 7)),
                rd=[consts["ones_col_bf"], v_bf, psC[cc]], wr=[psC[cc]])
        for cc in range(2):
            k.ins("scalar", lambda e, cc=cc: e.copy(carry[0:1, 512 * cc:512 * (cc + 1)], psC[cc][0:1, :512]),
                  rd=[psC[cc]], wr=[carry])
        k.ins("sync", lambda e, j=j: e.dma_start(sdv[:, j, :], s_tile[:, :]),
              rd=[s_tile], wr=[dram["s"]], dma=True)
    p.rel(wqA, wqB, qkvt[0], qkvt[1], psC[0], psC[1], psS, carry, v_bf, sii, mm_, esb, emb, qks, esv, den, s_tile)

    rwT = load_w_chunks(p, aps["L2_router_wT"], D, 8, "l2r")
    rb = load_f32(p, aps["L2_router_b"].rearrange("(a c) -> a c", a=1), [1, 8], "l2rb")
    mask = p.get("l2mk", [128, NJ, 8], F32)
    t8 = p.get("l2t8", [128, 8], F32)
    m1 = p.get("l2m1", [128, 1], F32)
    for j in range(NJ):
        for ck in range(8):
            k.ins("tensor", lambda e, j=j, ck=ck: e.matmul(
                ps[0][:, :8], xn_fm[:, ck, 128 * j:128 * (j + 1)], rwT[:, ck, :],
                start=(ck == 0), stop=False), rd=[xn_fm, rwT, ps[0]], wr=[ps[0]])
        k.ins("tensor", lambda e, j=j: e.matmul(
            ps[0][:, :8], consts["ones_row32"][:, :], rb[0:1, :], start=False, stop=True),
            rd=[consts["ones_row32"], rb, ps[0]], wr=[ps[0]])
        k.ins("vector", lambda e, j=j: e.tensor_copy(t8[:, :], ps[0][:, :8]), rd=[ps[0]], wr=[t8])
        k.ins("vector", lambda e, j=j: e.reduce_max(m1[:, :], t8[:, :], AX.X), rd=[t8], wr=[m1])
        k.ins("vector", lambda e, j=j: e.tensor_scalar(
            mask[:, j, :], t8[:, :], m1[:, 0:1], None, ALU.is_equal), rd=[t8], srd=[m1], wr=[mask])
        k.ins("vector", lambda e, j=j: e.scalar_tensor_tensor(
            t8[:, :], mask[:, j, :], -1e9, t8[:, :], ALU.mult, ALU.add), rd=[mask, t8], wr=[t8])
        k.ins("vector", lambda e, j=j: e.reduce_max(m1[:, :], t8[:, :], AX.X), rd=[t8], wr=[m1])
        k.ins("vector", lambda e, j=j: e.tensor_scalar(
            t8[:, :], t8[:, :], m1[:, 0:1], None, ALU.is_equal), rd=[t8], srd=[m1], wr=[t8])
        k.ins("vector", lambda e, j=j: e.tensor_add(mask[:, j, :], mask[:, j, :], t8[:, :]),
              rd=[t8, mask], wr=[mask])
    p.rel(rwT, rb, t8, m1)
    ew2 = [p.get("l2ew", [128, 8, 1024], BF16) for _ in range(2)]
    eb2 = [p.get("l2eb", [1, 1024], F32) for _ in range(2)]
    ysb = [p.get("l2y", [128, D], F32) for _ in range(2)]

    def l2load(e_):
        load_w_chunks(p, aps["L2_expert_wT"][e_], D, 1024, f"l2e{e_}", into=ew2[e_ % 2])
        k.ins("sync", lambda e, e_=e_: e.dma_start(eb2[e_ % 2][:, :], aps["L2_expert_b"][e_:e_ + 1, :]),
              rd=[aps["L2_expert_b"]], wr=[eb2[e_ % 2]], dma=True)

    l2load(0)
    for e_ in range(8):
        ew, eb = ew2[e_ % 2], eb2[e_ % 2]
        if e_ < 7:
            l2load(e_ + 1)
        ydv = dram["y"][e_].rearrange("(j q) c -> q j c", q=128)
        for j in range(NJ):
            yv = ysb[j % 2]
            for nb in range(2):
                for ck in range(8):
                    k.ins("tensor", lambda e, j=j, nb=nb, ck=ck, ew=ew: e.matmul(
                        ps[nb][:, :512], xn_fm[:, ck, 128 * j:128 * (j + 1)],
                        ew[:, ck, 512 * nb:512 * (nb + 1)],
                        start=(ck == 0), stop=False), rd=[xn_fm, ew, ps[nb]], wr=[ps[nb]])
                k.ins("tensor", lambda e, nb=nb, eb=eb: e.matmul(
                    ps[nb][:, :512], consts["ones_row32"][:, :], eb[0:1, 512 * nb:512 * (nb + 1)],
                    start=False, stop=True), rd=[consts["ones_row32"], eb, ps[nb]], wr=[ps[nb]])
                k.ins("vector", lambda e, j=j, nb=nb, e_=e_, yv=yv: e.tensor_scalar_mul(
                    yv[:, 512 * nb:512 * (nb + 1)], ps[nb][:, :512], mask[:, j, e_:e_ + 1]),
                    rd=[ps[nb]], srd=[mask], wr=[yv])
            k.ins("sync", lambda e, j=j, yv=yv, ydv=ydv: e.dma_start(ydv[:, j, :], yv[:, :]),
                  rd=[yv], wr=[dram["y"][e_]], dma=True)
    p.rel(ew2[0], ew2[1], eb2[0], eb2[1], ysb[0], ysb[1], mask, xn_fm)
    moe_combine(p, consts, state, dram, dram["y"][1:], x_src, dram["x2"], "l2cb",
                base_src=dram["y"][0], defer=True)
    p.rel(ps[0], ps[1])


def emit_L3(p, aps, consts, state, dram):
    k = p.k
    x_src = state["x_src"]
    ps = [p.gps("l3ps", [128, 512], F32) for _ in range(4)]
    xn_fm = p.get("l3xnf", [128, NI, T], BF16)
    lcb, lrel = _take_combine(state)
    stream_ln_to_fm(p, x_src, xn_fm, consts, xn_spill=dram["xn"], eps=1e-5, label="l3ln",
                    load_cb=lcb)
    if lrel is not None:
        lrel()

    gwT = load_w_chunks(p, aps["L3_gate_wT"], D, D, "l3gw")
    gb_row = load_f32(p, aps["L3_gate_b"].rearrange("(a c) -> a c", a=1), [1, D], "l3gb")
    gt_ = p.get("l3gt", [128, D], F32)
    gmean = p.get("l3gm", [128, 1], F32)
    eps5 = p.cval(1e-5)
    gnv = dram["acc"].rearrange("(j q) c -> q j c", q=128)
    for j in range(NJ):
        for nb in range(2):
            for ck in range(8):
                k.ins("tensor", lambda e, j=j, nb=nb, ck=ck: e.matmul(
                    ps[nb][:, :512], xn_fm[:, ck, 128 * j:128 * (j + 1)], gwT[:, ck, 512 * nb:512 * (nb + 1)],
                    start=(ck == 0), stop=False), rd=[xn_fm, gwT, ps[nb]], wr=[ps[nb]])
            k.ins("tensor", lambda e, nb=nb: e.matmul(
                ps[nb][:, :512], consts["ones_row32"][:, :], gb_row[:, 512 * nb:512 * (nb + 1)],
                start=False, stop=True), rd=[consts["ones_row32"], gb_row, ps[nb]], wr=[ps[nb]])
            k.ins("scalar", lambda e, nb=nb: e.activation(gt_[:, 512 * nb:512 * (nb + 1)], ps[nb][:, :512],
                                                          AF.Sigmoid), rd=[ps[nb]], wr=[gt_])
        k.ins("vector", lambda e: e.reduce_sum(gmean[:, :], gt_[:, :], AX.X), rd=[gt_], wr=[gmean])
        k.ins("vector", lambda e: e.tensor_scalar(gmean[:, :], gmean[:, :], 1.0 / D, 1e-5, ALU.mult, ALU.add),
              rd=[gmean], wr=[gmean])
        k.ins("vector", lambda e: e.reciprocal(gmean[:, :], gmean[:, :]), rd=[gmean], wr=[gmean])
        k.ins("vector", lambda e: e.tensor_scalar_mul(gt_[:, :], gt_[:, :], gmean[:, 0:1]),
              rd=[gt_], srd=[gmean], wr=[gt_])
        k.ins("sync", lambda e, j=j: e.dma_start(gnv[:, j, :], gt_[:, :]), rd=[gt_], wr=[dram["acc"]], dma=True)
    p.rel(gwT, gb_row, gt_, gmean, xn_fm)

    lngr = load_rep(p, aps["L3_ln_g"].rearrange("(a c) -> a c", a=1), D, consts, ps[3], "l3lg")
    lnbr = load_rep(p, aps["L3_ln_b"].rearrange("(a c) -> a c", a=1), D, consts, ps[3], "l3lb")
    xnorm_fm = p.get("l3xno", [128, NI, T], BF16)
    stream_ln_to_fm(p, dram["xn"], xnorm_fm, consts, xn_spill=None, eps=1e-5, label="l3ln2",
                    grep=lngr, brep=lnbr)
    p.rel(lngr, lnbr)

    qb_row = load_f32(p, aps["L3_qkv_b"].rearrange("(a c) -> a c", a=1), [1, 3072], "l3qb")
    q_fm = p.get("l3qf", [128, NI, T], BF16)
    k_tm = p.get("l3kt", [128, NJ, D], BF16)
    v_tm = p.get("l3vt", [128, NJ, D], BF16)
    wsbh = p.get("l3wq", [128, 8, 512], BF16)
    scr_a = p.get("l3sa", [128, 512], F32)
    scr_b = p.get("l3sb", [128, 512], F32)
    scr_c = p.get("l3sc", [128, 512], BF16)
    gnt = [p.get("l3gn", [128, 512], F32) for _ in range(2)]
    for part, off in (("q", 0), ("k", 1024), ("v", 2048)):
        for half in range(2):
            o2 = off + 512 * half
            load_w_chunks(p, aps["L3_qkv_wT"][:, o2:o2 + 512], D, 512, "l3w" + part, into=wsbh)
            for j in range(NJ):
                pp = ps[j % 2]
                for ck in range(8):
                    k.ins("tensor", lambda e, j=j, ck=ck, pp=pp: e.matmul(
                        pp[:, :512], xnorm_fm[:, ck, 128 * j:128 * (j + 1)], wsbh[:, ck, :],
                        start=(ck == 0), stop=False), rd=[xnorm_fm, wsbh, pp], wr=[pp])
                k.ins("tensor", lambda e, o2=o2, pp=pp: e.matmul(
                    pp[:, :512], consts["ones_row32"][:, :], qb_row[:, o2:o2 + 512],
                    start=False, stop=True), rd=[consts["ones_row32"], qb_row, pp], wr=[pp])
                if part == "v":
                    k.ins("scalar", lambda e, j=j, half=half, pp=pp: e.copy(
                        v_tm[:, j, 512 * half:512 * (half + 1)], pp[:, :512]), rd=[pp], wr=[v_tm])
                    continue
                gv = gnt[j % 2]
                k.ins("sync", lambda e, j=j, half=half, gv=gv: e.dma_start(
                    gv[:, :], gnv[:, j, 512 * half:512 * (half + 1)]),
                    rd=[dram["acc"]], wr=[gv], dma=True)
                k.ins("vector", lambda e, gv=gv, pp=pp: e.tensor_mul(scr_a[:, :], pp[:, :512], gv[:, :]),
                      rd=[pp, gv], wr=[scr_a])
                k.ins("vector", lambda e: e.tensor_scalar_min(scr_b[:, :], scr_a[:, :], 0.0),
                      rd=[scr_a], wr=[scr_b])
                k.ins("scalar", lambda e: e.activation(scr_b[:, :], scr_b[:, :], AF.Exp), rd=[scr_b], wr=[scr_b])
                k.ins("vector", lambda e: e.tensor_relu(scr_a[:, :], scr_a[:, :]), rd=[scr_a], wr=[scr_a])
                if part == "k":
                    k.ins("vector", lambda e, j=j, half=half: e.tensor_add(
                        k_tm[:, j, 512 * half:512 * (half + 1)], scr_a[:, :], scr_b[:, :]),
                        rd=[scr_a, scr_b], wr=[k_tm])
                else:
                    k.ins("vector", lambda e: e.tensor_add(scr_c[:, :], scr_a[:, :], scr_b[:, :]),
                          rd=[scr_a, scr_b], wr=[scr_c])
                    pairs = [(scr_c[:, 128 * i:128 * (i + 1)], q_fm[:, 4 * half + i, 128 * j:128 * (j + 1)],
                              scr_c, q_fm) for i in range(4)]
                    transpose_pairs(p, pairs, consts, "l3q")
    p.rel(wsbh, qb_row, scr_a, scr_b, scr_c, gnt[0], gnt[1], xnorm_fm, ps[0], ps[1], ps[2], ps[3])

    pkT = consts["ptr"][0]
    psS = p.gps("l3pS", [128, 512], F32)
    psD = p.gps("l3pD", [128, 512], F32)
    psN = p.gps("l3pN", [128, 512], F32)
    psK = p.gps("l3pK", [128, 512], F32)
    psT2 = consts["ptr"][1]
    kf_sb = p.get("l3kf", [128, 128], BF16)
    STb = p.get("l3STb", [128, 128], BF16)
    numb = p.get("l3num", [128, 128], BF16)
    KV_sb = p.get("l3KV", [128, 65], BF16)
    den_sb = p.get("l3dn", [128, 1], F32)
    out_tm = p.get("l3ot", [128, NJ, D], BF16)
    triu32 = p.get("l3triu32", [128, 128], F32)
    k.ins("sync", lambda e: e.dma_start(triu32[:, :], aps["c_triu01"][:, :]), rd=[aps["c_triu01"]],
          wr=[triu32], dma=True)
    for h in range(16):
        b = (h % 2) * 64
        ci = h // 2
        k.ins("vector", lambda e, b=b: e.memset(KV_sb[b:b + 64, :], 0.0), rd=[], wr=[KV_sb])
        k.ins("tensor", lambda e, b=b: e.matmul(
            psK[b:b + 64, 0:65], consts["ones_row_bf"][0:1, 0:64], consts["zeros_row_bf"][:, 0:65],
            start=True, stop=False), rd=[consts["ones_row_bf"], consts["zeros_row_bf"], psK], wr=[psK])
        for j in range(NJ):
            k.ins("tensor", lambda e, j=j, h=h: e.transpose(
                pkT[:64, :128], k_tm[:, j, 64 * h:64 * (h + 1)], consts["ident"][:, :]),
                rd=[k_tm, consts["ident"], pkT], wr=[pkT])
            k.ins(k.evac_eng(), lambda e, b=b: ecopy(e, kf_sb[b:b + 64, :], pkT[:64, :128]),
                  rd=[pkT], wr=[kf_sb])
            k.ins("tensor", lambda e, j=j, b=b, ci=ci: e.matmul(
                psS[:, :128], kf_sb[b:b + 64, :], q_fm[b:b + 64, ci, 128 * j:128 * (j + 1)],
                start=True, stop=True), rd=[kf_sb, q_fm, psS], wr=[psS])
            k.ins("vector", lambda e: e.tensor_mul(STb[:, :], psS[:, :128], triu32[:, :]),
                  rd=[psS, triu32], wr=[STb])
            k.ins("tensor", lambda e: e.matmul(psD[:, 0:1], STb[:, :], consts["ones_col_bf"][:, :],
                                               start=True, stop=False),
                  rd=[STb, consts["ones_col_bf"], psD], wr=[psD])
            k.ins("tensor", lambda e, j=j, b=b, ci=ci: e.matmul(
                psD[:, 0:1], q_fm[b:b + 64, ci, 128 * j:128 * (j + 1)], KV_sb[b:b + 64, 64:65],
                start=False, stop=True), rd=[q_fm, KV_sb, psD], wr=[psD])
            k.ins("tensor", lambda e, j=j, h=h: e.matmul(
                psN[:64, :128], v_tm[:, j, 64 * h:64 * (h + 1)], STb[:, :], start=True, stop=False),
                rd=[v_tm, STb, psN], wr=[psN])
            k.ins("tensor", lambda e, j=j, b=b, ci=ci: e.matmul(
                psN[:64, :128], KV_sb[b:b + 64, 0:64], q_fm[b:b + 64, ci, 128 * j:128 * (j + 1)],
                start=False, stop=True), rd=[KV_sb, q_fm, psN], wr=[psN])
            k.ins("tensor", lambda e, j=j, h=h, b=b: e.matmul(
                psK[b:b + 64, 0:64], k_tm[:, j, 64 * h:64 * (h + 1)], v_tm[:, j, 64 * h:64 * (h + 1)],
                start=False, stop=False), rd=[k_tm, v_tm, psK], wr=[psK])
            k.ins("tensor", lambda e, j=j, h=h, b=b: e.matmul(
                psK[b:b + 64, 64:65], k_tm[:, j, 64 * h:64 * (h + 1)], consts["ones_col_bf"][:, :],
                start=False, stop=(j == NJ - 1)), rd=[k_tm, consts["ones_col_bf"], psK], wr=[psK])
            k.ins("scalar", lambda e, b=b: e.copy(KV_sb[b:b + 64, :], psK[b:b + 64, 0:65]),
                  rd=[psK], wr=[KV_sb])
            k.ins("vector", lambda e: e.tensor_scalar(den_sb[:, :], psD[:, 0:1], 1e-5, None, ALU.add),
                  rd=[psD], wr=[den_sb])
            k.ins("vector", lambda e: e.reciprocal(den_sb[:, :], den_sb[:, :]), rd=[den_sb], wr=[den_sb])
            k.ins(k.evac_eng(), lambda e: ecopy(e, numb[:64, :], psN[:64, :128]), rd=[psN], wr=[numb])
            k.ins("tensor", lambda e: e.transpose(psT2[:128, :64], numb[:64, :], consts["ident"][:64, :64]),
                  rd=[numb, consts["ident"], psT2], wr=[psT2])
            k.ins("vector", lambda e, j=j, h=h: e.tensor_scalar_mul(
                out_tm[:, j, 64 * h:64 * (h + 1)], psT2[:, :64], den_sb[:, 0:1]),
                rd=[psT2], srd=[den_sb], wr=[out_tm])
    p.rel(psS, psD, psN, psK, kf_sb, STb, numb, KV_sb, den_sb, triu32, q_fm, k_tm, v_tm)

    ps = [p.gps("l3ps2", [128, 512], F32) for _ in range(2)]
    out_fm = p.get("l3ofm", [128, NI, T], BF16)
    tm_to_fm(p, out_fm, out_tm, consts, label="l3o")
    p.rel(out_tm)
    pwT = load_w_chunks(p, aps["L3_proj_wT"], D, D, "l3pw")
    pb_row = load_f32(p, aps["L3_proj_b"].rearrange("(a c) -> a c", a=1), [1, D], "l3pb")
    s_tile = p.get("l3st", [128, D], F32)
    sdv = dram["s"].rearrange("(j q) c -> q j c", q=128)
    for j in range(NJ):
        for nb in range(2):
            for ck in range(8):
                k.ins("tensor", lambda e, j=j, nb=nb, ck=ck: e.matmul(
                    ps[nb][:, :512], out_fm[:, ck, 128 * j:128 * (j + 1)], pwT[:, ck, 512 * nb:512 * (nb + 1)],
                    start=(ck == 0), stop=False), rd=[out_fm, pwT, ps[nb]], wr=[ps[nb]])
            k.ins("tensor", lambda e, nb=nb: e.matmul(
                ps[nb][:, :512], consts["ones_row32"][:, :], pb_row[:, 512 * nb:512 * (nb + 1)],
                start=False, stop=True), rd=[consts["ones_row32"], pb_row, ps[nb]], wr=[ps[nb]])
            k.ins("vector", lambda e, j=j, nb=nb: e.tensor_copy(
                s_tile[:, 512 * nb:512 * (nb + 1)], ps[nb][:, :512]), rd=[ps[nb]], wr=[s_tile])
        k.ins("sync", lambda e, j=j: e.dma_start(sdv[:, j, :], s_tile[:, :]),
              rd=[s_tile], wr=[dram["s"]], dma=True)
    p.rel(out_fm, pwT, pb_row, s_tile, ps[0], ps[1])

    xi = dram["xn"]
    bufs = [dram["acc"], dram["y"][0], dram["y"][1], dram["y"][2]]
    for l in range(2):
        xi = _l3_sublayer(p, aps, consts, dram, xi, bufs[2 * l], bufs[2 * l + 1], l)
    moe_combine(p, consts, state, dram, [xi], x_src, None, "l3cb", defer=True)


def _l3_shift_load(p, xi, j, xv, label):
    k = p.k
    k.ins("sync", lambda e, j=j, xv=xv: e.dma_start(
        xv[:, 0:512], xi.rearrange("(j q) c -> q j c", q=128)[:, j, 0:512]),
        rd=[xi], wr=[xv], dma=True, label=label)
    if j == 0:
        k.ins("vector", lambda e, xv=xv: e.memset(xv[0:1, 512:1024], 0.0), rd=[], wr=[xv], label=label)
        k.ins("sync", lambda e, xv=xv: e.dma_start(xv[1:128, 512:1024], xi[0:127, 512:1024]),
              rd=[xi], wr=[xv], dma=True, label=label)
    else:
        k.ins("sync", lambda e, j=j, xv=xv: e.dma_start(
            xv[:, 512:1024], xi[128 * j - 1:128 * j + 127, 512:1024]),
            rd=[xi], wr=[xv], dma=True, label=label)


def _l3_sublayer(p, aps, consts, dram, xi, xa_buf, xb_buf, l):
    k = p.k
    HID = 2730
    ps = [p.gps("f3ps", [128, 512], F32) for _ in range(4)]
    h_fm = p.get("f3hf", [128, NI, T], BF16)
    xt = [p.get("f3x", [128, D], F32) for _ in range(2)]
    bt = [p.get("f3b", [128, D], BF16) for _ in range(2)]
    for j in range(NJ):
        xv, bv = xt[j % 2], bt[j % 2]
        _l3_shift_load(p, xi, j, xv, "f3sh")
        k.ins("scalar", lambda e, xv=xv, bv=bv: e.copy(bv[:, :], xv[:, :]), rd=[xv], wr=[bv])
        transpose_pairs(p, [(bv[:, 128 * i:128 * (i + 1)], h_fm[:, i, 128 * j:128 * (j + 1)], bv, h_fm)
                            for i in range(NI)], consts, "f3h")
    wq = load_w_chunks(p, aps["L3f_wqkv"][l], D, 1536, "f3wq")
    qsr = p.get("f3qsr", [128, 512], F32)
    ksr = p.get("f3ksr", [128, 512], F32)
    qs_row = load_f32(p, aps["L3f_q_scale"][l:l + 1, :], [1, 64], "f3qs")
    ks_row = load_f32(p, aps["L3f_k_scale"][l:l + 1, :], [1, 64], "f3ks")
    for i in range(8):
        k.ins("tensor", lambda e, i=i: e.matmul(ps[2][:, 64 * i:64 * (i + 1)], consts["ones_row32"][:, :],
                                                qs_row[:, :], start=(i == 0), stop=(i == 7)),
              rd=[consts["ones_row32"], qs_row, ps[2]], wr=[ps[2]])
    k.ins("vector", lambda e: e.tensor_copy(qsr[:, :], ps[2][:, :512]), rd=[ps[2]], wr=[qsr])
    for i in range(8):
        k.ins("tensor", lambda e, i=i: e.matmul(ps[2][:, 64 * i:64 * (i + 1)], consts["ones_row32"][:, :],
                                                ks_row[:, :], start=(i == 0), stop=(i == 7)),
              rd=[consts["ones_row32"], ks_row, ps[2]], wr=[ps[2]])
    k.ins("vector", lambda e: e.tensor_copy(ksr[:, :], ps[2][:, :512]), rd=[ps[2]], wr=[ksr])
    p.rel(qs_row, ks_row)
    q_fm = p.get("f3qf", [128, 4, T], BF16)
    k_fm = p.get("f3kf", [128, 4, T], BF16)
    v_tm = p.get("f3vt", [128, NJ, 512], BF16)
    sa = p.get("f3sa", [128, 512], F32)
    sb_ = p.get("f3sb", [128, 512], F32)
    nco = p.get("f3nc", [128, 8], F32)
    tC = p.get("f3tC", [128, 512], F32)
    tS = p.get("f3tS", [128, 512], F32)
    cvq = aps["c_xqC"].rearrange("(j q) w -> j q w", q=128)
    svq = aps["c_xqS"].rearrange("(j q) w -> j q w", q=128)
    cvk = aps["c_xkC"].rearrange("(j q) w -> j q w", q=128)
    svk = aps["c_xkS"].rearrange("(j q) w -> j q w", q=128)
    for part, off in (("q", 0), ("k", 512), ("v", 1024)):
        for j in range(NJ):
            pp = ps[j % 2]
            for ck in range(8):
                k.ins("tensor", lambda e, j=j, ck=ck, off=off, pp=pp: e.matmul(
                    pp[:, :512], h_fm[:, ck, 128 * j:128 * (j + 1)], wq[:, ck, off:off + 512],
                    start=(ck == 0), stop=(ck == 7)), rd=[h_fm, wq, pp], wr=[pp])
            if part == "v":
                k.ins("scalar", lambda e, j=j, pp=pp: e.copy(v_tm[:, j, :], pp[:, :512]), rd=[pp], wr=[v_tm])
                continue
            k.ins("vector", lambda e, pp=pp: e.tensor_copy(sa[:, :], pp[:, :512]), rd=[pp], wr=[sa])
            k.ins("scalar", lambda e: e.activation(sb_[:, :], sa[:, :], AF.Square), rd=[sa], wr=[sb_])
            k.ins("vector", lambda e: e.reduce_sum(nco[:, :], sb_[:, :].rearrange("q (g d) -> q g d", g=8), AX.X),
                  rd=[sb_], wr=[nco])
            k.ins("scalar", lambda e: e.activation(nco[:, :], nco[:, :], AF.Sqrt), rd=[nco], wr=[nco])
            k.ins("vector", lambda e: e.tensor_scalar_max(nco[:, :], nco[:, :], 1e-12), rd=[nco], wr=[nco])
            k.ins("vector", lambda e: e.reciprocal(nco[:, :], nco[:, :]), rd=[nco], wr=[nco])
            for hh in range(8):
                k.ins("vector", lambda e, hh=hh: e.tensor_scalar_mul(
                    sa[:, 64 * hh:64 * (hh + 1)], sa[:, 64 * hh:64 * (hh + 1)], nco[:, hh:hh + 1]),
                    rd=[sa], srd=[nco], wr=[sa])
            sc_rep = qsr if part == "q" else ksr
            k.ins("vector", lambda e, sc_rep=sc_rep: e.tensor_mul(sa[:, :], sa[:, :], sc_rep[:, :]),
                  rd=[sa, sc_rep], wr=[sa])
            cv, sv = (cvq, svq) if part == "q" else (cvk, svk)
            k.ins("sync", lambda e, j=j, cv=cv: e.dma_start(tC[:, :], cv[j]), rd=["c_xqC" if part == "q" else "c_xkC"], wr=[tC], dma=True)
            k.ins("sync", lambda e, j=j, sv=sv: e.dma_start(tS[:, :], sv[j]), rd=["c_xqS" if part == "q" else "c_xkS"], wr=[tS], dma=True)
            k.ins("vector", lambda e: e.tensor_mul(sb_[:, :], sa[:, :], tC[:, :]), rd=[sa, tC], wr=[sb_])
            av = sa[:, :].rearrange("q (h s d) -> q h s d", h=8, s=2)
            sv2 = tS[:, :].rearrange("q (h s d) -> q h s d", h=8, s=2)
            ov = tC[:, :].rearrange("q (h s d) -> q h s d", h=8, s=2)
            k.ins("vector", lambda e, av=av, sv2=sv2, ov=ov: e.tensor_tensor(
                ov[:, :, 0, :], av[:, :, 1, :], sv2[:, :, 0, :], ALU.mult), rd=[sa, tS], wr=[tC])
            k.ins("vector", lambda e, av=av, sv2=sv2, ov=ov: e.tensor_tensor(
                ov[:, :, 1, :], av[:, :, 0, :], sv2[:, :, 1, :], ALU.mult), rd=[sa, tS], wr=[tC])
            k.ins("vector", lambda e: e.tensor_add(sb_[:, :], sb_[:, :], tC[:, :]), rd=[tC, sb_], wr=[sb_])
            dst = q_fm if part == "q" else k_fm
            k.ins("scalar", lambda e: e.copy(sa[:, :].bitcast(F32) if False else sb_[:, :], sb_[:, :]),
                  rd=[sb_], wr=[sb_]) if False else None
            cb = p.get("f3cb", [128, 512], BF16)
            k.ins("scalar", lambda e, cb=cb: e.copy(cb[:, :], sb_[:, :]), rd=[sb_], wr=[cb])
            transpose_pairs(p, [(cb[:, 128 * i:128 * (i + 1)], dst[:, i, 128 * j:128 * (j + 1)], cb, dst)
                                for i in range(4)], consts, "f3t")
            p.rel(cb)
    p.rel(wq, sa, sb_, nco, tC, tS, qsr, ksr, h_fm)

    psA, psB, psD = ps[0], ps[1], ps[2]
    psN = p.gps("f3pN", [128, 512], F32)
    prb = p.get("f3prb", [128, 640], BF16)
    den2 = p.get("f3dn", [128, 1], F32)
    o_tm = p.get("f3ot", [128, NJ, 512], BF16)
    for h in range(8):
        b = (h % 2) * 64
        ci = h // 2
        for a in range(NJ):
            w_ = a // 4
            al = a % 4
            kts = []
            if w_ > 0:
                for beta in range(al, 4):
                    kts.append((4 * (w_ - 1) + beta, "prev_diag" if beta == al else "full"))
            for beta in range(0, al + 1):
                kts.append((4 * w_ + beta, "own_diag" if beta == al else "full"))
            nn = len(kts)

            def lqk(n_):
                kt, kind = kts[n_]
                pp = psA if n_ < 4 else psB
                colo = 128 * (n_ % 4)
                k.ins("tensor", lambda e, kt=kt, a=a, b=b, ci=ci, pp=pp, colo=colo: e.matmul(
                    pp[:, colo:colo + 128], k_fm[b:b + 64, ci, 128 * kt:128 * (kt + 1)],
                    q_fm[b:b + 64, ci, 128 * a:128 * (a + 1)], start=True, stop=True),
                    rd=[k_fm, q_fm, pp], wr=[pp], label="f3qk")
                k.ins("scalar", lambda e, pp=pp, colo=colo, n_=n_: e.activation(
                    prb[:, 128 * n_:128 * (n_ + 1)], pp[:, colo:colo + 128], AF.Exp, scale=8.0),
                    rd=[pp], wr=[prb], label="f3ex")
                if kind == "own_diag":
                    k.ins("vector", lambda e, n_=n_: e.tensor_mul(
                        prb[:, 128 * n_:128 * (n_ + 1)], prb[:, 128 * n_:128 * (n_ + 1)], consts["triu"][:, :]),
                        rd=[consts["triu"]], wr=[prb])
                elif kind == "prev_diag":
                    k.ins("vector", lambda e, n_=n_: e.tensor_mul(
                        prb[:, 128 * n_:128 * (n_ + 1)], prb[:, 128 * n_:128 * (n_ + 1)], consts["tril"][:, :]),
                        rd=[consts["tril"]], wr=[prb])

            def ldnpv(n_):
                kt, kind = kts[n_]
                k.ins("tensor", lambda e, n_=n_, nn=nn: e.matmul(
                    psD[:, 0:1], prb[:, 128 * n_:128 * (n_ + 1)], consts["ones_col_bf"][:, :],
                    start=(n_ == 0), stop=(n_ == nn - 1)),
                    rd=[prb, consts["ones_col_bf"], psD], wr=[psD], label="f3dn")
                k.ins("tensor", lambda e, n_=n_, kt=kt, h=h, nn=nn: e.matmul(
                    psN[:, 0:64], prb[:, 128 * n_:128 * (n_ + 1)], v_tm[:, kt, 64 * h:64 * (h + 1)],
                    start=(n_ == 0), stop=(n_ == nn - 1)),
                    rd=[prb, v_tm, psN], wr=[psN], label="f3pv")

            lqk(0)
            for n_ in range(1, nn):
                lqk(n_)
                ldnpv(n_ - 1)
            ldnpv(nn - 1)
            k.ins("vector", lambda e: e.reciprocal(den2[:, :], psD[:, 0:1]), rd=[psD], wr=[den2])
            k.ins("vector", lambda e, a=a, h=h: e.tensor_scalar_mul(
                o_tm[:, a, 64 * h:64 * (h + 1)], psN[:, 0:64], den2[:, 0:1]),
                rd=[psN], srd=[den2], wr=[o_tm])
    p.rel(psN, prb, den2, q_fm, k_fm, v_tm)

    o_fm = p.get("f3of", [128, 4, T], BF16)
    transpose_pairs(p, [(o_tm[:, j, 128 * i:128 * (i + 1)], o_fm[:, i, 128 * j:128 * (j + 1)], o_tm, o_fm)
                        for j in range(NJ) for i in range(4)], consts, "f3oT")
    p.rel(o_tm)
    wo = load_w_chunks(p, aps["L3f_wo"][l], 512, 1024, "f3wo")
    xat = [p.get("f3xa", [128, D], F32) for _ in range(2)]
    xav = xa_buf.rearrange("(j q) c -> q j c", q=128)
    xiv = xi.rearrange("(j q) c -> q j c", q=128)
    for j in range(NJ):
        xv = xat[j % 2]
        k.ins("sync", lambda e, j=j, xv=xv: e.dma_start(xv[:, :], xiv[:, j, :]), rd=[xi], wr=[xv], dma=True)
        for nb in range(2):
            pp = ps[nb]
            for ck in range(4):
                k.ins("tensor", lambda e, j=j, nb=nb, ck=ck, pp=pp: e.matmul(
                    pp[:, :512], o_fm[:, ck, 128 * j:128 * (j + 1)], wo[:, ck, 512 * nb:512 * (nb + 1)],
                    start=(ck == 0), stop=(ck == 3)), rd=[o_fm, wo, pp], wr=[pp])
            k.ins("vector", lambda e, nb=nb, pp=pp, xv=xv: e.tensor_add(
                xv[:, 512 * nb:512 * (nb + 1)], xv[:, 512 * nb:512 * (nb + 1)], pp[:, :512]),
                rd=[pp, xv], wr=[xv])
        k.ins("sync", lambda e, j=j, xv=xv: e.dma_start(xav[:, j, :], xv[:, :]), rd=[xv], wr=[xa_buf], dma=True)
    p.rel(o_fm, wo, xat[0], xat[1])

    lngr = load_rep(p, aps["L3f_ln_g"][l:l + 1, :], D, consts, ps[2], "f3lg")
    lnbr = load_rep(p, aps["L3f_ln_b"][l:l + 1, :], D, consts, ps[2], "f3lb")
    hln_fm = p.get("f3hln", [128, NI, T], BF16)
    mv = p.get("f3mv", [128, 1, 2], F32)
    s12 = p.get("f3s12", [128, 12], F32)
    rstd = p.get("f3rs", [128, 1], F32)
    nmean = p.get("f3nm", [128, 1], F32)
    epsap = p.cval(1e-5)
    for j in range(NJ):
        xv, bv = xt[j % 2], bt[j % 2]
        _l3_shift_load(p, xa_buf, j, xv, "f3sh2")
        k.ins("vector", lambda e, xv=xv: e.bn_stats(s12[:, 0:6], xv[:, 0:512]), rd=[xv], wr=[s12])
        k.ins("vector", lambda e, xv=xv: e.bn_stats(s12[:, 6:12], xv[:, 512:1024]), rd=[xv], wr=[s12])
        k.ins("vector", lambda e: e.bn_aggr(mv[:, 0, :], s12[:, 0:12]), rd=[s12], wr=[mv])
        k.ins("scalar", lambda e: e.activation(rstd[:, :], mv[:, 0, 1:2], AF.Sqrt, bias=epsap),
              rd=[mv, epsap], wr=[rstd])
        k.ins("vector", lambda e: e.reciprocal(rstd[:, :], rstd[:, :]), rd=[rstd], wr=[rstd])
        k.ins("vector", lambda e: e.tensor_scalar_mul(nmean[:, :], mv[:, 0, 0:1], -1.0), rd=[mv], wr=[nmean])
        k.ins("vector", lambda e, xv=xv: e.tensor_scalar(
            xv[:, :], xv[:, :], nmean[:, 0:1], rstd[:, 0:1], ALU.add, ALU.mult),
            rd=[xv], srd=[nmean, rstd], wr=[xv])
        k.ins("vector", lambda e, xv=xv: e.tensor_mul(xv[:, :], xv[:, :], lngr[:, :]), rd=[lngr, xv], wr=[xv])
        k.ins("vector", lambda e, xv=xv, bv=bv: e.tensor_add(bv[:, :].bitcast(F32) if False else xv[:, :],
                                                             xv[:, :], lnbr[:, :]), rd=[lnbr, xv], wr=[xv])
        k.ins("scalar", lambda e, xv=xv, bv=bv: e.copy(bv[:, :], xv[:, :]), rd=[xv], wr=[bv])
        transpose_pairs(p, [(bv[:, 128 * i:128 * (i + 1)], hln_fm[:, i, 128 * j:128 * (j + 1)], bv, hln_fm)
                            for i in range(NI)], consts, "f3hl")
    p.rel(lngr, lnbr, mv, s12, rstd, nmean, xt[0], xt[1], bt[0], bt[1])

    w2sb = [p.get("f3w2a", [128, 11, 1024], BF16), p.get("f3w2b", [128, 11, 1024], BF16)]
    w2s = aps["L3f_w2"][l]
    for kc in range(21):
        t2, s2 = w2sb[kc // 11], kc % 11
        k.ins("gpsimd", lambda e, kc=kc, t2=t2, s2=s2: e.dma_start(
            t2[:, s2, :], w2s[128 * kc:128 * (kc + 1), :]), rd=[w2s], wr=[t2], dma=True)
    k.ins("gpsimd", lambda e: e.dma_start(w2sb[1][0:42, 10, :], w2s[2688:2730, :]),
          rd=[w2s], wr=[w2sb[1]], dma=True)
    b1row = load_f32(p, aps["L3f_b1"][l:l + 1, :], [1, 5460], "f3b1")
    b2row = load_f32(p, aps["L3f_b2"][l:l + 1, :], [1, D], "f3b2")
    bca = p.get("f3bca", [128, 22], F32)
    bcg = p.get("f3bcg", [128, 22], F32)
    row_to_cols(p, bca, b1row[:, 0:HID], HID, consts, ps[2], "f3ba")
    row_to_cols(p, bcg, b1row[:, HID:2 * HID], HID, consts, ps[2], "f3bg")
    w1v = aps["L3f_w1"][l].rearrange("(kc kp) n -> kc kp n", kp=128)
    w1t = [p.get("f3w1s", [128, 8, 128], BF16) for _ in range(4)]
    act_ci = [p.get("f3aci", [128, T], BF16) for _ in range(2)]
    gf = p.get("f3gf", [128, 2048], F32)
    af = p.get("f3af", [128, 2048], BF16)
    ysb = [p.get("f3y", [128, D], F32) for _ in range(2)]
    xbv = xb_buf.rearrange("(j q) c -> q j c", q=128)
    for ci2 in range(22):
        w_rows = 128 if ci2 < 21 else 42
        wa, wg = w1t[2 * (ci2 % 2)], w1t[2 * (ci2 % 2) + 1]
        for ck in range(8):
            k.ins("gpsimd", lambda e, ck=ck, ci2=ci2, wa=wa, w_rows=w_rows: e.dma_start(
                wa[:, ck, 0:w_rows], w1v[ck, :, 128 * ci2:128 * ci2 + w_rows]),
                rd=[w1v], wr=[wa], dma=True)
            k.ins("gpsimd", lambda e, ck=ck, ci2=ci2, wg=wg, w_rows=w_rows: e.dma_start(
                wg[:, ck, 0:w_rows], w1v[ck, :, HID + 128 * ci2:HID + 128 * ci2 + w_rows]),
                rd=[w1v], wr=[wg], dma=True)
        at = act_ci[ci2 % 2]
        for tb in range(8):
            t0 = 256 * tb
            pA, pB = ps[tb % 2], ps[2 + tb % 2]
            for ck in range(8):
                k.ins("tensor", lambda e, ck=ck, t0=t0, w_rows=w_rows, pA=pA, wa=wa: e.matmul(
                    pA[:w_rows, :256], wa[:, ck, 0:w_rows], hln_fm[:, ck, t0:t0 + 256],
                    start=(ck == 0), stop=(ck == 7)), rd=[wa, hln_fm, pA], wr=[pA])
            k.ins("scalar", lambda e, ci2=ci2, w_rows=w_rows, pA=pA, t0=t0: e.activation(
                af[:w_rows, t0:t0 + 256], pA[:w_rows, :256], AF.Identity, bias=bca[:w_rows, ci2:ci2 + 1]),
                rd=[pA, bca], wr=[af])
            for ck in range(8):
                k.ins("tensor", lambda e, ck=ck, t0=t0, w_rows=w_rows, pB=pB, wg=wg: e.matmul(
                    pB[:w_rows, :256], wg[:, ck, 0:w_rows], hln_fm[:, ck, t0:t0 + 256],
                    start=(ck == 0), stop=(ck == 7)), rd=[wg, hln_fm, pB], wr=[pB])
            k.ins("scalar", lambda e, ci2=ci2, w_rows=w_rows, pB=pB, t0=t0: e.activation(
                gf[:w_rows, t0:t0 + 256], pB[:w_rows, :256], AF.Gelu, bias=bcg[:w_rows, ci2:ci2 + 1]),
                rd=[pB, bcg], wr=[gf])
        k.ins("vector", lambda e, w_rows=w_rows, at=at: e.tensor_mul(
            at[:w_rows, :], af[:w_rows, :], gf[:w_rows, :]),
            rd=[af, gf], wr=[at])
        k.ins("sync", lambda e, ci2=ci2, at=at, w_rows=w_rows: e.dma_start(
            dram["act"][ci2, 0:w_rows, :], at[:w_rows, :]),
            rd=[at], wr=[dram["act"]], dma=True)
    p.rel(w1t[0], w1t[1], w1t[2], w1t[3], gf, af, hln_fm, bca, bcg, b1row, act_ci[0], act_ci[1])
    actj = [p.get("f3aj", [128, 22, 128], BF16) for _ in range(2)]
    actv = dram["act"].transpose([1, 0, 2])
    for j in range(NJ):
        yv = ysb[j % 2]
        aj = actj[j % 2]
        k.ins("sync", lambda e, j=j, aj=aj: e.dma_start(aj[:, :, :], actv[:, :, 128 * j:128 * (j + 1)]),
              rd=[dram["act"]], wr=[aj], dma=True)
        k.ins("sync", lambda e, j=j, yv=yv: e.dma_start(
            yv[:, :], xa_buf.rearrange("(j q) c -> q j c", q=128)[:, j, :]),
            rd=[xa_buf], wr=[yv], dma=True)
        for nb in range(2):
            pY = ps[nb]
            for ci2 in range(22):
                w_rows = 128 if ci2 < 21 else 42
                t2, s2 = (w2sb[ci2 // 11], ci2 % 11)
                k.ins("tensor", lambda e, ci2=ci2, j=j, nb=nb, w_rows=w_rows, pY=pY,
                      aj=aj, t2=t2, s2=s2: e.matmul(
                    pY[:, :512], aj[:w_rows, ci2, :],
                    t2[:w_rows, s2, 512 * nb:512 * (nb + 1)],
                    start=(ci2 == 0), stop=False), rd=[aj, t2, pY], wr=[pY])
            k.ins("tensor", lambda e, nb=nb, pY=pY: e.matmul(
                pY[:, :512], consts["ones_row32"][:, :], b2row[:, 512 * nb:512 * (nb + 1)],
                start=False, stop=True), rd=[consts["ones_row32"], b2row, pY], wr=[pY])
            k.ins("vector", lambda e, nb=nb, pY=pY, yv=yv: e.tensor_add(
                yv[:, 512 * nb:512 * (nb + 1)], yv[:, 512 * nb:512 * (nb + 1)], pY[:, :512]),
                rd=[pY, yv], wr=[yv])
        k.ins("sync", lambda e, j=j, yv=yv: e.dma_start(xbv[:, j, :], yv[:, :]),
              rd=[yv], wr=[xb_buf], dma=True)
    p.rel(w2sb[0], w2sb[1], b2row, actj[0], actj[1], ysb[0], ysb[1],
          ps[0], ps[1], ps[2], ps[3])
    return xb_buf



def build_program(w, layers=(0, 1, 2, 3), dbg=()):
    nc = bass.Bass("TRN2", target_bir_lowering=False)
    aps = {"x": nc.dram_tensor("x", [T, D], F32, kind="ExternalInput").ap()}
    for n, a in w.items():
        aps[n] = nc.dram_tensor(n, list(a.shape), F32, kind="ExternalInput").ap()
    out = nc.dram_tensor("out", [T, D], F32, kind="ExternalOutput").ap()
    dram = {
        "x": nc.dram_tensor("ds_x", [T, D], F32).ap(),
        "x2": nc.dram_tensor("ds_x2", [T, D], F32).ap(),
        "xn": nc.dram_tensor("ds_xn", [T, D], F32).ap(),
        "s": nc.dram_tensor("ds_s", [T, D], F32).ap(),
        "acc": nc.dram_tensor("ds_acc", [T, D], F32).ap(),
        "y": [nc.dram_tensor(f"ds_y{i}", [T, D], F32).ap() for i in range(8)],
        "act": nc.dram_tensor("ds_act", [22, 128, T], BF16).ap(),
    }
    k = K(nc)
    with ExitStack() as ctx:
        prog = {e: ctx.enter_context(nc.semaphore(f"pr_{e}")) for e in K.ENGINES}
        dsem = {e: ctx.enter_context(nc.semaphore(f"dm_{e}")) for e in K.ENGINES}
        blk = ctx.enter_context(nc.Block())
        p = P(nc, k, ctx)
        consts = setup_consts(p, aps)
        state = {"x_src": aps["x"], "flip": 0}
        emitters = {0: emit_L0, 1: emit_L1, 2: emit_L2, 3: emit_L3}
        import os as _os
        for L in layers:
            emitters[L](p, aps, consts, state, dram)
            if _os.environ.get("KPOOL"):
                inv = {}
                for kk, vv in p.freel.items():
                    if vv:
                        inv[str(kk)] = len(vv)
                print(f"after L{L}: free={inv} used={sorted(str(u[0]) for u in p.used.values())}")
        for dn in dbg:
            dsrc = dram[dn] if not dn.startswith("y") else dram["y"][int(dn[1:])]
            dout = nc.dram_tensor("dbg_" + dn, [T, D], F32, kind="ExternalOutput").ap()
            dt_ = p.get("dbgT", [128, D], F32)
            dsv = dsrc.rearrange("(j q) c -> q j c", q=128)
            dov = dout.rearrange("(j q) c -> q j c", q=128)
            for j in range(NJ):
                k.ins("sync", lambda e, j=j, dt_=dt_, dsv=dsv: e.dma_start(dt_[:, :], dsv[:, j, :]),
                      rd=[dsrc], wr=[dt_], dma=True)
                k.ins("sync", lambda e, j=j, dt_=dt_, dov=dov: e.dma_start(dov[:, j, :], dt_[:, :]),
                      rd=[dt_], wr=["dbg_" + dn], dma=True)
            p.rel(dt_)
        ot = [p.get("outT", [128, D], F32) for _ in range(2)]
        lcb, lrel = _take_combine(state)
        xsv = (state["x_src"].rearrange("(j q) c -> q j c", q=128)
               if state["x_src"] is not None else None)
        ov = out.rearrange("(j q) c -> q j c", q=128)
        for j in range(NJ):
            tcur = ot[j % 2]
            if lcb is not None:
                lcb(j, tcur, None)
            else:
                k.ins("sync", lambda e, j=j, tc=tcur: e.dma_start(tc[:, :], xsv[:, j, :]),
                      rd=[state["x_src"]], wr=[tcur], dma=True)
            k.ins("sync", lambda e, j=j, tc=tcur: e.dma_start(ov[:, j, :], tc[:, :]),
                  rd=[tcur], wr=["out"], dma=True)
        if lrel is not None:
            lrel()
        k.finalize(blk, prog, dsem)
    return nc


def kernel(**inputs):
    w = prep_inputs(inputs)
    x = np.ascontiguousarray(np.asarray(inputs["L0_attn_x"], np.float32))
    nc = build_program(w, layers=(0, 1, 2, 3))
    in_maps = []
    for core in range(8):
        m = {"x": np.ascontiguousarray(x[core])}
        m.update(w)
        in_maps.append(m)
    res = run_bass_kernel_spmd(nc, in_maps, core_ids=list(range(8)))
    out = np.stack([np.asarray(r["out"]) for r in res.results], axis=0)
    return out.astype(np.float32)
